# version 08
# baseline (speedup 1.0000x reference)
import contextlib
import numpy as np
import concourse.bass as bass
import concourse.mybir as mybir
from concourse.bass_utils import run_bass_kernel_spmd

F32 = mybir.dt.float32
BF16 = mybir.dt.bfloat16
U8 = mybir.dt.uint8
AF = mybir.ActivationFunctionType
ALU = mybir.AluOpType

D = 1024
NCH = 8
HID = 4096
NHC = 32
TT = 512
NSEQ = 2
SEQ_TILES = 4
WIN_TILES = 2
EPS = 1e-5

KB = 1024


class P:

    def __init__(self, nc, ename, eng, t, sems):
        self.nc = nc
        self.E = ename
        self.h = eng
        self.t = t
        self.S = sems
        self.c = {k: 0 for k in sems}

    def bump(self, sem, n):
        self.c[sem] += n
        return self.c[sem]

    def _inc(self, ins, sem, n):
        if sem is None:
            return
        self.bump(sem, n)
        if ins is not None:
            ins.then_inc(self.S[sem], n)

    def w(self, eng, sem, val):
        if val <= 0:
            return
        if self.E == eng:
            self.h.wait_ge(self.S[sem], val)

    def mm(self, out, lhsT, rhs, start, stop, sem=None):
        ins = None
        if self.E == "T":
            ins = self.h.matmul(out, lhsT, rhs, start=start, stop=stop)
        self._inc(ins, sem, 1)

    def gtt(self, out, in0, in1, op, sem="gp"):
        ins = None
        if self.E == "G":
            if self.c["gp"] > 0:
                self.h.wait_ge(self.S["gp"], self.c["gp"])
            ins = self.h.tensor_tensor(out, in0, in1, op)
        self._inc(ins, sem, 1)

    def gw(self, sem, val):
        self.w("G", sem, val)

    def mmwarm(self, out, lhsT, rhs):
        if self.E == "T":
            self.h.matmul(out, lhsT, rhs, start=True, stop=True)

    def mm2(self, outA, outB, lhsT, rhs, start, stop, sem=None):
        ins = None
        if self.E == "T":
            self.h.matmul(outA, lhsT[0:64, :], rhs[0:64, :], start=start,
                          stop=stop, tile_position=(0, 0))
            ins = self.h.matmul(outB, lhsT[64:128, :], rhs[64:128, :], start=start,
                                stop=stop, tile_position=(64, 0))
        self._inc(ins, sem, 1)

    def act(self, out, in_, func, bias=0.0, scale=1.0, sem="se"):
        ins = None
        if self.E == "S":
            if self.c["se"] > 0:
                self.h.wait_ge(self.S["se"], self.c["se"])
            ins = self.h.activation(out, in_, func, bias=bias, scale=scale)
        self._inc(ins, sem, 1)

    def _vsync(self):
        if self.E == "V" and self.c["ve"] > 0:
            self.h.wait_ge(self.S["ve"], self.c["ve"])

    def tt(self, out, in0, in1, op, sem="ve"):
        ins = None
        if self.E == "V":
            self._vsync()
            ins = self.h.tensor_tensor(out, in0, in1, op)
        self._inc(ins, sem, 1)

    def ts(self, out, in0, s1, op, sem="ve"):
        ins = None
        if self.E == "V":
            self._vsync()
            ins = self.h.tensor_single_scalar(out, in0, s1, op)
        self._inc(ins, sem, 1)

    def stt(self, out, in0, scal, in1, op0, op1, sem="ve"):
        ins = None
        if self.E == "V":
            self._vsync()
            ins = self.h.scalar_tensor_tensor(out, in0, scal, in1, op0, op1)
        self._inc(ins, sem, 1)

    def vcopy(self, out, in_, sem="ve"):
        ins = None
        if self.E == "V":
            self._vsync()
            ins = self.h.tensor_copy(out, in_)
        self._inc(ins, sem, 1)

    def vrecip(self, out, in_, sem="ve"):
        ins = None
        if self.E == "V":
            self._vsync()
            ins = self.h.reciprocal(out, in_)
        self._inc(ins, sem, 1)

    def vmemset(self, ap, val, sem="ve"):
        ins = None
        if self.E == "V":
            self._vsync()
            ins = self.h.memset(ap, val)
        self._inc(ins, sem, 1)

    def dma(self, q, out, in_, sem):
        ins = None
        if self.E == q:
            ins = self.h.dma_start(out=out, in_=in_)
        self._inc(ins, sem, 16)


def _alloc_views(nc, arena, layout):
    base = nc.lookup_mloc(arena).addr
    out = {}
    for name, shape, dtype, off in layout:
        out[name] = nc.alloc_sbuf_tensor_at(name, shape, dtype, offset=base + off)
    return out


def build_program(nseq=NSEQ, seq_tiles=SEQ_TILES, win_tiles=WIN_TILES):
    nc = bass.Bass("TRN2", target_bir_lowering=False)
    seq = seq_tiles * TT
    W = win_tiles * TT

    dr = {}
    dr["x"] = nc.dram_tensor("x", [nseq, NCH, 128, seq], F32, kind="ExternalInput")
    dr["out"] = nc.dram_tensor("out", [nseq, NCH, 128, seq], F32, kind="ExternalOutput")
    for n in ("wq", "wk", "wv", "wo"):
        dr[n] = nc.dram_tensor(n, [D, D], F32, kind="ExternalInput")
    for n, shp in (("wfbd", [128, 128]), ("bfbc", [128, 128]), ("bo_t", [128, NCH]),
                   ("bf2", [128, 1]),
                   ("w1_t", [NCH, 128, 4096]), ("w2_t", [NCH, 128, 4096]),
                   ("h1_t", [NHC, 128, 4096]), ("h2_t", [NHC, 128, 4096]),
                   ("w3_t", [NCH, 128, 4096]),
                   ("s1_t", [128, NHC]), ("s2_t", [128, NHC]),
                   ("sh1_t", [128, NHC]), ("sh2_t", [128, NHC]), ("s3_t", [128, NCH]),
                   ("pat2", [2, 128]), ("onestat", [128, 1]), ("onesrow", [1, 128]),
                   ("bfrow", [1, 128]), ("warm512", [128, TT])):
        dr[n] = nc.dram_tensor(n, shp, F32, kind="ExternalInput")
    dr["xp_hbm"] = nc.dram_tensor("xp_hbm", [nseq, seq_tiles, 128, 4096], F32,
                                  kind="Internal")

    ARENA = 206 * KB
    arena = nc.alloc_sbuf_tensor("arena", [128, ARENA], U8)
    SC = 3584
    B0 = SC + 49 * KB
    lay = [
        ("wfbd", [128, 128], BF16, 0),
        ("bfbc", [128, 128], BF16, 256),
        ("bo_t", [128, NCH], F32, 512),
        ("bf2", [128, 1], F32, 544),
        ("s1", [128, NHC], F32, 576),
        ("s2", [128, NHC], F32, 704),
        ("sh1", [128, NHC], F32, 832),
        ("sh2", [128, NHC], F32, 960),
        ("s3", [128, NCH], F32, 1088),
        ("onestat", [128, 1], BF16, 1120),
        ("onesrow", [1, 128], BF16, 1152),
        ("bfrow", [1, 128], BF16, 1408),
        ("warm512", [128, TT], BF16, 1920),
        ("pat2", [2, 128], BF16, 1664),
        ("ltmp", [128, TT], F32, SC),
        ("ltmp2", [128, TT], F32, SC + 2 * KB),
        ("msb", [1, TT], F32, SC + 4 * KB),
        ("vsb", [1, TT], F32, SC + 6 * KB),
        ("r_bf", [1, TT], BF16, SC + 8 * KB),
        ("mr_bf", [1, TT], BF16, SC + 9 * KB),
        ("normsb", [2, TT], F32, SC + 10 * KB),
        ("rcp_bf", [2, TT], BF16, SC + 12 * KB),
        ("rcp_bfb", [2, TT], BF16, SC + 47 * KB),
        ("r_save", [1, 4 * TT], BF16, SC + 13 * KB),
        ("mr_save", [1, 4 * TT], BF16, SC + 17 * KB),
        ("sq", [128, 4096], BF16, SC + 21 * KB),
        ("wc0", [128, 4096], BF16, SC + 29 * KB),
        ("wc1", [128, 4096], BF16, SC + 37 * KB),
        ("ltmp3", [128, TT], F32, SC + 45 * KB),
        ("ltmpB", [128, TT], F32, B0 + 138 * KB),
        ("ltmp2B", [128, TT], F32, B0 + 140 * KB),
        ("ltmp3B", [128, TT], F32, B0 + 142 * KB),

        ("xb0", [128, 4096], F32, B0 + 0),
        ("xb1", [128, 4096], F32, B0 + 16 * KB),
        ("xbf", [128, 4096], BF16, B0 + 32 * KB),
        ("WA", [128, NCH * D], BF16, B0 + 40 * KB),
        ("WB", [128, NCH * D], BF16, B0 + 56 * KB),
        ("xa", [128, 4096], BF16, B0 + 72 * KB),
        ("kq", [128, 4096], BF16, B0 + 80 * KB),
        ("vtm", [128, 32 * 132], BF16, B0 + 88 * KB),
        ("qf", [128, 4096], BF16, B0 + 88 * KB),
        ("kftm", [128, 4096], BF16, B0 + 97 * KB),
        ("y", [128, 4096], BF16, B0 + 97 * KB),
        ("kv", [128, NCH * 132], F32, B0 + 105 * KB),
        ("BD", [128, NCH * 128], BF16, B0 + 110 * KB),
        ("ksc", [128, NCH * 2], BF16, B0 + 112 * KB),
        ("rbc", [128, TT], BF16, B0 + 113 * KB),
        ("z", [128, 4096], BF16, B0 + 114 * KB),
        ("xp", [128, 4096], F32, B0 + 122 * KB),
        ("a", [128, NHC * W], BF16, B0 + 0),
        ("b", [128, NHC * W], BF16, B0 + NHC * W * 2),
        ("xnw", [128, win_tiles * 4096], BF16,
         B0 + NHC * W * 2 + win_tiles * 16 * KB),
        ("xpr", [128, win_tiles * 4096], F32, B0 + NHC * W * 2),
        ("f", [128, NCH * W], BF16, B0 + NHC * W * 2 + win_tiles * 16 * KB),
        ("outt", [128, 4096], F32, B0 + NHC * W * 2 + win_tiles * 16 * KB + NCH * W * 2),
    ]
    need = max(B0 + 2 * NHC * W * 2, B0 + 144 * KB)
    assert need <= ARENA, ("big region overflow", need, ARENA)
    t = _alloc_views(nc, arena, lay)
    t["arena"] = arena
    t["ps"] = [nc.alloc_psum_tensor(f"psb{i}", [128, 512], F32) for i in range(8)]

    sem_names = ["pe", "ve", "se", "gp", "dcon", "dwA", "dwB", "dbf",
                 "dxa", "dxb", "dw0", "dw1", "dsp", "dxr", "dout"]
    stack = contextlib.ExitStack()
    sems = {n: stack.enter_context(nc.semaphore(n)) for n in sem_names}

    counters = []
    with nc.Block() as block:
        def run(ename):
            def f(eng):
                p = P(nc, ename, eng, t, sems)
                emit_all(p, dr, nseq, seq_tiles, win_tiles)
                counters.append(dict(p.c))
            return f
        block.tensor(run("T"))
        block.vector(run("V"))
        block.scalar(run("S"))
        block.gpsimd(run("G"))
        block.sync(run("Y"))
    stack.close()
    for c in counters[1:]:
        assert c == counters[0], ("counter mismatch", counters)
    return nc



def _barrier(p, engines="TVSY", dma=()):
    cur = {k: p.c[k] for k in ("pe", "ve", "se", "gp")}
    own = {"T": "pe", "V": "ve", "S": "se", "G": "gp"}
    for e in engines:
        for sem, v in cur.items():
            if own.get(e) != sem:
                p.w(e, sem, v)
        for dsem in dma:
            p.w(e, dsem, p.c[dsem])


def emit_all(p, dr, nseq, seq_tiles, win_tiles):
    t = p.t

    st = {
        "bank_free": [[] for _ in range(8)],
        "xbuf_free": [[], []],
        "xbf_free": [],
        "x_val": {},
        "xbf_val": {},
        "wslice_done": {},
        "ffn_slice": 0,
        "ffn_dma": 0,
        "sq_last_pe": 0,
        "l3_ve": 0,
        "l3_gp": 0,
        "ltmp_ve": 0,
        "ltmp_veB": 0,
        "wA_val": 0, "wB_val": 0,
    }

    def bank_wait(b):
        for semv in st["bank_free"][b]:
            p.w("T", *semv)
        st["bank_free"][b] = []

    def bank_consumed(b, sem="ve"):
        st["bank_free"][b].append((sem, p.c[sem]))

    for n, src in (("wfbd", "wfbd"), ("bfbc", "bfbc"), ("bo_t", "bo_t"), ("bf2", "bf2"),
                   ("s1", "s1_t"), ("s2", "s2_t"), ("sh1", "sh1_t"), ("sh2", "sh2_t"),
                   ("s3", "s3_t"), ("onestat", "onestat"), ("onesrow", "onesrow"),
                   ("pat2", "pat2"), ("bfrow", "bfrow"), ("warm512", "warm512")):
        p.dma("G", t[n][:, :], dr[src][:, :], "dcon")
    NCONST = p.c["dcon"]
    for e in "TVSY":
        p.w(e, "dcon", NCONST)
    p.w("G", "dcon", NCONST)

    out_dmas = [0]
    for s in range(nseq):
        emit_seq(p, dr, st, s, seq_tiles, win_tiles, bank_wait, bank_consumed, out_dmas)

    p.w("Y", "dout", p.c["dout"])


def weave(p, bank_wait, wb, checkpoints, per=3):
    t = p.t
    bank_wait(wb)
    for sem, v in checkpoints:
        p.w("T", sem, v)
        for _ in range(per):
            p.mmwarm(t["ps"][wb][0:1, :], t["onestat"][:, 0:1], t["warm512"][:, :])


def ln_fm(p, st, bank_wait, bank_consumed, src_f32, src_bf, dst, stat_banks,
          x_add=None, src_ve=0, sq_ck=None, warm_bank=None, save_to=None):
    t = p.t
    ps = t["ps"]
    bm, bs = stat_banks
    se_sq = p.c["se"]
    bank_wait(bm)
    bank_wait(bs)
    if warm_bank is not None and sq_ck:
        weave(p, bank_wait, warm_bank, sq_ck, per=3)
    p.w("T", "se", se_sq)
    p.w("T", "ve", src_ve)
    for ic in range(NCH):
        p.mm(ps[bm][0:1, :], t["onestat"][:, 0:1], src_bf[ic],
             start=(ic == 0), stop=(ic == NCH - 1),
             sem="pe" if ic == NCH - 1 else None)
    for ic in range(NCH):
        p.mm(ps[bs][0:1, :], t["onestat"][:, 0:1], t["sq"][:, ic * TT:(ic + 1) * TT],
             start=(ic == 0), stop=(ic == NCH - 1),
             sem="pe" if ic == NCH - 1 else None)
    pe_stats = p.c["pe"]
    st["sq_last_pe"] = pe_stats
    p.w("V", "pe", pe_stats)
    p.vcopy(t["msb"][:, :], ps[bm][0:1, :])
    p.tt(t["vsb"][:, :], t["msb"][:, :], t["msb"][:, :], ALU.mult)
    p.stt(t["vsb"][:, :], ps[bs][0:1, :], float(EPS), t["vsb"][:, :],
          ALU.add, ALU.subtract)
    ve_v = p.c["ve"]
    p.w("S", "ve", ve_v)
    p.act(t["vsb"][:, :], t["vsb"][:, :], AF.Ln)
    p.act(t["vsb"][:, :], t["vsb"][:, :], AF.Exp, scale=-0.5)
    se_sqrt = p.c["se"]
    p.w("V", "se", se_sqrt)
    p.vcopy(t["r_bf"][:, :], t["vsb"][:, :])
    p.tt(t["msb"][:, :], t["msb"][:, :], t["vsb"][:, :], ALU.mult)
    p.vcopy(t["mr_bf"][:, :], t["msb"][:, :])
    if save_to is not None:
        p.vcopy(save_to[0], t["r_bf"][:, :])
        p.vcopy(save_to[1], t["mr_bf"][:, :])
    ve_r = p.c["ve"]
    if warm_bank is not None:
        ck = [("ve", ve_v), ("se", p.c["se"]), ("ve", ve_r)]
        weave(p, bank_wait, warm_bank, ck, per=4)
    p.w("T", "ve", ve_r)
    p.mm(ps[bm][:, :], t["onesrow"][:, :], t["r_bf"][:, :], start=True, stop=True,
         sem="pe")
    p.mm(ps[bs][:, :], t["onesrow"][:, :], t["mr_bf"][:, :], start=True, stop=True,
         sem="pe")
    pe_bc = p.c["pe"]
    p.w("V", "pe", pe_bc)
    apply_ck = []
    for ic in range(NCH):
        srcm = src_f32[ic] if src_f32 is not None else src_bf[ic]
        p.tt(t["ltmp"][:, :], srcm, ps[bm][:, :], ALU.mult)
        if x_add is None:
            p.tt(dst[ic], t["ltmp"][:, :], ps[bs][:, :], ALU.subtract)
        else:
            p.tt(t["ltmp2"][:, :], t["ltmp"][:, :], ps[bs][:, :], ALU.subtract)
            p.tt(dst[ic], t["ltmp2"][:, :], x_add[ic], ALU.add)
        apply_ck.append(("ve", p.c["ve"]))
    st["ltmp_ve"] = p.c["ve"]
    bank_consumed(bm)
    bank_consumed(bs)
    return {"pe_stats": pe_stats, "pe_bc": pe_bc, "ve_end": p.c["ve"],
            "gp_end": p.c["gp"], "apply_ck": apply_ck}


def emit_seq(p, dr, st, s, seq_tiles, win_tiles, bank_wait, bank_consumed, out_dmas):
    t = p.t
    ps = t["ps"]
    seq = seq_tiles * TT
    nwin = seq_tiles // win_tiles
    xbufs = [t["xb0"], t["xb1"]]
    dx = ["dxa", "dxb"]

    _barrier(p, engines="TVSY", dma=("dsp", "dout"))
    _barrier(p, engines="G")

    p.dma("G", t["WA"][:, :].rearrange("p (i o) -> p i o", o=D),
          dr["wk"].ap().rearrange("(i p) o -> p i o", p=128), "dwA")
    st["wA_val"] = p.c["dwA"]
    p.dma("G", t["WB"][:, :].rearrange("p (i o) -> p i o", o=D),
          dr["wv"].ap().rearrange("(i p) o -> p i o", p=128), "dwB")
    st["wB_val"] = p.c["dwB"]

    entries = [(1, i) for i in range(seq_tiles)] + [(2, i) for i in range(seq_tiles)]

    def load_entry(j):
        if j >= len(entries):
            return
        _, tile = entries[j]
        b = j % 2
        for semv in st["xbuf_free"][b]:
            p.w("Y", *semv)
        st["xbuf_free"][b] = []
        p.dma("Y", xbufs[b][:, :].rearrange("p (i tt) -> p i tt", tt=TT),
              dr["x"][s][:, :, tile * TT:(tile + 1) * TT].rearrange("i p tt -> p i tt"),
              dx[b])
        st["x_val"][j] = p.c[dx[b]]

    def load_xbf(j):
        if j >= len(entries):
            return
        _, tile = entries[j]
        for semv in st["xbf_free"]:
            p.w("G", *semv)
        st["xbf_free"] = []
        p.dma("G", t["xbf"][:, :].rearrange("p (i tt) -> p i tt", tt=TT),
              dr["x"][s][:, :, tile * TT:(tile + 1) * TT].rearrange("i p tt -> p i tt"),
              "dbf")
        st["xbf_val"][j] = p.c["dbf"]

    load_entry(0)
    load_entry(1)
    load_xbf(0)

    p.vmemset(t["kv"][:, :], 0.0)
    vtm_r = t["vtm"][:, :].rearrange("p (sl e) -> p sl e", e=132)
    p.vmemset(vtm_r[:, :, 128:129], 1.0)

    for tile in range(seq_tiles):
        j = tile
        b = j % 2
        xb = xbufs[b]
        xf = [xb[:, ic * TT:(ic + 1) * TT] for ic in range(NCH)]
        xbfl = [t["xbf"][:, ic * TT:(ic + 1) * TT] for ic in range(NCH)]
        p.w("S", dx[b], st["x_val"][j])
        p.w("V", dx[b], st["x_val"][j])
        p.w("T", "dbf", st["xbf_val"][j])
        p.w("S", "pe", st["sq_last_pe"])
        sq_ck = []
        for ic in range(NCH):
            p.act(t["sq"][:, ic * TT:(ic + 1) * TT], xf[ic], AF.Square)
            sq_ck.append(("se", p.c["se"]))
        xal = [t["xa"][:, ic * TT:(ic + 1) * TT] for ic in range(NCH)]
        sn1 = ln_fm(p, st, bank_wait, bank_consumed, xf, xbfl, xal, (6, 7),
                    sq_ck=sq_ck, warm_bank=0,
                    save_to=(t["r_save"][:, tile * TT:(tile + 1) * TT],
                             t["mr_save"][:, tile * TT:(tile + 1) * TT]))
        ve_xa = p.c["ve"]
        st["xbf_free"] = [("pe", p.c["pe"])]
        if j + 1 < seq_tiles:
            load_xbf(j + 1)

        p.w("T", "dwA", st["wA_val"])
        weave(p, bank_wait, 0, sn1["apply_ck"], per=4)
        p.w("T", "ve", ve_xa)
        p.w("T", "gp", sn1["gp_end"])
        kql = [t["kq"][:, oc * TT:(oc + 1) * TT] for oc in range(NCH)]
        for oc in range(NCH):
            bA, bB = (0, 1) if oc % 2 == 0 else (2, 3)
            bank_wait(bA)
            bank_wait(bB)
            for ic in range(NCH):
                p.mm2(ps[bA][:, :], ps[bB][:, :],
                      t["WA"][:, ic * D + oc * 128: ic * D + (oc + 1) * 128],
                      xal[ic], start=(ic == 0), stop=(ic == NCH - 1),
                      sem="pe" if ic == NCH - 1 else None)
            pe_k = p.c["pe"]
            p.w("S", "pe", pe_k)
            p.w("S", "ve", st["l3_ve"])
            p.w("S", "gp", st["l3_gp"])
            p.act(t["ltmp3"][:, :], ps[bA][:, :], AF.Identity)
            bank_consumed(bA, "se")
            se_i = p.c["se"]
            p.w("V", "se", se_i)
            p.tt(kql[oc], t["ltmp3"][:, :], ps[bB][:, :], ALU.add)
            st["l3_ve"] = p.c["ve"]
            bank_consumed(bB)
        se_k = p.c["se"]
        ve_k = p.c["ve"]
        p.w("T", "ve", ve_k)

        p.w("T", "dwB", st["wB_val"])
        for sub in range(4):
            pairs = ((0, 1), (2, 3)) if sub % 2 == 0 else ((4, 5), (6, 7))
            for vh in range(2):
                bA, bB = pairs[vh]
                bank_wait(bA)
                bank_wait(bB)
                for ic in range(NCH):
                    p.mm2(ps[bA][:, :], ps[bB][:, :],
                          xal[ic][:, sub * 128:(sub + 1) * 128],
                          t["WB"][:, ic * D + vh * 512: ic * D + (vh + 1) * 512],
                          start=(ic == 0), stop=(ic == NCH - 1),
                          sem="pe" if ic == NCH - 1 else None)
                pe_v = p.c["pe"]
                p.w("S", "pe", pe_v)
                p.w("S", "ve", st["l3_ve"])
                p.w("S", "gp", st["l3_gp"])
                p.act(t["ltmp3"][:, :], ps[bA][:, :], AF.Identity)
                bank_consumed(bA, "se")
                se_i = p.c["se"]
                p.w("V", "se", se_i)
                dstv = vtm_r[:, sub * 8 + vh * 4: sub * 8 + (vh + 1) * 4, 0:128]
                p.tt(dstv, t["ltmp3"][:, :].rearrange("p (c e) -> p c e", e=128),
                     ps[bB][:, :].rearrange("p (c e) -> p c e", e=128), ALU.add)
                st["l3_ve"] = p.c["ve"]
                bank_consumed(bB)

        kf_se = [0] * NCH
        for c in range(NCH):
            bk = 6 + c % 2
            bank_wait(bk)
            p.w("T", "se", se_k)
            for sub in range(4):
                p.mm(ps[bk][:, sub * 128:(sub + 1) * 128],
                     kql[c][:, sub * 128:(sub + 1) * 128],
                     t["wfbd"][:, :], start=True, stop=False)
                p.mm(ps[bk][:, sub * 128:(sub + 1) * 128],
                     t["onesrow"][:, :], t["bfrow"][:, :], start=False, stop=True,
                     sem="pe" if sub == 3 else None)
            pe_kf = p.c["pe"]
            p.w("S", "pe", pe_kf)
            kf_dst = t["kftm"][:, :].rearrange("p (s e) -> p s e", e=128)[:, c:c + 25:8, :]
            p.act(kf_dst, ps[bk][:, :].rearrange("p (s e) -> p s e", e=128), AF.Relu)
            kf_se[c] = p.c["se"]
            bank_consumed(bk, "se")

        for c in range(NCH):
            bk = 2 + c % 2
            bank_wait(bk)
            p.w("T", "se", kf_se[c])
            for sub in range(4):
                p.mm(ps[bk][:, 0:129],
                     t["kftm"][:, (sub * 8 + c) * 128:(sub * 8 + c) * 128 + 128],
                     t["vtm"][:, (sub * 8 + c) * 132:(sub * 8 + c) * 132 + 129],
                     start=(sub == 0), stop=(sub == 3),
                     sem="pe" if sub == 3 else None)
            pe_kv = p.c["pe"]
            p.w("V", "pe", pe_kv)
            p.tt(t["kv"][:, c * 132: c * 132 + 129], t["kv"][:, c * 132: c * 132 + 129],
                 ps[bk][:, 0:129], ALU.add)
            bank_consumed(bk)
        st["xbuf_free"][b] = [("ve", p.c["ve"]), ("se", p.c["se"]),
                              ("gp", p.c["gp"])]
        load_entry(j + 2)

    p.vmemset(t["BD"][:, :], 0.0)
    p.vmemset(t["ksc"][:, :], 0.0)
    for c in range(NCH):
        p.vcopy(t["BD"][0:64, c * 128: c * 128 + 64],
                t["kv"][0:64, c * 132: c * 132 + 64])
        p.vcopy(t["BD"][64:128, c * 128 + 64: c * 128 + 128],
                t["kv"][64:128, c * 132 + 64: c * 132 + 128])
        p.vcopy(t["ksc"][0:64, c * 2: c * 2 + 1],
                t["kv"][0:64, c * 132 + 128: c * 132 + 129])
        p.vcopy(t["ksc"][64:128, c * 2 + 1: c * 2 + 2],
                t["kv"][64:128, c * 132 + 128: c * 132 + 129])
    ve_bd = p.c["ve"]

    p.w("G", "pe", p.c["pe"])
    p.dma("G", t["WA"][:, :].rearrange("p (i o) -> p i o", o=D),
          dr["wq"].ap().rearrange("(i p) o -> p i o", p=128), "dwA")
    st["wA_val"] = p.c["dwA"]
    p.dma("G", t["WB"][:, :].rearrange("p (i o) -> p i o", o=D),
          dr["wo"].ap().rearrange("(i p) o -> p i o", p=128), "dwB")
    st["wB_val"] = p.c["dwB"]

    for tile in range(seq_tiles):
        j = seq_tiles + tile
        b = j % 2
        xb = xbufs[b]
        xf = [xb[:, ic * TT:(ic + 1) * TT] for ic in range(NCH)]
        xbfl = [t["xbf"][:, ic * TT:(ic + 1) * TT] for ic in range(NCH)]
        p.w("S", dx[b], st["x_val"][j])
        p.w("V", dx[b], st["x_val"][j])
        xal = [t["xa"][:, ic * TT:(ic + 1) * TT] for ic in range(NCH)]
        bm, bs = 6, 7
        bank_wait(bm)
        bank_wait(bs)
        p.mm(ps[bm][:, :], t["onesrow"][:, :],
             t["r_save"][:, tile * TT:(tile + 1) * TT], start=True, stop=True,
             sem="pe")
        p.mm(ps[bs][:, :], t["onesrow"][:, :],
             t["mr_save"][:, tile * TT:(tile + 1) * TT], start=True, stop=True,
             sem="pe")
        pe_bc1 = p.c["pe"]
        p.w("V", "pe", pe_bc1)
        apply_ck = []
        for ic in range(NCH):
            p.tt(t["ltmp"][:, :], xf[ic], ps[bm][:, :], ALU.mult)
            p.tt(xal[ic], t["ltmp"][:, :], ps[bs][:, :], ALU.subtract)
            apply_ck.append(("ve", p.c["ve"]))
        st["ltmp_ve"] = p.c["ve"]
        bank_consumed(bm)
        bank_consumed(bs)
        ve_xa = p.c["ve"]

        p.w("T", "dwA", st["wA_val"])
        weave(p, bank_wait, 0, apply_ck, per=4)
        p.w("T", "ve", ve_xa)
        kql = [t["kq"][:, oc * TT:(oc + 1) * TT] for oc in range(NCH)]
        for oc in range(NCH):
            bA, bB = (0, 1) if oc % 2 == 0 else (2, 3)
            bank_wait(bA)
            bank_wait(bB)
            for ic in range(NCH):
                p.mm2(ps[bA][:, :], ps[bB][:, :],
                      t["WA"][:, ic * D + oc * 128: ic * D + (oc + 1) * 128],
                      xal[ic], start=(ic == 0), stop=(ic == NCH - 1),
                      sem="pe" if ic == NCH - 1 else None)
            pe_q = p.c["pe"]
            p.w("S", "pe", pe_q)
            p.w("S", "ve", st["l3_ve"])
            p.w("S", "gp", st["l3_gp"])
            p.act(t["ltmp3"][:, :], ps[bA][:, :], AF.Identity)
            bank_consumed(bA, "se")
            se_i = p.c["se"]
            p.w("V", "se", se_i)
            p.tt(kql[oc], t["ltmp3"][:, :], ps[bB][:, :], ALU.add)
            st["l3_ve"] = p.c["ve"]
            bank_consumed(bB)
        ve_q = p.c["ve"]
        p.w("T", "ve", ve_q)

        qfl = [t["qf"][:, c * TT:(c + 1) * TT] for c in range(NCH)]
        qf_se = [0] * NCH
        for c in range(NCH):
            bk = 4 + c % 2
            bank_wait(bk)
            p.mm(ps[bk][:, :], t["wfbd"][:, :], kql[c], start=True, stop=True, sem="pe")
            pe_qf = p.c["pe"]
            p.w("S", "pe", pe_qf)
            p.act(qfl[c], ps[bk][:, :], AF.Relu, bias=t["bf2"][:, 0:1])
            qf_se[c] = p.c["se"]
            bank_consumed(bk, "se")

        if tile == 0:
            p.w("T", "ve", ve_bd)
        rcp2 = [t["rcp_bf"], t["rcp_bfb"]]
        rcp_done = [0] * NCH
        bcast_pe = [0] * NCH
        yl = [t["y"][:, c * TT:(c + 1) * TT] for c in range(NCH)]
        for c in range(NCH):
            bk = 4 + c % 2
            bo_ = 6 + c % 2
            bank_wait(bk)
            p.w("T", "se", qf_se[c])
            p.mm(ps[bk][0:2, :], t["ksc"][:, c * 2:(c + 1) * 2], qfl[c],
                 start=True, stop=True, sem="pe")
            pe_n = p.c["pe"]
            p.w("V", "pe", pe_n)
            p.ts(t["normsb"][:, :], ps[bk][0:2, :], 1e-6, ALU.max)
            bank_consumed(bk)
            p.vrecip(t["normsb"][:, :], t["normsb"][:, :])
            if c >= 2:
                p.w("V", "pe", bcast_pe[c - 2])
            p.vcopy(rcp2[c % 2][:, :], t["normsb"][:, :])
            rcp_done[c] = p.c["ve"]
            bank_wait(bk)
            p.w("T", "ve", rcp_done[c])
            p.mm(ps[bk][:, :], t["pat2"][:, :], rcp2[c % 2][:, :],
                 start=True, stop=True, sem="pe")
            bank_wait(bo_)
            p.mm(ps[bo_][:, :], t["BD"][:, c * 128:(c + 1) * 128], qfl[c],
                 start=True, stop=True, sem="pe")
            bcast_pe[c] = p.c["pe"]
            pe_o2 = p.c["pe"]
            p.w("V", "pe", pe_o2)
            p.vcopy(t["rbc"][:, :], ps[bk][:, :])
            bank_consumed(bk)
            p.tt(yl[c], ps[bo_][:, :], t["rbc"][:, :], ALU.mult)
            bank_consumed(bo_)
        ve_y = p.c["ve"]

        p.w("T", "dwB", st["wB_val"])
        p.w("T", "ve", ve_y)
        zl = [t["z"][:, oc * TT:(oc + 1) * TT] for oc in range(NCH)]
        p.w("S", "pe", st["sq_last_pe"])
        zsq_ck = []
        for oc in range(NCH):
            bA, bB = (0, 1) if oc % 2 == 0 else (2, 3)
            bank_wait(bA)
            bank_wait(bB)
            for ic in range(NCH):
                p.mm2(ps[bA][:, :], ps[bB][:, :],
                      t["WB"][:, ic * D + oc * 128: ic * D + (oc + 1) * 128],
                      yl[ic], start=(ic == 0), stop=(ic == NCH - 1),
                      sem="pe" if ic == NCH - 1 else None)
            pe_z = p.c["pe"]
            p.w("S", "pe", pe_z)
            p.w("S", "ve", st["l3_ve"])
            p.w("S", "gp", st["l3_gp"])
            p.act(t["ltmp3"][:, :], ps[bA][:, :], AF.Identity,
                  bias=t["bo_t"][:, oc:oc + 1])
            bank_consumed(bA, "se")
            se_i = p.c["se"]
            p.w("V", "se", se_i)
            p.tt(zl[oc], t["ltmp3"][:, :], ps[bB][:, :], ALU.add)
            st["l3_ve"] = p.c["ve"]
            bank_consumed(bB)
            ve_z = p.c["ve"]
            p.w("S", "ve", ve_z)
            p.act(t["sq"][:, oc * TT:(oc + 1) * TT], zl[oc], AF.Square)
            zsq_ck.append(("se", p.c["se"]))

        p.w("V", "dsp", p.c["dsp"])
        xpl = [t["xp"][:, ic * TT:(ic + 1) * TT] for ic in range(NCH)]
        sn2 = ln_fm(p, st, bank_wait, bank_consumed, None, zl, xpl, (4, 5), x_add=xf,
                    src_ve=p.c["ve"], sq_ck=zsq_ck, warm_bank=0)
        ve_xp = p.c["ve"]
        gp_xp = sn2["gp_end"]
        st["xbuf_free"][b] = [("ve", ve_xp), ("se", p.c["se"]), ("gp", gp_xp)]
        load_entry(j + 2)

        p.w("Y", "ve", ve_xp)
        p.w("Y", "gp", gp_xp)
        p.dma("Y", dr["xp_hbm"][s, tile], t["xp"][:, :], "dsp")


    _barrier(p, engines="TVS", dma=("dsp",))
    for w in range(nwin):
        emit_ffn_window(p, dr, st, s, w, seq_tiles, win_tiles, bank_wait,
                        bank_consumed, out_dmas)


def ffn_slice_dma(p, st, dr, which, idx):
    k = st["ffn_dma"]
    buf = p.t["wc0"] if k % 2 == 0 else p.t["wc1"]
    sem = "dw0" if k % 2 == 0 else "dw1"
    if k >= 2:
        done = st["wslice_done"].get(k - 2)
        if done is not None:
            p.w("G", "pe", done)
    p.dma("G", buf[:, :], dr[which][idx], sem)
    st["ffn_dma"] += 1


def ffn_slice_use(p, st):
    k = st["ffn_slice"]
    buf = p.t["wc0"] if k % 2 == 0 else p.t["wc1"]
    sem = "dw0" if k % 2 == 0 else "dw1"
    p.w("T", sem, 16 * (k // 2 + 1))
    st["ffn_slice"] += 1
    return buf, k


def emit_ffn_window(p, dr, st, s, w, seq_tiles, win_tiles, bank_wait, bank_consumed,
                    out_dmas):
    t = p.t
    ps = t["ps"]
    seq = seq_tiles * TT
    W = win_tiles * TT
    NH = win_tiles

    al = lambda oc, h: t["a"][:, oc * W + h * TT: oc * W + (h + 1) * TT]
    bl = lambda oc, h: t["b"][:, oc * W + h * TT: oc * W + (h + 1) * TT]
    xn_rhs = lambda ic, h: t["xnw"][:, h * 4096 + ic * TT: h * 4096 + (ic + 1) * TT]

    _barrier(p, engines="VS", dma=("dout", "dxr"))
    p.w("Y", "pe", p.c["pe"])
    p.w("Y", "ve", p.c["ve"])
    p.w("Y", "se", p.c["se"])
    p.w("Y", "dsp", p.c["dsp"])
    for ti in range(win_tiles):
        p.dma("Y", t["xpr"][:, ti * 4096:(ti + 1) * 4096],
              dr["xp_hbm"][s, w * win_tiles + ti], "dxr")
    dxr0 = p.c["dxr"]

    for ti in range(win_tiles):
        xpl3 = [t["xpr"][:, ti * 4096 + ic * TT: ti * 4096 + (ic + 1) * TT]
                for ic in range(NCH)]
        p.w("S", "dxr", dxr0)
        p.w("S", "pe", st["sq_last_pe"])
        sq3_ck = []
        x1bf = [t["a"][:, ic * TT:(ic + 1) * TT] for ic in range(NCH)]
        for ic in range(NCH):
            p.act(x1bf[ic], xpl3[ic], AF.Copy)
            p.act(t["sq"][:, ic * TT:(ic + 1) * TT], xpl3[ic], AF.Square)
            sq3_ck.append(("se", p.c["se"]))
        xnl = [t["xnw"][:, ti * 4096 + ic * TT: ti * 4096 + (ic + 1) * TT]
               for ic in range(NCH)]
        ln_fm(p, st, bank_wait, bank_consumed, xpl3, x1bf, xnl, (2, 3),
              sq_ck=sq3_ck, warm_bank=0)
    ve_xnw = p.c["ve"]
    p.w("T", "ve", ve_xnw)

    for slab in range(NCH):
        ffn_slice_dma(p, st, dr, "w1_t", slab)
        ffn_slice_dma(p, st, dr, "w2_t", slab)
        b1, k1 = ffn_slice_use(p, st)
        b2, k2 = ffn_slice_use(p, st)
        for ocl in range(4):
            oc = slab * 4 + ocl
            for h in range(NH):
                w1A, w1B = (0 + 2 * h, 1 + 2 * h)
                w2A, w2B = (4 + 2 * h, 5 + 2 * h)
                bank_wait(w1A)
                bank_wait(w1B)
                for ic in range(NCH):
                    p.mm2(ps[w1A][:, :], ps[w1B][:, :],
                          b1[:, ic * 512 + ocl * 128: ic * 512 + ocl * 128 + 128],
                          xn_rhs(ic, h), start=(ic == 0), stop=(ic == NCH - 1),
                          sem="pe" if ic == NCH - 1 else None)
                pe_w1 = p.c["pe"]
                bank_wait(w2A)
                bank_wait(w2B)
                for ic in range(NCH):
                    p.mm2(ps[w2A][:, :], ps[w2B][:, :],
                          b2[:, ic * 512 + ocl * 128: ic * 512 + ocl * 128 + 128],
                          xn_rhs(ic, h), start=(ic == 0), stop=(ic == NCH - 1),
                          sem="pe" if ic == NCH - 1 else None)
                pe_w2 = p.c["pe"]
                lt0, lt2, lt3 = (("ltmp", "ltmp2", "ltmp3") if h == 0 else
                                 ("ltmpB", "ltmp2B", "ltmp3B"))
                lkey = "ltmp_ve" if h == 0 else "ltmp_veB"
                p.w("S", "pe", pe_w1)
                p.w("S", "ve", st[lkey])
                p.act(t[lt2][:, :], ps[w1A][:, :], AF.Identity)
                bank_consumed(w1A, "se")
                se_i1 = p.c["se"]
                p.w("V", "se", se_i1)
                p.tt(t[lt3][:, :], t[lt2][:, :], ps[w1B][:, :], ALU.add)
                bank_consumed(w1B)
                ve_s1 = p.c["ve"]
                p.w("S", "ve", ve_s1)
                p.act(t[lt0][:, :], t[lt3][:, :], AF.Silu,
                      scale=t["s1"][:, oc:oc + 1])
                se_t1 = p.c["se"]
                p.w("S", "pe", pe_w2)
                p.act(t[lt2][:, :], ps[w2A][:, :], AF.Identity)
                bank_consumed(w2A, "se")
                se_i2 = p.c["se"]
                p.w("V", "pe", pe_w2)
                p.w("V", "se", se_i2)
                p.tt(t[lt3][:, :], t[lt2][:, :], ps[w2B][:, :], ALU.add)
                bank_consumed(w2B)
                p.stt(al(oc, h), t[lt3][:, :], t["s2"][:, oc:oc + 1],
                      t[lt0][:, :], ALU.mult, ALU.mult)
                st[lkey] = p.c["ve"]
        st["wslice_done"][k1] = p.c["pe"]
        st["wslice_done"][k2] = p.c["pe"]

    ve_a = p.c["ve"]
    for wt, sc_t, srcf, dstf in (("h1_t", "sh1", al, bl), ("h2_t", "sh2", bl, al)):
        if wt == "h1_t":
            p.w("T", "ve", ve_a)
        else:
            p.w("T", "se", p.c["se"])
        for oc in range(NHC):
            ffn_slice_dma(p, st, dr, wt, oc)
            wb, k = ffn_slice_use(p, st)
            for h in range(NH):
                bA = 4 * (oc % 2) + 2 * h
                bB = bA + 1
                bank_wait(bA)
                bank_wait(bB)
                for ic in range(NHC):
                    p.mm2(ps[bA][:, :], ps[bB][:, :], wb[:, ic * 128:(ic + 1) * 128],
                          srcf(ic, h), start=(ic == 0), stop=(ic == NHC - 1),
                          sem="pe" if ic == NHC - 1 else None)
                pe_h = p.c["pe"]
                lt0, lt2 = ("ltmp", "ltmp2") if h == 0 else ("ltmpB", "ltmp2B")
                lkey = "ltmp_ve" if h == 0 else "ltmp_veB"
                p.w("S", "pe", pe_h)
                p.w("S", "ve", st[lkey])
                p.act(t[lt0][:, :], ps[bA][:, :], AF.Identity)
                bank_consumed(bA, "se")
                se_i = p.c["se"]
                p.w("V", "se", se_i)
                p.tt(t[lt2][:, :], t[lt0][:, :], ps[bB][:, :], ALU.add)
                bank_consumed(bB)
                st[lkey] = p.c["ve"]
                p.w("S", "ve", st[lkey])
                p.act(dstf(oc, h), t[lt2][:, :], AF.Gelu,
                      scale=t[sc_t][:, oc:oc + 1])
            st["wslice_done"][k] = p.c["pe"]

    p.w("Y", "pe", p.c["pe"])
    p.w("Y", "dsp", p.c["dsp"])
    for ti in range(win_tiles):
        p.dma("Y", t["xpr"][:, ti * 4096:(ti + 1) * 4096],
              dr["xp_hbm"][s, w * win_tiles + ti], "dxr")
    dxr_val = p.c["dxr"]
    fl = lambda oc, h: t["f"][:, oc * W + h * TT: oc * W + (h + 1) * TT]
    p.w("T", "se", p.c["se"])
    for oc in range(NCH):
        ffn_slice_dma(p, st, dr, "w3_t", oc)
        wb, k = ffn_slice_use(p, st)
        for h in range(NH):
            bA = 4 * (oc % 2) + 2 * h
            bB = bA + 1
            bank_wait(bA)
            bank_wait(bB)
            for ic in range(NHC):
                p.mm2(ps[bA][:, :], ps[bB][:, :], wb[:, ic * 128:(ic + 1) * 128],
                      al(ic, h), start=(ic == 0), stop=(ic == NHC - 1),
                      sem="pe" if ic == NHC - 1 else None)
            pe_f = p.c["pe"]
            lt0 = "ltmp" if h == 0 else "ltmpB"
            lkey = "ltmp_ve" if h == 0 else "ltmp_veB"
            p.w("S", "pe", pe_f)
            p.w("S", "ve", st[lkey])
            p.act(t[lt0][:, :], ps[bA][:, :], AF.Identity,
                  scale=t["s3"][:, oc:oc + 1])
            bank_consumed(bA, "se")
            se_i = p.c["se"]
            p.w("V", "se", se_i)
            p.stt(fl(oc, h), ps[bB][:, :], t["s3"][:, oc:oc + 1], t[lt0][:, :],
                  ALU.mult, ALU.add)
            bank_consumed(bB)
            st[lkey] = p.c["ve"]
        st["wslice_done"][k] = p.c["pe"]

    p.w("V", "dxr", dxr_val)
    for h in range(NH):
        fbf = [fl(ic, h) for ic in range(NCH)]
        p.w("S", "pe", st["sq_last_pe"])
        p.w("S", "ve", st["ltmp_ve"])
        p.w("S", "ve", st["ltmp_veB"])
        sq4_ck = []
        for ic in range(NCH):
            p.act(t["sq"][:, ic * TT:(ic + 1) * TT], fbf[ic], AF.Square)
            sq4_ck.append(("se", p.c["se"]))
        xprl = [t["xpr"][:, h * 4096 + ic * TT: h * 4096 + (ic + 1) * TT]
                for ic in range(NCH)]
        outl = [t["outt"][:, ic * TT:(ic + 1) * TT] for ic in range(NCH)]
        p.w("V", "dout", p.c["dout"])
        ln_fm(p, st, bank_wait, bank_consumed, None, fbf, outl, (0, 1), x_add=xprl,
              src_ve=p.c["ve"], sq_ck=sq4_ck, warm_bank=4)
        ve_out = p.c["ve"]
        p.w("Y", "ve", ve_out)
        p.w("Y", "gp", p.c["gp"])
        tglob = w * win_tiles + h
        p.dma("Y",
              dr["out"][s][:, :, tglob * TT:(tglob + 1) * TT].rearrange("i p tt -> p i tt"),
              t["outt"][:, :].rearrange("p (i tt) -> p i tt", tt=TT), "dout")
        out_dmas[0] += 1



def _prep_inputs(L0_attn_x, L0_attn_Wq, L0_attn_Wk, L0_attn_Wv, L0_attn_Wo,
                 L0_attn_bo, L0_attn_Wf, L0_attn_bf,
                 L0_ffn_w1_w, L0_ffn_w1_s, L0_ffn_w2_w, L0_ffn_w2_s,
                 L0_ffn_h1_w, L0_ffn_h1_s, L0_ffn_h2_w, L0_ffn_h2_s,
                 L0_ffn_w3_w, L0_ffn_w3_s):
    f32 = np.float32
    asc = np.ascontiguousarray

    def tile_oi(wT, n_in_ch, n_out_ch, ocp):
        r = wT.reshape(n_in_ch, 128, n_out_ch, ocp).transpose(2, 1, 0, 3)
        return asc(r.reshape(n_out_ch, 128, n_in_ch * ocp).astype(f32))

    sh = {}
    for n, wmat in (("wq", L0_attn_Wq), ("wk", L0_attn_Wk), ("wv", L0_attn_Wv),
                    ("wo", L0_attn_Wo)):
        sh[n] = asc(np.asarray(wmat, dtype=f32))
    wf = np.asarray(L0_attn_Wf, dtype=f32)
    wfbd = np.zeros((128, 128), f32)
    wfbd[:64, :64] = wf
    wfbd[64:, 64:] = wf
    sh["wfbd"] = wfbd
    bfv = np.asarray(L0_attn_bf, dtype=f32)
    sh["bfbc"] = asc(np.broadcast_to(np.concatenate([bfv, bfv])[None, :],
                                     (128, 128)).astype(f32))
    sh["bf2"] = asc(np.concatenate([bfv, bfv])[:, None].astype(f32))
    sh["bo_t"] = asc(np.asarray(L0_attn_bo, dtype=f32).reshape(NCH, 128).T)
    w1T = np.asarray(L0_ffn_w1_w, dtype=f32).T
    w2T = np.asarray(L0_ffn_w2_w, dtype=f32).T
    h1T = np.asarray(L0_ffn_h1_w, dtype=f32).T
    h2T = np.asarray(L0_ffn_h2_w, dtype=f32).T
    w3T = np.asarray(L0_ffn_w3_w, dtype=f32).T
    sh["w1_t"] = tile_oi(w1T, 8, 8, 512)
    sh["w2_t"] = tile_oi(w2T, 8, 8, 512)
    sh["h1_t"] = tile_oi(h1T, 32, 32, 128)
    sh["h2_t"] = tile_oi(h2T, 32, 32, 128)
    sh["w3_t"] = tile_oi(w3T, 32, 8, 128)
    for n, v in (("s1_t", L0_ffn_w1_s), ("s2_t", L0_ffn_w2_s),
                 ("sh1_t", L0_ffn_h1_s), ("sh2_t", L0_ffn_h2_s)):
        sh[n] = asc(np.asarray(v, dtype=f32).reshape(-1, 128).T)
    sh["s3_t"] = asc(np.asarray(L0_ffn_w3_s, dtype=f32).reshape(NCH, 128).T)
    pat = np.zeros((2, 128), f32)
    pat[0, :64] = 1.0
    pat[1, 64:] = 1.0
    sh["pat2"] = pat
    sh["bfrow"] = asc(np.concatenate([bfv, bfv])[None, :].astype(f32))
    sh["warm512"] = np.ones((128, TT), f32)
    sh["onestat"] = np.full((128, 1), 1.0 / D, f32)
    sh["onesrow"] = np.ones((1, 128), f32)

    x = np.asarray(L0_attn_x, dtype=f32)
    percore = []
    for c in range(8):
        xc = x[c * 2:(c + 1) * 2]
        xt = xc.transpose(0, 2, 1).reshape(2, NCH, 128, xc.shape[1])
        percore.append(asc(xt.astype(f32)))
    return sh, percore


_NC_CACHE = {}


def kernel(**inputs):
    sh, percore = _prep_inputs(**inputs)
    if "full" not in _NC_CACHE:
        _NC_CACHE["full"] = build_program()
    nc = _NC_CACHE["full"]
    in_maps = []
    for c in range(8):
        m = dict(sh)
        m["x"] = percore[c]
        in_maps.append(m)
    res = run_bass_kernel_spmd(nc, in_maps, core_ids=list(range(8)))
    outs = []
    for c in range(8):
        o = res.results[c]["out"]
        o = o.reshape(2, D, -1).transpose(0, 2, 1)
        outs.append(o)
    return np.ascontiguousarray(np.concatenate(outs, axis=0)).astype(np.float32)


# version 09
# speedup vs baseline: 1.0569x; 1.0569x over previous
import contextlib
import os as _os
import numpy as np
import concourse.bass as bass
import concourse.mybir as mybir
from concourse.bass_utils import run_bass_kernel_spmd

F32 = mybir.dt.float32
BF16 = mybir.dt.bfloat16
U8 = mybir.dt.uint8
AF = mybir.ActivationFunctionType
ALU = mybir.AluOpType

D = 1024
NCH = 8
HID = 4096
NHC = 32
TT = 512
NSEQ = 2
SEQ_TILES = 4
WIN_TILES = 2
EPS = 1e-5

KB = 1024


class P:

    def __init__(self, nc, ename, eng, t, sems):
        self.nc = nc
        self.E = ename
        self.h = eng
        self.t = t
        self.S = sems
        self.c = {k: 0 for k in sems}

    def bump(self, sem, n):
        self.c[sem] += n
        return self.c[sem]

    def _inc(self, ins, sem, n):
        if sem is None:
            return
        self.bump(sem, n)
        if ins is not None:
            ins.then_inc(self.S[sem], n)

    def w(self, eng, sem, val):
        if val <= 0:
            return
        if self.E == eng:
            self.h.wait_ge(self.S[sem], val)

    def mm(self, out, lhsT, rhs, start, stop, sem=None):
        ins = None
        if self.E == "T":
            ins = self.h.matmul(out, lhsT, rhs, start=start, stop=stop)
        self._inc(ins, sem, 1)

    def gtt(self, out, in0, in1, op, sem="gp"):
        ins = None
        if self.E == "G":
            if self.c["gp"] > 0:
                self.h.wait_ge(self.S["gp"], self.c["gp"])
            ins = self.h.tensor_tensor(out, in0, in1, op)
        self._inc(ins, sem, 1)

    def gw(self, sem, val):
        self.w("G", sem, val)

    def mmwarm(self, out, lhsT, rhs):
        if self.E == "T":
            self.h.matmul(out, lhsT, rhs, start=True, stop=True)

    def mm2(self, outA, outB, lhsT, rhs, start, stop, sem=None):
        ins = None
        if self.E == "T":
            self.h.matmul(outA, lhsT[0:64, :], rhs[0:64, :], start=start,
                          stop=stop, tile_position=(0, 0))
            ins = self.h.matmul(outB, lhsT[64:128, :], rhs[64:128, :], start=start,
                                stop=stop, tile_position=(64, 0))
        self._inc(ins, sem, 1)

    def act(self, out, in_, func, bias=0.0, scale=1.0, sem="se"):
        ins = None
        if self.E == "S":
            if self.c["se"] > 0:
                self.h.wait_ge(self.S["se"], self.c["se"])
            ins = self.h.activation(out, in_, func, bias=bias, scale=scale)
        self._inc(ins, sem, 1)

    def _vsync(self):
        if self.E == "V" and self.c["ve"] > 0:
            self.h.wait_ge(self.S["ve"], self.c["ve"])

    def tt(self, out, in0, in1, op, sem="ve"):
        ins = None
        if self.E == "V":
            self._vsync()
            ins = self.h.tensor_tensor(out, in0, in1, op)
        self._inc(ins, sem, 1)

    def ts(self, out, in0, s1, op, sem="ve"):
        ins = None
        if self.E == "V":
            self._vsync()
            ins = self.h.tensor_single_scalar(out, in0, s1, op)
        self._inc(ins, sem, 1)

    def stt(self, out, in0, scal, in1, op0, op1, sem="ve"):
        ins = None
        if self.E == "V":
            self._vsync()
            ins = self.h.scalar_tensor_tensor(out, in0, scal, in1, op0, op1)
        self._inc(ins, sem, 1)

    def vcopy(self, out, in_, sem="ve"):
        ins = None
        if self.E == "V":
            self._vsync()
            ins = self.h.tensor_copy(out, in_)
        self._inc(ins, sem, 1)

    def vrecip(self, out, in_, sem="ve"):
        ins = None
        if self.E == "V":
            self._vsync()
            ins = self.h.reciprocal(out, in_)
        self._inc(ins, sem, 1)

    def vmemset(self, ap, val, sem="ve"):
        ins = None
        if self.E == "V":
            self._vsync()
            ins = self.h.memset(ap, val)
        self._inc(ins, sem, 1)

    def dma(self, q, out, in_, sem):
        ins = None
        if self.E == q:
            ins = self.h.dma_start(out=out, in_=in_)
        self._inc(ins, sem, 16)


def _alloc_views(nc, arena, layout):
    base = nc.lookup_mloc(arena).addr
    out = {}
    for name, shape, dtype, off in layout:
        out[name] = nc.alloc_sbuf_tensor_at(name, shape, dtype, offset=base + off)
    return out


def build_program(nseq=NSEQ, seq_tiles=SEQ_TILES, win_tiles=WIN_TILES):
    nc = bass.Bass("TRN2", target_bir_lowering=False)
    seq = seq_tiles * TT
    W = win_tiles * TT

    dr = {}
    dr["x"] = nc.dram_tensor("x", [nseq, NCH, 128, seq], F32, kind="ExternalInput")
    dr["out"] = nc.dram_tensor("out", [nseq, NCH, 128, seq], F32, kind="ExternalOutput")
    for n in ("wq", "wk", "wv", "wo"):
        dr[n] = nc.dram_tensor(n, [D, D], F32, kind="ExternalInput")
    for n, shp in (("wfbd", [128, 128]), ("bfbc", [128, 128]), ("bo_t", [128, NCH]),
                   ("bf2", [128, 1]),
                   ("w1_t", [NCH, 128, 4096]), ("w2_t", [NCH, 128, 4096]),
                   ("h1_t", [NHC, 128, 4096]), ("h2_t", [NHC, 128, 4096]),
                   ("w3_t", [NCH, 128, 4096]),
                   ("s1_t", [128, NHC]), ("s2_t", [128, NHC]),
                   ("sh1_t", [128, NHC]), ("sh2_t", [128, NHC]), ("s3_t", [128, NCH]),
                   ("pat2", [2, 128]), ("onestat", [128, 1]), ("onesrow", [1, 128]),
                   ("bfrow", [1, 128]), ("warm512", [128, TT])):
        dr[n] = nc.dram_tensor(n, shp, F32, kind="ExternalInput")
    dr["xp_hbm"] = nc.dram_tensor("xp_hbm", [nseq, seq_tiles, 128, 4096], F32,
                                  kind="Internal")

    ARENA = 206 * KB
    arena = nc.alloc_sbuf_tensor("arena", [128, ARENA], U8)
    SC = 3584
    B0 = SC + 49 * KB
    lay = [
        ("wfbd", [128, 128], BF16, 0),
        ("bfbc", [128, 128], BF16, 256),
        ("bo_t", [128, NCH], F32, 512),
        ("bf2", [128, 1], F32, 544),
        ("s1", [128, NHC], F32, 576),
        ("s2", [128, NHC], F32, 704),
        ("sh1", [128, NHC], F32, 832),
        ("sh2", [128, NHC], F32, 960),
        ("s3", [128, NCH], F32, 1088),
        ("onestat", [128, 1], BF16, 1120),
        ("onesrow", [1, 128], BF16, 1152),
        ("bfrow", [1, 128], BF16, 1408),
        ("warm512", [128, TT], BF16, 1920),
        ("pat2", [2, 128], BF16, 1664),
        ("ltmp", [128, TT], F32, SC),
        ("ltmp2", [128, TT], F32, SC + 2 * KB),
        ("msb", [1, TT], F32, SC + 4 * KB),
        ("vsb", [1, TT], F32, SC + 6 * KB),
        ("r_bf", [1, TT], BF16, SC + 8 * KB),
        ("mr_bf", [1, TT], BF16, SC + 9 * KB),
        ("normsb", [2, TT], F32, SC + 10 * KB),
        ("rcp_bf", [2, TT], BF16, SC + 12 * KB),
        ("rcp_bfb", [2, TT], BF16, SC + 47 * KB),
        ("r_save", [1, 4 * TT], BF16, SC + 13 * KB),
        ("mr_save", [1, 4 * TT], BF16, SC + 17 * KB),
        ("sq", [128, 4096], BF16, SC + 21 * KB),
        ("wc0", [128, 4096], BF16, SC + 29 * KB),
        ("wc1", [128, 4096], BF16, SC + 37 * KB),
        ("ltmp3", [128, TT], F32, SC + 45 * KB),
        ("ltmpB", [128, TT], F32, B0 + 138 * KB),
        ("ltmp2B", [128, TT], F32, B0 + 140 * KB),
        ("ltmp3B", [128, TT], F32, B0 + 142 * KB),

        ("xb0", [128, 4096], F32, B0 + 0),
        ("xb1", [128, 4096], F32, B0 + 16 * KB),
        ("xbf", [128, 4096], BF16, B0 + 32 * KB),
        ("WA", [128, NCH * D], BF16, B0 + 40 * KB),
        ("WB", [128, NCH * D], BF16, B0 + 56 * KB),
        ("xa", [128, 4096], BF16, B0 + 72 * KB),
        ("kq", [128, 4096], BF16, B0 + 80 * KB),
        ("vtm", [128, 32 * 132], BF16, B0 + 88 * KB),
        ("qf", [128, 4096], BF16, B0 + 88 * KB),
        ("kftm", [128, 4096], BF16, B0 + 97 * KB),
        ("y", [128, 4096], BF16, B0 + 97 * KB),
        ("kv", [128, NCH * 132], F32, B0 + 105 * KB),
        ("BD", [128, NCH * 128], BF16, B0 + 110 * KB),
        ("ksc", [128, NCH * 2], BF16, B0 + 112 * KB),
        ("rbc", [128, TT], BF16, B0 + 113 * KB),
        ("z", [128, 4096], BF16, B0 + 114 * KB),
        ("xp", [128, 4096], F32, B0 + 122 * KB),
        ("a", [128, NHC * W], BF16, B0 + 0),
        ("b", [128, NHC * W], BF16, B0 + NHC * W * 2),
        ("xnw", [128, win_tiles * 4096], BF16,
         B0 + NHC * W * 2 + win_tiles * 16 * KB),
        ("xpr", [128, win_tiles * 4096], F32, B0 + NHC * W * 2),
        ("f", [128, NCH * W], BF16, B0 + NHC * W * 2 + win_tiles * 16 * KB),
        ("outt", [128, 4096], F32, B0 + NHC * W * 2 + win_tiles * 16 * KB + NCH * W * 2),
    ]
    need = max(B0 + 2 * NHC * W * 2, B0 + 144 * KB)
    assert need <= ARENA, ("big region overflow", need, ARENA)
    t = _alloc_views(nc, arena, lay)
    t["arena"] = arena
    t["ps"] = [nc.alloc_psum_tensor(f"psb{i}", [128, 512], F32) for i in range(8)]

    sem_names = ["pe", "ve", "se", "gp", "dcon", "dwA", "dwB", "dbf",
                 "dxa", "dxb", "dw0", "dw1", "dsp", "dxr", "dout"]
    stack = contextlib.ExitStack()
    sems = {n: stack.enter_context(nc.semaphore(n)) for n in sem_names}

    counters = []
    with nc.Block() as block:
        def run(ename):
            def f(eng):
                p = P(nc, ename, eng, t, sems)
                emit_all(p, dr, nseq, seq_tiles, win_tiles)
                counters.append(dict(p.c))
            return f
        block.tensor(run("T"))
        block.vector(run("V"))
        block.scalar(run("S"))
        block.gpsimd(run("G"))
        block.sync(run("Y"))
    stack.close()
    for c in counters[1:]:
        assert c == counters[0], ("counter mismatch", counters)
    return nc



def _barrier(p, engines="TVSY", dma=()):
    cur = {k: p.c[k] for k in ("pe", "ve", "se", "gp")}
    own = {"T": "pe", "V": "ve", "S": "se", "G": "gp"}
    for e in engines:
        for sem, v in cur.items():
            if own.get(e) != sem:
                p.w(e, sem, v)
        for dsem in dma:
            p.w(e, dsem, p.c[dsem])


def emit_all(p, dr, nseq, seq_tiles, win_tiles):
    t = p.t

    st = {
        "bank_free": [[] for _ in range(8)],
        "xbuf_free": [[], []],
        "xbf_free": [],
        "x_val": {},
        "xbf_val": {},
        "wslice_done": {},
        "ffn_slice": 0,
        "ffn_dma": 0,
        "sq_last_pe": 0,
        "l3_ve": 0,
        "l3_gp": 0,
        "ltmp_ve": 0,
        "ltmp_veB": 0,
        "wA_val": 0, "wB_val": 0,
    }

    def bank_wait(b):
        for semv in st["bank_free"][b]:
            p.w("T", *semv)
        st["bank_free"][b] = []

    def bank_consumed(b, sem="ve"):
        st["bank_free"][b].append((sem, p.c[sem]))

    for n, src in (("wfbd", "wfbd"), ("bfbc", "bfbc"), ("bo_t", "bo_t"), ("bf2", "bf2"),
                   ("s1", "s1_t"), ("s2", "s2_t"), ("sh1", "sh1_t"), ("sh2", "sh2_t"),
                   ("s3", "s3_t"), ("onestat", "onestat"), ("onesrow", "onesrow"),
                   ("pat2", "pat2"), ("bfrow", "bfrow"), ("warm512", "warm512")):
        p.dma("G", t[n][:, :], dr[src][:, :], "dcon")
    NCONST = p.c["dcon"]
    for e in "TVSY":
        p.w(e, "dcon", NCONST)
    p.w("G", "dcon", NCONST)

    out_dmas = [0]
    for s in range(nseq):
        emit_seq(p, dr, st, s, seq_tiles, win_tiles, bank_wait, bank_consumed, out_dmas)

    p.w("Y", "dout", p.c["dout"])


_WARM_PER = int(_os.environ.get("MK_WARM_PER", "0"))


def weave(p, bank_wait, wb, checkpoints, per=3):
    if _WARM_PER >= 0:
        per = _WARM_PER
    """Keep the PE clock warm while tracking a cross-engine chain: step through
    semaphore checkpoints, emitting a few dummy matmuls between each so PE-idle
    stretches stay under the HAM re-throttle window."""
    t = p.t
    bank_wait(wb)
    for sem, v in checkpoints:
        p.w("T", sem, v)
        for _ in range(per):
            p.mmwarm(t["ps"][wb][0:1, :], t["onestat"][:, 0:1], t["warm512"][:, :])


def ln_fm(p, st, bank_wait, bank_consumed, src_f32, src_bf, dst, stat_banks,
          x_add=None, src_ve=0, sq_ck=None, warm_bank=None, save_to=None):
    t = p.t
    ps = t["ps"]
    bm, bs = stat_banks
    se_sq = p.c["se"]
    bank_wait(bm)
    bank_wait(bs)
    if warm_bank is not None and sq_ck:
        weave(p, bank_wait, warm_bank, sq_ck, per=3)
    p.w("T", "se", se_sq)
    p.w("T", "ve", src_ve)
    for ic in range(NCH):
        p.mm(ps[bm][0:1, :], t["onestat"][:, 0:1], src_bf[ic],
             start=(ic == 0), stop=(ic == NCH - 1),
             sem="pe" if ic == NCH - 1 else None)
    for ic in range(NCH):
        p.mm(ps[bs][0:1, :], t["onestat"][:, 0:1], t["sq"][:, ic * TT:(ic + 1) * TT],
             start=(ic == 0), stop=(ic == NCH - 1),
             sem="pe" if ic == NCH - 1 else None)
    pe_stats = p.c["pe"]
    st["sq_last_pe"] = pe_stats
    p.w("V", "pe", pe_stats)
    p.vcopy(t["msb"][:, :], ps[bm][0:1, :])
    p.tt(t["vsb"][:, :], t["msb"][:, :], t["msb"][:, :], ALU.mult)
    p.stt(t["vsb"][:, :], ps[bs][0:1, :], float(EPS), t["vsb"][:, :],
          ALU.add, ALU.subtract)
    ve_v = p.c["ve"]
    p.w("S", "ve", ve_v)
    p.act(t["vsb"][:, :], t["vsb"][:, :], AF.Ln)
    p.act(t["vsb"][:, :], t["vsb"][:, :], AF.Exp, scale=-0.5)
    se_sqrt = p.c["se"]
    p.w("V", "se", se_sqrt)
    p.vcopy(t["r_bf"][:, :], t["vsb"][:, :])
    p.tt(t["msb"][:, :], t["msb"][:, :], t["vsb"][:, :], ALU.mult)
    p.vcopy(t["mr_bf"][:, :], t["msb"][:, :])
    if save_to is not None:
        p.vcopy(save_to[0], t["r_bf"][:, :])
        p.vcopy(save_to[1], t["mr_bf"][:, :])
    ve_r = p.c["ve"]
    if warm_bank is not None:
        ck = [("ve", ve_v), ("se", p.c["se"]), ("ve", ve_r)]
        weave(p, bank_wait, warm_bank, ck, per=4)
    p.w("T", "ve", ve_r)
    p.mm(ps[bm][:, :], t["onesrow"][:, :], t["r_bf"][:, :], start=True, stop=True,
         sem="pe")
    p.mm(ps[bs][:, :], t["onesrow"][:, :], t["mr_bf"][:, :], start=True, stop=True,
         sem="pe")
    pe_bc = p.c["pe"]
    p.w("V", "pe", pe_bc)
    apply_ck = []
    for ic in range(NCH):
        srcm = src_f32[ic] if src_f32 is not None else src_bf[ic]
        p.tt(t["ltmp"][:, :], srcm, ps[bm][:, :], ALU.mult)
        if x_add is None:
            p.tt(dst[ic], t["ltmp"][:, :], ps[bs][:, :], ALU.subtract)
        else:
            p.tt(t["ltmp2"][:, :], t["ltmp"][:, :], ps[bs][:, :], ALU.subtract)
            p.tt(dst[ic], t["ltmp2"][:, :], x_add[ic], ALU.add)
        apply_ck.append(("ve", p.c["ve"]))
    st["ltmp_ve"] = p.c["ve"]
    bank_consumed(bm)
    bank_consumed(bs)
    return {"pe_stats": pe_stats, "pe_bc": pe_bc, "ve_end": p.c["ve"],
            "gp_end": p.c["gp"], "apply_ck": apply_ck}


def emit_seq(p, dr, st, s, seq_tiles, win_tiles, bank_wait, bank_consumed, out_dmas):
    t = p.t
    ps = t["ps"]
    seq = seq_tiles * TT
    nwin = seq_tiles // win_tiles
    xbufs = [t["xb0"], t["xb1"]]
    dx = ["dxa", "dxb"]

    _barrier(p, engines="TVSY", dma=("dsp", "dout"))
    _barrier(p, engines="G")

    p.dma("G", t["WA"][:, :].rearrange("p (i o) -> p i o", o=D),
          dr["wk"].ap().rearrange("(i p) o -> p i o", p=128), "dwA")
    st["wA_val"] = p.c["dwA"]
    p.dma("G", t["WB"][:, :].rearrange("p (i o) -> p i o", o=D),
          dr["wv"].ap().rearrange("(i p) o -> p i o", p=128), "dwB")
    st["wB_val"] = p.c["dwB"]

    entries = [(1, i) for i in range(seq_tiles)] + [(2, i) for i in range(seq_tiles)]

    def load_entry(j):
        if j >= len(entries):
            return
        _, tile = entries[j]
        b = j % 2
        for semv in st["xbuf_free"][b]:
            p.w("Y", *semv)
        st["xbuf_free"][b] = []
        p.dma("Y", xbufs[b][:, :].rearrange("p (i tt) -> p i tt", tt=TT),
              dr["x"][s][:, :, tile * TT:(tile + 1) * TT].rearrange("i p tt -> p i tt"),
              dx[b])
        st["x_val"][j] = p.c[dx[b]]

    def load_xbf(j):
        if j >= len(entries):
            return
        _, tile = entries[j]
        for semv in st["xbf_free"]:
            p.w("G", *semv)
        st["xbf_free"] = []
        p.dma("G", t["xbf"][:, :].rearrange("p (i tt) -> p i tt", tt=TT),
              dr["x"][s][:, :, tile * TT:(tile + 1) * TT].rearrange("i p tt -> p i tt"),
              "dbf")
        st["xbf_val"][j] = p.c["dbf"]

    load_entry(0)
    load_entry(1)
    load_xbf(0)

    p.vmemset(t["kv"][:, :], 0.0)
    vtm_r = t["vtm"][:, :].rearrange("p (sl e) -> p sl e", e=132)
    p.vmemset(vtm_r[:, :, 128:129], 1.0)

    for tile in range(seq_tiles):
        j = tile
        b = j % 2
        xb = xbufs[b]
        xf = [xb[:, ic * TT:(ic + 1) * TT] for ic in range(NCH)]
        xbfl = [t["xbf"][:, ic * TT:(ic + 1) * TT] for ic in range(NCH)]
        p.w("S", dx[b], st["x_val"][j])
        p.w("V", dx[b], st["x_val"][j])
        p.w("T", "dbf", st["xbf_val"][j])
        p.w("S", "pe", st["sq_last_pe"])
        sq_ck = []
        for ic in range(NCH):
            p.act(t["sq"][:, ic * TT:(ic + 1) * TT], xf[ic], AF.Square)
            sq_ck.append(("se", p.c["se"]))
        xal = [t["xa"][:, ic * TT:(ic + 1) * TT] for ic in range(NCH)]
        sn1 = ln_fm(p, st, bank_wait, bank_consumed, xf, xbfl, xal, (6, 7),
                    sq_ck=sq_ck, warm_bank=0,
                    save_to=(t["r_save"][:, tile * TT:(tile + 1) * TT],
                             t["mr_save"][:, tile * TT:(tile + 1) * TT]))
        ve_xa = p.c["ve"]
        st["xbf_free"] = [("pe", p.c["pe"])]
        if j + 1 < seq_tiles:
            load_xbf(j + 1)

        p.w("T", "dwA", st["wA_val"])
        weave(p, bank_wait, 0, sn1["apply_ck"], per=4)
        p.w("T", "ve", ve_xa)
        p.w("T", "gp", sn1["gp_end"])
        kql = [t["kq"][:, oc * TT:(oc + 1) * TT] for oc in range(NCH)]
        for oc in range(NCH):
            bA, bB = (0, 1) if oc % 2 == 0 else (2, 3)
            bank_wait(bA)
            bank_wait(bB)
            for ic in range(NCH):
                p.mm2(ps[bA][:, :], ps[bB][:, :],
                      t["WA"][:, ic * D + oc * 128: ic * D + (oc + 1) * 128],
                      xal[ic], start=(ic == 0), stop=(ic == NCH - 1),
                      sem="pe" if ic == NCH - 1 else None)
            pe_k = p.c["pe"]
            p.w("S", "pe", pe_k)
            p.w("S", "ve", st["l3_ve"])
            p.w("S", "gp", st["l3_gp"])
            p.act(t["ltmp3"][:, :], ps[bA][:, :], AF.Identity)
            bank_consumed(bA, "se")
            se_i = p.c["se"]
            p.w("V", "se", se_i)
            p.tt(kql[oc], t["ltmp3"][:, :], ps[bB][:, :], ALU.add)
            st["l3_ve"] = p.c["ve"]
            bank_consumed(bB)
        se_k = p.c["se"]
        ve_k = p.c["ve"]
        p.w("T", "ve", ve_k)

        p.w("T", "dwB", st["wB_val"])
        for sub in range(4):
            pairs = ((0, 1), (2, 3)) if sub % 2 == 0 else ((4, 5), (6, 7))
            for vh in range(2):
                bA, bB = pairs[vh]
                bank_wait(bA)
                bank_wait(bB)
                for ic in range(NCH):
                    p.mm2(ps[bA][:, :], ps[bB][:, :],
                          xal[ic][:, sub * 128:(sub + 1) * 128],
                          t["WB"][:, ic * D + vh * 512: ic * D + (vh + 1) * 512],
                          start=(ic == 0), stop=(ic == NCH - 1),
                          sem="pe" if ic == NCH - 1 else None)
                pe_v = p.c["pe"]
                p.w("S", "pe", pe_v)
                p.w("S", "ve", st["l3_ve"])
                p.w("S", "gp", st["l3_gp"])
                p.act(t["ltmp3"][:, :], ps[bA][:, :], AF.Identity)
                bank_consumed(bA, "se")
                se_i = p.c["se"]
                p.w("V", "se", se_i)
                dstv = vtm_r[:, sub * 8 + vh * 4: sub * 8 + (vh + 1) * 4, 0:128]
                p.tt(dstv, t["ltmp3"][:, :].rearrange("p (c e) -> p c e", e=128),
                     ps[bB][:, :].rearrange("p (c e) -> p c e", e=128), ALU.add)
                st["l3_ve"] = p.c["ve"]
                bank_consumed(bB)

        kf_se = [0] * NCH
        for c in range(NCH):
            bk = 6 + c % 2
            bank_wait(bk)
            p.w("T", "se", se_k)
            for sub in range(4):
                p.mm(ps[bk][:, sub * 128:(sub + 1) * 128],
                     kql[c][:, sub * 128:(sub + 1) * 128],
                     t["wfbd"][:, :], start=True, stop=False)
                p.mm(ps[bk][:, sub * 128:(sub + 1) * 128],
                     t["onesrow"][:, :], t["bfrow"][:, :], start=False, stop=True,
                     sem="pe" if sub == 3 else None)
            pe_kf = p.c["pe"]
            p.w("S", "pe", pe_kf)
            kf_dst = t["kftm"][:, :].rearrange("p (s e) -> p s e", e=128)[:, c:c + 25:8, :]
            p.act(kf_dst, ps[bk][:, :].rearrange("p (s e) -> p s e", e=128), AF.Relu)
            kf_se[c] = p.c["se"]
            bank_consumed(bk, "se")

        for c in range(NCH):
            bk = 2 + c % 2
            bank_wait(bk)
            p.w("T", "se", kf_se[c])
            for sub in range(4):
                p.mm(ps[bk][:, 0:129],
                     t["kftm"][:, (sub * 8 + c) * 128:(sub * 8 + c) * 128 + 128],
                     t["vtm"][:, (sub * 8 + c) * 132:(sub * 8 + c) * 132 + 129],
                     start=(sub == 0), stop=(sub == 3),
                     sem="pe" if sub == 3 else None)
            pe_kv = p.c["pe"]
            p.w("V", "pe", pe_kv)
            p.tt(t["kv"][:, c * 132: c * 132 + 129], t["kv"][:, c * 132: c * 132 + 129],
                 ps[bk][:, 0:129], ALU.add)
            bank_consumed(bk)
        st["xbuf_free"][b] = [("ve", p.c["ve"]), ("se", p.c["se"]),
                              ("gp", p.c["gp"])]
        load_entry(j + 2)

    p.vmemset(t["BD"][:, :], 0.0)
    p.vmemset(t["ksc"][:, :], 0.0)
    for c in range(NCH):
        p.vcopy(t["BD"][0:64, c * 128: c * 128 + 64],
                t["kv"][0:64, c * 132: c * 132 + 64])
        p.vcopy(t["BD"][64:128, c * 128 + 64: c * 128 + 128],
                t["kv"][64:128, c * 132 + 64: c * 132 + 128])
        p.vcopy(t["ksc"][0:64, c * 2: c * 2 + 1],
                t["kv"][0:64, c * 132 + 128: c * 132 + 129])
        p.vcopy(t["ksc"][64:128, c * 2 + 1: c * 2 + 2],
                t["kv"][64:128, c * 132 + 128: c * 132 + 129])
    ve_bd = p.c["ve"]

    p.w("G", "pe", p.c["pe"])
    p.dma("G", t["WA"][:, :].rearrange("p (i o) -> p i o", o=D),
          dr["wq"].ap().rearrange("(i p) o -> p i o", p=128), "dwA")
    st["wA_val"] = p.c["dwA"]
    p.dma("G", t["WB"][:, :].rearrange("p (i o) -> p i o", o=D),
          dr["wo"].ap().rearrange("(i p) o -> p i o", p=128), "dwB")
    st["wB_val"] = p.c["dwB"]

    for tile in range(seq_tiles):
        j = seq_tiles + tile
        b = j % 2
        xb = xbufs[b]
        xf = [xb[:, ic * TT:(ic + 1) * TT] for ic in range(NCH)]
        xbfl = [t["xbf"][:, ic * TT:(ic + 1) * TT] for ic in range(NCH)]
        p.w("S", dx[b], st["x_val"][j])
        p.w("V", dx[b], st["x_val"][j])
        xal = [t["xa"][:, ic * TT:(ic + 1) * TT] for ic in range(NCH)]
        bm, bs = 6, 7
        bank_wait(bm)
        bank_wait(bs)
        p.mm(ps[bm][:, :], t["onesrow"][:, :],
             t["r_save"][:, tile * TT:(tile + 1) * TT], start=True, stop=True,
             sem="pe")
        p.mm(ps[bs][:, :], t["onesrow"][:, :],
             t["mr_save"][:, tile * TT:(tile + 1) * TT], start=True, stop=True,
             sem="pe")
        pe_bc1 = p.c["pe"]
        p.w("V", "pe", pe_bc1)
        apply_ck = []
        for ic in range(NCH):
            p.tt(t["ltmp"][:, :], xf[ic], ps[bm][:, :], ALU.mult)
            p.tt(xal[ic], t["ltmp"][:, :], ps[bs][:, :], ALU.subtract)
            apply_ck.append(("ve", p.c["ve"]))
        st["ltmp_ve"] = p.c["ve"]
        bank_consumed(bm)
        bank_consumed(bs)
        ve_xa = p.c["ve"]

        p.w("T", "dwA", st["wA_val"])
        weave(p, bank_wait, 0, apply_ck, per=4)
        p.w("T", "ve", ve_xa)
        kql = [t["kq"][:, oc * TT:(oc + 1) * TT] for oc in range(NCH)]
        for oc in range(NCH):
            bA, bB = (0, 1) if oc % 2 == 0 else (2, 3)
            bank_wait(bA)
            bank_wait(bB)
            for ic in range(NCH):
                p.mm2(ps[bA][:, :], ps[bB][:, :],
                      t["WA"][:, ic * D + oc * 128: ic * D + (oc + 1) * 128],
                      xal[ic], start=(ic == 0), stop=(ic == NCH - 1),
                      sem="pe" if ic == NCH - 1 else None)
            pe_q = p.c["pe"]
            p.w("S", "pe", pe_q)
            p.w("S", "ve", st["l3_ve"])
            p.w("S", "gp", st["l3_gp"])
            p.act(t["ltmp3"][:, :], ps[bA][:, :], AF.Identity)
            bank_consumed(bA, "se")
            se_i = p.c["se"]
            p.w("V", "se", se_i)
            p.tt(kql[oc], t["ltmp3"][:, :], ps[bB][:, :], ALU.add)
            st["l3_ve"] = p.c["ve"]
            bank_consumed(bB)
        ve_q = p.c["ve"]
        p.w("T", "ve", ve_q)

        qfl = [t["qf"][:, c * TT:(c + 1) * TT] for c in range(NCH)]
        qf_se = [0] * NCH
        for c in range(NCH):
            bk = 4 + c % 2
            bank_wait(bk)
            p.mm(ps[bk][:, :], t["wfbd"][:, :], kql[c], start=True, stop=True, sem="pe")
            pe_qf = p.c["pe"]
            p.w("S", "pe", pe_qf)
            p.act(qfl[c], ps[bk][:, :], AF.Relu, bias=t["bf2"][:, 0:1])
            qf_se[c] = p.c["se"]
            bank_consumed(bk, "se")

        if tile == 0:
            p.w("T", "ve", ve_bd)
        rcp2 = [t["rcp_bf"], t["rcp_bfb"]]
        rcp_done = [0] * NCH
        bcast_pe = [0] * NCH
        yl = [t["y"][:, c * TT:(c + 1) * TT] for c in range(NCH)]
        for c in range(NCH):
            bk = 4 + c % 2
            bo_ = 6 + c % 2
            bank_wait(bk)
            p.w("T", "se", qf_se[c])
            p.mm(ps[bk][0:2, :], t["ksc"][:, c * 2:(c + 1) * 2], qfl[c],
                 start=True, stop=True, sem="pe")
            pe_n = p.c["pe"]
            p.w("V", "pe", pe_n)
            p.ts(t["normsb"][:, :], ps[bk][0:2, :], 1e-6, ALU.max)
            bank_consumed(bk)
            p.vrecip(t["normsb"][:, :], t["normsb"][:, :])
            if c >= 2:
                p.w("V", "pe", bcast_pe[c - 2])
            p.vcopy(rcp2[c % 2][:, :], t["normsb"][:, :])
            rcp_done[c] = p.c["ve"]
            bank_wait(bk)
            p.w("T", "ve", rcp_done[c])
            p.mm(ps[bk][:, :], t["pat2"][:, :], rcp2[c % 2][:, :],
                 start=True, stop=True, sem="pe")
            bank_wait(bo_)
            p.mm(ps[bo_][:, :], t["BD"][:, c * 128:(c + 1) * 128], qfl[c],
                 start=True, stop=True, sem="pe")
            bcast_pe[c] = p.c["pe"]
            pe_o2 = p.c["pe"]
            p.w("V", "pe", pe_o2)
            p.vcopy(t["rbc"][:, :], ps[bk][:, :])
            bank_consumed(bk)
            p.tt(yl[c], ps[bo_][:, :], t["rbc"][:, :], ALU.mult)
            bank_consumed(bo_)
        ve_y = p.c["ve"]

        p.w("T", "dwB", st["wB_val"])
        p.w("T", "ve", ve_y)
        zl = [t["z"][:, oc * TT:(oc + 1) * TT] for oc in range(NCH)]
        p.w("S", "pe", st["sq_last_pe"])
        zsq_ck = []
        for oc in range(NCH):
            bA, bB = (0, 1) if oc % 2 == 0 else (2, 3)
            bank_wait(bA)
            bank_wait(bB)
            for ic in range(NCH):
                p.mm2(ps[bA][:, :], ps[bB][:, :],
                      t["WB"][:, ic * D + oc * 128: ic * D + (oc + 1) * 128],
                      yl[ic], start=(ic == 0), stop=(ic == NCH - 1),
                      sem="pe" if ic == NCH - 1 else None)
            pe_z = p.c["pe"]
            p.w("S", "pe", pe_z)
            p.w("S", "ve", st["l3_ve"])
            p.w("S", "gp", st["l3_gp"])
            p.act(t["ltmp3"][:, :], ps[bA][:, :], AF.Identity,
                  bias=t["bo_t"][:, oc:oc + 1])
            bank_consumed(bA, "se")
            se_i = p.c["se"]
            p.w("V", "se", se_i)
            p.tt(zl[oc], t["ltmp3"][:, :], ps[bB][:, :], ALU.add)
            st["l3_ve"] = p.c["ve"]
            bank_consumed(bB)
            ve_z = p.c["ve"]
            p.w("S", "ve", ve_z)
            p.act(t["sq"][:, oc * TT:(oc + 1) * TT], zl[oc], AF.Square)
            zsq_ck.append(("se", p.c["se"]))

        p.w("V", "dsp", p.c["dsp"])
        xpl = [t["xp"][:, ic * TT:(ic + 1) * TT] for ic in range(NCH)]
        sn2 = ln_fm(p, st, bank_wait, bank_consumed, None, zl, xpl, (4, 5), x_add=xf,
                    src_ve=p.c["ve"], sq_ck=zsq_ck, warm_bank=0)
        ve_xp = p.c["ve"]
        gp_xp = sn2["gp_end"]
        st["xbuf_free"][b] = [("ve", ve_xp), ("se", p.c["se"]), ("gp", gp_xp)]
        load_entry(j + 2)

        p.w("Y", "ve", ve_xp)
        p.w("Y", "gp", gp_xp)
        p.dma("Y", dr["xp_hbm"][s, tile], t["xp"][:, :], "dsp")


    _barrier(p, engines="TVS", dma=("dsp",))
    for w in range(nwin):
        emit_ffn_window(p, dr, st, s, w, seq_tiles, win_tiles, bank_wait,
                        bank_consumed, out_dmas)


def ffn_slice_dma(p, st, dr, which, idx):
    k = st["ffn_dma"]
    buf = p.t["wc0"] if k % 2 == 0 else p.t["wc1"]
    sem = "dw0" if k % 2 == 0 else "dw1"
    if k >= 2:
        done = st["wslice_done"].get(k - 2)
        if done is not None:
            p.w("G", "pe", done)
    p.dma("G", buf[:, :], dr[which][idx], sem)
    st["ffn_dma"] += 1


def ffn_slice_use(p, st):
    k = st["ffn_slice"]
    buf = p.t["wc0"] if k % 2 == 0 else p.t["wc1"]
    sem = "dw0" if k % 2 == 0 else "dw1"
    p.w("T", sem, 16 * (k // 2 + 1))
    st["ffn_slice"] += 1
    return buf, k


def emit_ffn_window(p, dr, st, s, w, seq_tiles, win_tiles, bank_wait, bank_consumed,
                    out_dmas):
    t = p.t
    ps = t["ps"]
    seq = seq_tiles * TT
    W = win_tiles * TT
    NH = win_tiles

    al = lambda oc, h: t["a"][:, oc * W + h * TT: oc * W + (h + 1) * TT]
    bl = lambda oc, h: t["b"][:, oc * W + h * TT: oc * W + (h + 1) * TT]
    xn_rhs = lambda ic, h: t["xnw"][:, h * 4096 + ic * TT: h * 4096 + (ic + 1) * TT]

    _barrier(p, engines="VS", dma=("dout", "dxr"))
    p.w("Y", "pe", p.c["pe"])
    p.w("Y", "ve", p.c["ve"])
    p.w("Y", "se", p.c["se"])
    p.w("Y", "dsp", p.c["dsp"])
    for ti in range(win_tiles):
        p.dma("Y", t["xpr"][:, ti * 4096:(ti + 1) * 4096],
              dr["xp_hbm"][s, w * win_tiles + ti], "dxr")
    dxr0 = p.c["dxr"]

    for ti in range(win_tiles):
        xpl3 = [t["xpr"][:, ti * 4096 + ic * TT: ti * 4096 + (ic + 1) * TT]
                for ic in range(NCH)]
        p.w("S", "dxr", dxr0)
        p.w("S", "pe", st["sq_last_pe"])
        sq3_ck = []
        x1bf = [t["a"][:, ic * TT:(ic + 1) * TT] for ic in range(NCH)]
        for ic in range(NCH):
            p.act(x1bf[ic], xpl3[ic], AF.Copy)
            p.act(t["sq"][:, ic * TT:(ic + 1) * TT], xpl3[ic], AF.Square)
            sq3_ck.append(("se", p.c["se"]))
        xnl = [t["xnw"][:, ti * 4096 + ic * TT: ti * 4096 + (ic + 1) * TT]
               for ic in range(NCH)]
        ln_fm(p, st, bank_wait, bank_consumed, xpl3, x1bf, xnl, (2, 3),
              sq_ck=sq3_ck, warm_bank=0)
    ve_xnw = p.c["ve"]
    p.w("T", "ve", ve_xnw)

    for slab in range(NCH):
        ffn_slice_dma(p, st, dr, "w1_t", slab)
        ffn_slice_dma(p, st, dr, "w2_t", slab)
        b1, k1 = ffn_slice_use(p, st)
        b2, k2 = ffn_slice_use(p, st)
        for ocl in range(4):
            oc = slab * 4 + ocl
            for h in range(NH):
                w1A, w1B = (0 + 2 * h, 1 + 2 * h)
                w2A, w2B = (4 + 2 * h, 5 + 2 * h)
                bank_wait(w1A)
                bank_wait(w1B)
                for ic in range(NCH):
                    p.mm2(ps[w1A][:, :], ps[w1B][:, :],
                          b1[:, ic * 512 + ocl * 128: ic * 512 + ocl * 128 + 128],
                          xn_rhs(ic, h), start=(ic == 0), stop=(ic == NCH - 1),
                          sem="pe" if ic == NCH - 1 else None)
                pe_w1 = p.c["pe"]
                bank_wait(w2A)
                bank_wait(w2B)
                for ic in range(NCH):
                    p.mm2(ps[w2A][:, :], ps[w2B][:, :],
                          b2[:, ic * 512 + ocl * 128: ic * 512 + ocl * 128 + 128],
                          xn_rhs(ic, h), start=(ic == 0), stop=(ic == NCH - 1),
                          sem="pe" if ic == NCH - 1 else None)
                pe_w2 = p.c["pe"]
                lt0, lt2, lt3 = (("ltmp", "ltmp2", "ltmp3") if h == 0 else
                                 ("ltmpB", "ltmp2B", "ltmp3B"))
                lkey = "ltmp_ve" if h == 0 else "ltmp_veB"
                p.w("S", "pe", pe_w1)
                p.w("S", "ve", st[lkey])
                p.act(t[lt2][:, :], ps[w1A][:, :], AF.Identity)
                bank_consumed(w1A, "se")
                se_i1 = p.c["se"]
                p.w("V", "se", se_i1)
                p.tt(t[lt3][:, :], t[lt2][:, :], ps[w1B][:, :], ALU.add)
                bank_consumed(w1B)
                ve_s1 = p.c["ve"]
                p.w("S", "ve", ve_s1)
                p.act(t[lt0][:, :], t[lt3][:, :], AF.Silu,
                      scale=t["s1"][:, oc:oc + 1])
                se_t1 = p.c["se"]
                p.w("S", "pe", pe_w2)
                p.act(t[lt2][:, :], ps[w2A][:, :], AF.Identity)
                bank_consumed(w2A, "se")
                se_i2 = p.c["se"]
                p.w("V", "pe", pe_w2)
                p.w("V", "se", se_i2)
                p.tt(t[lt3][:, :], t[lt2][:, :], ps[w2B][:, :], ALU.add)
                bank_consumed(w2B)
                p.stt(al(oc, h), t[lt3][:, :], t["s2"][:, oc:oc + 1],
                      t[lt0][:, :], ALU.mult, ALU.mult)
                st[lkey] = p.c["ve"]
        st["wslice_done"][k1] = p.c["pe"]
        st["wslice_done"][k2] = p.c["pe"]

    ve_a = p.c["ve"]
    for wt, sc_t, srcf, dstf in (("h1_t", "sh1", al, bl), ("h2_t", "sh2", bl, al)):
        if wt == "h1_t":
            p.w("T", "ve", ve_a)
        else:
            p.w("T", "se", p.c["se"])
        for oc in range(NHC):
            ffn_slice_dma(p, st, dr, wt, oc)
            wb, k = ffn_slice_use(p, st)
            for h in range(NH):
                bA = 4 * (oc % 2) + 2 * h
                bB = bA + 1
                bank_wait(bA)
                bank_wait(bB)
                for ic in range(NHC):
                    p.mm2(ps[bA][:, :], ps[bB][:, :], wb[:, ic * 128:(ic + 1) * 128],
                          srcf(ic, h), start=(ic == 0), stop=(ic == NHC - 1),
                          sem="pe" if ic == NHC - 1 else None)
                pe_h = p.c["pe"]
                lt0, lt2 = ("ltmp", "ltmp2") if h == 0 else ("ltmpB", "ltmp2B")
                lkey = "ltmp_ve" if h == 0 else "ltmp_veB"
                p.w("S", "pe", pe_h)
                p.w("S", "ve", st[lkey])
                p.act(t[lt0][:, :], ps[bA][:, :], AF.Identity)
                bank_consumed(bA, "se")
                se_i = p.c["se"]
                p.w("V", "se", se_i)
                p.tt(t[lt2][:, :], t[lt0][:, :], ps[bB][:, :], ALU.add)
                bank_consumed(bB)
                st[lkey] = p.c["ve"]
                p.w("S", "ve", st[lkey])
                p.act(dstf(oc, h), t[lt2][:, :], AF.Gelu,
                      scale=t[sc_t][:, oc:oc + 1])
            st["wslice_done"][k] = p.c["pe"]

    p.w("Y", "pe", p.c["pe"])
    p.w("Y", "dsp", p.c["dsp"])
    for ti in range(win_tiles):
        p.dma("Y", t["xpr"][:, ti * 4096:(ti + 1) * 4096],
              dr["xp_hbm"][s, w * win_tiles + ti], "dxr")
    dxr_val = p.c["dxr"]
    fl = lambda oc, h: t["f"][:, oc * W + h * TT: oc * W + (h + 1) * TT]
    p.w("T", "se", p.c["se"])
    for oc in range(NCH):
        ffn_slice_dma(p, st, dr, "w3_t", oc)
        wb, k = ffn_slice_use(p, st)
        for h in range(NH):
            bA = 4 * (oc % 2) + 2 * h
            bB = bA + 1
            bank_wait(bA)
            bank_wait(bB)
            for ic in range(NHC):
                p.mm2(ps[bA][:, :], ps[bB][:, :], wb[:, ic * 128:(ic + 1) * 128],
                      al(ic, h), start=(ic == 0), stop=(ic == NHC - 1),
                      sem="pe" if ic == NHC - 1 else None)
            pe_f = p.c["pe"]
            lt0 = "ltmp" if h == 0 else "ltmpB"
            lkey = "ltmp_ve" if h == 0 else "ltmp_veB"
            p.w("S", "pe", pe_f)
            p.w("S", "ve", st[lkey])
            p.act(t[lt0][:, :], ps[bA][:, :], AF.Identity,
                  scale=t["s3"][:, oc:oc + 1])
            bank_consumed(bA, "se")
            se_i = p.c["se"]
            p.w("V", "se", se_i)
            p.stt(fl(oc, h), ps[bB][:, :], t["s3"][:, oc:oc + 1], t[lt0][:, :],
                  ALU.mult, ALU.add)
            bank_consumed(bB)
            st[lkey] = p.c["ve"]
        st["wslice_done"][k] = p.c["pe"]

    p.w("V", "dxr", dxr_val)
    for h in range(NH):
        fbf = [fl(ic, h) for ic in range(NCH)]
        p.w("S", "pe", st["sq_last_pe"])
        p.w("S", "ve", st["ltmp_ve"])
        p.w("S", "ve", st["ltmp_veB"])
        sq4_ck = []
        for ic in range(NCH):
            p.act(t["sq"][:, ic * TT:(ic + 1) * TT], fbf[ic], AF.Square)
            sq4_ck.append(("se", p.c["se"]))
        xprl = [t["xpr"][:, h * 4096 + ic * TT: h * 4096 + (ic + 1) * TT]
                for ic in range(NCH)]
        outl = [t["outt"][:, ic * TT:(ic + 1) * TT] for ic in range(NCH)]
        p.w("V", "dout", p.c["dout"])
        ln_fm(p, st, bank_wait, bank_consumed, None, fbf, outl, (0, 1), x_add=xprl,
              src_ve=p.c["ve"], sq_ck=sq4_ck, warm_bank=4)
        ve_out = p.c["ve"]
        p.w("Y", "ve", ve_out)
        p.w("Y", "gp", p.c["gp"])
        tglob = w * win_tiles + h
        p.dma("Y",
              dr["out"][s][:, :, tglob * TT:(tglob + 1) * TT].rearrange("i p tt -> p i tt"),
              t["outt"][:, :].rearrange("p (i tt) -> p i tt", tt=TT), "dout")
        out_dmas[0] += 1



def _prep_inputs(L0_attn_x, L0_attn_Wq, L0_attn_Wk, L0_attn_Wv, L0_attn_Wo,
                 L0_attn_bo, L0_attn_Wf, L0_attn_bf,
                 L0_ffn_w1_w, L0_ffn_w1_s, L0_ffn_w2_w, L0_ffn_w2_s,
                 L0_ffn_h1_w, L0_ffn_h1_s, L0_ffn_h2_w, L0_ffn_h2_s,
                 L0_ffn_w3_w, L0_ffn_w3_s):
    f32 = np.float32
    asc = np.ascontiguousarray

    def tile_oi(wT, n_in_ch, n_out_ch, ocp):
        r = wT.reshape(n_in_ch, 128, n_out_ch, ocp).transpose(2, 1, 0, 3)
        return asc(r.reshape(n_out_ch, 128, n_in_ch * ocp).astype(f32))

    sh = {}
    for n, wmat in (("wq", L0_attn_Wq), ("wk", L0_attn_Wk), ("wv", L0_attn_Wv),
                    ("wo", L0_attn_Wo)):
        sh[n] = asc(np.asarray(wmat, dtype=f32))
    wf = np.asarray(L0_attn_Wf, dtype=f32)
    wfbd = np.zeros((128, 128), f32)
    wfbd[:64, :64] = wf
    wfbd[64:, 64:] = wf
    sh["wfbd"] = wfbd
    bfv = np.asarray(L0_attn_bf, dtype=f32)
    sh["bfbc"] = asc(np.broadcast_to(np.concatenate([bfv, bfv])[None, :],
                                     (128, 128)).astype(f32))
    sh["bf2"] = asc(np.concatenate([bfv, bfv])[:, None].astype(f32))
    sh["bo_t"] = asc(np.asarray(L0_attn_bo, dtype=f32).reshape(NCH, 128).T)
    w1T = np.asarray(L0_ffn_w1_w, dtype=f32).T
    w2T = np.asarray(L0_ffn_w2_w, dtype=f32).T
    h1T = np.asarray(L0_ffn_h1_w, dtype=f32).T
    h2T = np.asarray(L0_ffn_h2_w, dtype=f32).T
    w3T = np.asarray(L0_ffn_w3_w, dtype=f32).T
    sh["w1_t"] = tile_oi(w1T, 8, 8, 512)
    sh["w2_t"] = tile_oi(w2T, 8, 8, 512)
    sh["h1_t"] = tile_oi(h1T, 32, 32, 128)
    sh["h2_t"] = tile_oi(h2T, 32, 32, 128)
    sh["w3_t"] = tile_oi(w3T, 32, 8, 128)
    for n, v in (("s1_t", L0_ffn_w1_s), ("s2_t", L0_ffn_w2_s),
                 ("sh1_t", L0_ffn_h1_s), ("sh2_t", L0_ffn_h2_s)):
        sh[n] = asc(np.asarray(v, dtype=f32).reshape(-1, 128).T)
    sh["s3_t"] = asc(np.asarray(L0_ffn_w3_s, dtype=f32).reshape(NCH, 128).T)
    pat = np.zeros((2, 128), f32)
    pat[0, :64] = 1.0
    pat[1, 64:] = 1.0
    sh["pat2"] = pat
    sh["bfrow"] = asc(np.concatenate([bfv, bfv])[None, :].astype(f32))
    sh["warm512"] = np.ones((128, TT), f32)
    sh["onestat"] = np.full((128, 1), 1.0 / D, f32)
    sh["onesrow"] = np.ones((1, 128), f32)

    x = np.asarray(L0_attn_x, dtype=f32)
    percore = []
    for c in range(8):
        xc = x[c * 2:(c + 1) * 2]
        xt = xc.transpose(0, 2, 1).reshape(2, NCH, 128, xc.shape[1])
        percore.append(asc(xt.astype(f32)))
    return sh, percore


_NC_CACHE = {}


def kernel(**inputs):
    sh, percore = _prep_inputs(**inputs)
    if "full" not in _NC_CACHE:
        _NC_CACHE["full"] = build_program()
    nc = _NC_CACHE["full"]
    in_maps = []
    for c in range(8):
        m = dict(sh)
        m["x"] = percore[c]
        in_maps.append(m)
    res = run_bass_kernel_spmd(nc, in_maps, core_ids=list(range(8)))
    outs = []
    for c in range(8):
        o = res.results[c]["out"]
        o = o.reshape(2, D, -1).transpose(0, 2, 1)
        outs.append(o)
    return np.ascontiguousarray(np.concatenate(outs, axis=0)).astype(np.float32)


# version 10
# speedup vs baseline: 1.0782x; 1.0202x over previous
import contextlib
import os as _os
import numpy as np
import concourse.bass as bass
import concourse.mybir as mybir
from concourse.bass_utils import run_bass_kernel_spmd

F32 = mybir.dt.float32
BF16 = mybir.dt.bfloat16
U8 = mybir.dt.uint8
AF = mybir.ActivationFunctionType
ALU = mybir.AluOpType

D = 1024
NCH = 8
HID = 4096
NHC = 32
TT = 512
NSEQ = 2
SEQ_TILES = 4
WIN_TILES = 2
EPS = 1e-5

KB = 1024


class P:

    def __init__(self, nc, ename, eng, t, sems):
        self.nc = nc
        self.E = ename
        self.h = eng
        self.t = t
        self.S = sems
        self.c = {k: 0 for k in sems}

    def bump(self, sem, n):
        self.c[sem] += n
        return self.c[sem]

    def _inc(self, ins, sem, n):
        if sem is None:
            return
        self.bump(sem, n)
        if ins is not None:
            ins.then_inc(self.S[sem], n)

    def w(self, eng, sem, val):
        if val <= 0:
            return
        if self.E == eng:
            self.h.wait_ge(self.S[sem], val)

    def mm(self, out, lhsT, rhs, start, stop, sem=None):
        ins = None
        if self.E == "T":
            ins = self.h.matmul(out, lhsT, rhs, start=start, stop=stop)
        self._inc(ins, sem, 1)

    def gtt(self, out, in0, in1, op, sem="gp"):
        ins = None
        if self.E == "G":
            if self.c["gp"] > 0:
                self.h.wait_ge(self.S["gp"], self.c["gp"])
            ins = self.h.tensor_tensor(out, in0, in1, op)
        self._inc(ins, sem, 1)

    def gw(self, sem, val):
        self.w("G", sem, val)

    def mmwarm(self, out, lhsT, rhs):
        if self.E == "T":
            self.h.matmul(out, lhsT, rhs, start=True, stop=True)

    def mm2(self, outA, outB, lhsT, rhs, start, stop, sem=None):
        ins = None
        if self.E == "T":
            self.h.matmul(outA, lhsT[0:64, :], rhs[0:64, :], start=start,
                          stop=stop, tile_position=(0, 0))
            ins = self.h.matmul(outB, lhsT[64:128, :], rhs[64:128, :], start=start,
                                stop=stop, tile_position=(64, 0))
        self._inc(ins, sem, 1)

    def act(self, out, in_, func, bias=0.0, scale=1.0, sem="se"):
        ins = None
        if self.E == "S":
            if self.c["se"] > 0:
                self.h.wait_ge(self.S["se"], self.c["se"])
            ins = self.h.activation(out, in_, func, bias=bias, scale=scale)
        self._inc(ins, sem, 1)

    def _vsync(self):
        if self.E == "V" and self.c["ve"] > 0:
            self.h.wait_ge(self.S["ve"], self.c["ve"])

    def tt(self, out, in0, in1, op, sem="ve"):
        ins = None
        if self.E == "V":
            self._vsync()
            ins = self.h.tensor_tensor(out, in0, in1, op)
        self._inc(ins, sem, 1)

    def ts(self, out, in0, s1, op, sem="ve"):
        ins = None
        if self.E == "V":
            self._vsync()
            ins = self.h.tensor_single_scalar(out, in0, s1, op)
        self._inc(ins, sem, 1)

    def stt(self, out, in0, scal, in1, op0, op1, sem="ve"):
        ins = None
        if self.E == "V":
            self._vsync()
            ins = self.h.scalar_tensor_tensor(out, in0, scal, in1, op0, op1)
        self._inc(ins, sem, 1)

    def vcopy(self, out, in_, sem="ve"):
        ins = None
        if self.E == "V":
            self._vsync()
            ins = self.h.tensor_copy(out, in_)
        self._inc(ins, sem, 1)

    def vrecip(self, out, in_, sem="ve"):
        ins = None
        if self.E == "V":
            self._vsync()
            ins = self.h.reciprocal(out, in_)
        self._inc(ins, sem, 1)

    def vmemset(self, ap, val, sem="ve"):
        ins = None
        if self.E == "V":
            self._vsync()
            ins = self.h.memset(ap, val)
        self._inc(ins, sem, 1)

    def dma(self, q, out, in_, sem):
        ins = None
        if self.E == q:
            ins = self.h.dma_start(out=out, in_=in_)
        self._inc(ins, sem, 16)


def _alloc_views(nc, arena, layout):
    base = nc.lookup_mloc(arena).addr
    out = {}
    for name, shape, dtype, off in layout:
        out[name] = nc.alloc_sbuf_tensor_at(name, shape, dtype, offset=base + off)
    return out


def build_program(nseq=NSEQ, seq_tiles=SEQ_TILES, win_tiles=WIN_TILES):
    nc = bass.Bass("TRN2", target_bir_lowering=False)
    seq = seq_tiles * TT
    W = win_tiles * TT

    dr = {}
    dr["x"] = nc.dram_tensor("x", [nseq, NCH, 128, seq], F32, kind="ExternalInput")
    dr["out"] = nc.dram_tensor("out", [nseq, NCH, 128, seq], F32, kind="ExternalOutput")
    for n in ("wq", "wk", "wv", "wo"):
        dr[n] = nc.dram_tensor(n, [D, D], F32, kind="ExternalInput")
    for n, shp in (("wfbd", [128, 128]), ("bfbc", [128, 128]), ("bo_t", [128, NCH]),
                   ("bf2", [128, 1]),
                   ("w1_t", [NCH, 128, 4096]), ("w2_t", [NCH, 128, 4096]),
                   ("h1_t", [NHC, 128, 4096]), ("h2_t", [NHC, 128, 4096]),
                   ("w3_t", [NCH, 128, 4096]),
                   ("s1_t", [128, NHC]), ("s2_t", [128, NHC]),
                   ("sh1_t", [128, NHC]), ("sh2_t", [128, NHC]), ("s3_t", [128, NCH]),
                   ("pat2", [2, 128]), ("onestat", [128, 1]), ("onesrow", [1, 128]),
                   ("bfrow", [1, 128]), ("warm512", [128, TT])):
        dr[n] = nc.dram_tensor(n, shp, F32, kind="ExternalInput")
    dr["xp_hbm"] = nc.dram_tensor("xp_hbm", [nseq, seq_tiles, 128, 4096], F32,
                                  kind="Internal")

    ARENA = 206 * KB
    arena = nc.alloc_sbuf_tensor("arena", [128, ARENA], U8)
    SC = 3584
    B0 = SC + 49 * KB
    lay = [
        ("wfbd", [128, 128], BF16, 0),
        ("bfbc", [128, 128], BF16, 256),
        ("bo_t", [128, NCH], F32, 512),
        ("bf2", [128, 1], F32, 544),
        ("s1", [128, NHC], F32, 576),
        ("s2", [128, NHC], F32, 704),
        ("sh1", [128, NHC], F32, 832),
        ("sh2", [128, NHC], F32, 960),
        ("s3", [128, NCH], F32, 1088),
        ("onestat", [128, 1], BF16, 1120),
        ("onesrow", [1, 128], BF16, 1152),
        ("bfrow", [1, 128], BF16, 1408),
        ("warm512", [128, TT], BF16, 1920),
        ("pat2", [2, 128], BF16, 1664),
        ("ltmp", [128, TT], F32, SC),
        ("ltmp2", [128, TT], F32, SC + 2 * KB),
        ("msb", [1, TT], F32, SC + 4 * KB),
        ("vsb", [1, TT], F32, SC + 6 * KB),
        ("r_bf", [1, TT], BF16, SC + 8 * KB),
        ("mr_bf", [1, TT], BF16, SC + 9 * KB),
        ("normsb", [2, TT], F32, SC + 10 * KB),
        ("rcp_bf", [2, TT], BF16, SC + 12 * KB),
        ("rcp_bfb", [2, TT], BF16, SC + 47 * KB),
        ("r_save", [1, 4 * TT], BF16, SC + 13 * KB),
        ("mr_save", [1, 4 * TT], BF16, SC + 17 * KB),
        ("sq", [128, 4096], BF16, SC + 21 * KB),
        ("wc0", [128, 4096], BF16, SC + 29 * KB),
        ("wc1", [128, 4096], BF16, SC + 37 * KB),
        ("ltmp3", [128, TT], F32, SC + 45 * KB),
        ("ltmpB", [128, TT], F32, B0 + 138 * KB),
        ("ltmp2B", [128, TT], F32, B0 + 140 * KB),
        ("ltmp3B", [128, TT], F32, B0 + 142 * KB),
        ("wc2", [128, 4096], BF16, B0 + 144 * KB),

        ("xb0", [128, 4096], F32, B0 + 0),
        ("xb1", [128, 4096], F32, B0 + 16 * KB),
        ("xbf", [128, 4096], BF16, B0 + 32 * KB),
        ("WA", [128, NCH * D], BF16, B0 + 40 * KB),
        ("WB", [128, NCH * D], BF16, B0 + 56 * KB),
        ("xa", [128, 4096], BF16, B0 + 72 * KB),
        ("kq", [128, 4096], BF16, B0 + 80 * KB),
        ("vtm", [128, 32 * 132], BF16, B0 + 88 * KB),
        ("qf", [128, 4096], BF16, B0 + 88 * KB),
        ("kftm", [128, 4096], BF16, B0 + 97 * KB),
        ("y", [128, 4096], BF16, B0 + 97 * KB),
        ("kv", [128, NCH * 132], F32, B0 + 105 * KB),
        ("BD", [128, NCH * 128], BF16, B0 + 110 * KB),
        ("ksc", [128, NCH * 2], BF16, B0 + 112 * KB),
        ("rbc", [128, TT], BF16, B0 + 113 * KB),
        ("z", [128, 4096], BF16, B0 + 114 * KB),
        ("xp", [128, 4096], F32, B0 + 122 * KB),
        ("a", [128, NHC * W], BF16, B0 + 0),
        ("b", [128, NHC * W], BF16, B0 + NHC * W * 2),
        ("xnw", [128, win_tiles * 4096], BF16,
         B0 + NHC * W * 2 + win_tiles * 16 * KB),
        ("xpr", [128, win_tiles * 4096], F32, B0 + NHC * W * 2),
        ("f", [128, NCH * W], BF16, B0 + NHC * W * 2 + win_tiles * 16 * KB),
        ("outt", [128, 4096], F32, B0 + NHC * W * 2 + win_tiles * 16 * KB + NCH * W * 2),
    ]
    need = max(B0 + 2 * NHC * W * 2, B0 + 152 * KB)
    assert need <= ARENA, ("big region overflow", need, ARENA)
    t = _alloc_views(nc, arena, lay)
    t["arena"] = arena
    t["ps"] = [nc.alloc_psum_tensor(f"psb{i}", [128, 512], F32) for i in range(8)]

    sem_names = ["pe", "ve", "se", "gp", "dcon", "dwA", "dwB", "dbf",
                 "dxa", "dxb", "dw0", "dw1", "dw2", "dsp", "dxr", "dout"]
    stack = contextlib.ExitStack()
    sems = {n: stack.enter_context(nc.semaphore(n)) for n in sem_names}

    counters = []
    with nc.Block() as block:
        def run(ename):
            def f(eng):
                p = P(nc, ename, eng, t, sems)
                emit_all(p, dr, nseq, seq_tiles, win_tiles)
                counters.append(dict(p.c))
            return f
        block.tensor(run("T"))
        block.vector(run("V"))
        block.scalar(run("S"))
        block.gpsimd(run("G"))
        block.sync(run("Y"))
    stack.close()
    for c in counters[1:]:
        assert c == counters[0], ("counter mismatch", counters)
    return nc



def _barrier(p, engines="TVSY", dma=()):
    cur = {k: p.c[k] for k in ("pe", "ve", "se", "gp")}
    own = {"T": "pe", "V": "ve", "S": "se", "G": "gp"}
    for e in engines:
        for sem, v in cur.items():
            if own.get(e) != sem:
                p.w(e, sem, v)
        for dsem in dma:
            p.w(e, dsem, p.c[dsem])


def emit_all(p, dr, nseq, seq_tiles, win_tiles):
    t = p.t

    st = {
        "bank_free": [[] for _ in range(8)],
        "xbuf_free": [[], []],
        "xbf_free": [],
        "x_val": {},
        "xbf_val": {},
        "wslice_done": {},
        "ffn_slice": 0,
        "ffn_dma": 0,
        "sq_last_pe": 0,
        "l3_ve": 0,
        "l3_gp": 0,
        "ltmp_ve": 0,
        "ltmp_veB": 0,
        "wA_val": 0, "wB_val": 0,
    }

    def bank_wait(b):
        for semv in st["bank_free"][b]:
            p.w("T", *semv)
        st["bank_free"][b] = []

    def bank_consumed(b, sem="ve"):
        st["bank_free"][b].append((sem, p.c[sem]))

    for n, src in (("wfbd", "wfbd"), ("bfbc", "bfbc"), ("bo_t", "bo_t"), ("bf2", "bf2"),
                   ("s1", "s1_t"), ("s2", "s2_t"), ("sh1", "sh1_t"), ("sh2", "sh2_t"),
                   ("s3", "s3_t"), ("onestat", "onestat"), ("onesrow", "onesrow"),
                   ("pat2", "pat2"), ("bfrow", "bfrow"), ("warm512", "warm512")):
        p.dma("G", t[n][:, :], dr[src][:, :], "dcon")
    NCONST = p.c["dcon"]
    for e in "TVSY":
        p.w(e, "dcon", NCONST)
    p.w("G", "dcon", NCONST)

    out_dmas = [0]
    for s in range(nseq):
        emit_seq(p, dr, st, s, seq_tiles, win_tiles, bank_wait, bank_consumed, out_dmas)

    p.w("Y", "dout", p.c["dout"])


_WARM_PER = int(_os.environ.get("MK_WARM_PER", "0"))


def weave(p, bank_wait, wb, checkpoints, per=3):
    if _WARM_PER >= 0:
        per = _WARM_PER
    """Keep the PE clock warm while tracking a cross-engine chain: step through
    semaphore checkpoints, emitting a few dummy matmuls between each so PE-idle
    stretches stay under the HAM re-throttle window."""
    t = p.t
    bank_wait(wb)
    for sem, v in checkpoints:
        p.w("T", sem, v)
        for _ in range(per):
            p.mmwarm(t["ps"][wb][0:1, :], t["onestat"][:, 0:1], t["warm512"][:, :])


def ln_fm(p, st, bank_wait, bank_consumed, src_f32, src_bf, dst, stat_banks,
          x_add=None, src_ve=0, sq_ck=None, warm_bank=None, save_to=None):
    t = p.t
    ps = t["ps"]
    bm, bs = stat_banks
    se_sq = p.c["se"]
    bank_wait(bm)
    bank_wait(bs)
    if warm_bank is not None and sq_ck:
        weave(p, bank_wait, warm_bank, sq_ck, per=3)
    p.w("T", "se", se_sq)
    p.w("T", "ve", src_ve)
    for ic in range(NCH):
        p.mm(ps[bm][0:1, :], t["onestat"][:, 0:1], src_bf[ic],
             start=(ic == 0), stop=(ic == NCH - 1),
             sem="pe" if ic == NCH - 1 else None)
    for ic in range(NCH):
        p.mm(ps[bs][0:1, :], t["onestat"][:, 0:1], t["sq"][:, ic * TT:(ic + 1) * TT],
             start=(ic == 0), stop=(ic == NCH - 1),
             sem="pe" if ic == NCH - 1 else None)
    pe_stats = p.c["pe"]
    st["sq_last_pe"] = pe_stats
    p.w("V", "pe", pe_stats)
    p.vcopy(t["msb"][:, :], ps[bm][0:1, :])
    p.tt(t["vsb"][:, :], t["msb"][:, :], t["msb"][:, :], ALU.mult)
    p.stt(t["vsb"][:, :], ps[bs][0:1, :], float(EPS), t["vsb"][:, :],
          ALU.add, ALU.subtract)
    ve_v = p.c["ve"]
    p.w("S", "ve", ve_v)
    p.act(t["vsb"][:, :], t["vsb"][:, :], AF.Ln)
    p.act(t["vsb"][:, :], t["vsb"][:, :], AF.Exp, scale=-0.5)
    se_sqrt = p.c["se"]
    p.w("V", "se", se_sqrt)
    p.vcopy(t["r_bf"][:, :], t["vsb"][:, :])
    p.tt(t["msb"][:, :], t["msb"][:, :], t["vsb"][:, :], ALU.mult)
    p.vcopy(t["mr_bf"][:, :], t["msb"][:, :])
    if save_to is not None:
        p.vcopy(save_to[0], t["r_bf"][:, :])
        p.vcopy(save_to[1], t["mr_bf"][:, :])
    ve_r = p.c["ve"]
    if warm_bank is not None:
        ck = [("ve", ve_v), ("se", p.c["se"]), ("ve", ve_r)]
        weave(p, bank_wait, warm_bank, ck, per=4)
    p.w("T", "ve", ve_r)
    p.mm(ps[bm][:, :], t["onesrow"][:, :], t["r_bf"][:, :], start=True, stop=True,
         sem="pe")
    p.mm(ps[bs][:, :], t["onesrow"][:, :], t["mr_bf"][:, :], start=True, stop=True,
         sem="pe")
    pe_bc = p.c["pe"]
    p.w("V", "pe", pe_bc)
    apply_ck = []
    for ic in range(NCH):
        srcm = src_f32[ic] if src_f32 is not None else src_bf[ic]
        p.tt(t["ltmp"][:, :], srcm, ps[bm][:, :], ALU.mult)
        if x_add is None:
            p.tt(dst[ic], t["ltmp"][:, :], ps[bs][:, :], ALU.subtract)
        else:
            p.tt(t["ltmp2"][:, :], t["ltmp"][:, :], ps[bs][:, :], ALU.subtract)
            p.tt(dst[ic], t["ltmp2"][:, :], x_add[ic], ALU.add)
        apply_ck.append(("ve", p.c["ve"]))
    st["ltmp_ve"] = p.c["ve"]
    bank_consumed(bm)
    bank_consumed(bs)
    return {"pe_stats": pe_stats, "pe_bc": pe_bc, "ve_end": p.c["ve"],
            "gp_end": p.c["gp"], "apply_ck": apply_ck}


def emit_seq(p, dr, st, s, seq_tiles, win_tiles, bank_wait, bank_consumed, out_dmas):
    t = p.t
    ps = t["ps"]
    seq = seq_tiles * TT
    nwin = seq_tiles // win_tiles
    xbufs = [t["xb0"], t["xb1"]]
    dx = ["dxa", "dxb"]

    _barrier(p, engines="TVSY", dma=("dsp", "dout"))
    _barrier(p, engines="G")

    p.dma("G", t["WA"][:, :].rearrange("p (i o) -> p i o", o=D),
          dr["wk"].ap().rearrange("(i p) o -> p i o", p=128), "dwA")
    st["wA_val"] = p.c["dwA"]
    p.dma("G", t["WB"][:, :].rearrange("p (i o) -> p i o", o=D),
          dr["wv"].ap().rearrange("(i p) o -> p i o", p=128), "dwB")
    st["wB_val"] = p.c["dwB"]

    entries = [(1, i) for i in range(seq_tiles)] + [(2, i) for i in range(seq_tiles)]

    def load_entry(j):
        if j >= len(entries):
            return
        _, tile = entries[j]
        b = j % 2
        for semv in st["xbuf_free"][b]:
            p.w("Y", *semv)
        st["xbuf_free"][b] = []
        p.dma("Y", xbufs[b][:, :].rearrange("p (i tt) -> p i tt", tt=TT),
              dr["x"][s][:, :, tile * TT:(tile + 1) * TT].rearrange("i p tt -> p i tt"),
              dx[b])
        st["x_val"][j] = p.c[dx[b]]

    def load_xbf(j):
        if j >= len(entries):
            return
        _, tile = entries[j]
        for semv in st["xbf_free"]:
            p.w("G", *semv)
        st["xbf_free"] = []
        p.dma("G", t["xbf"][:, :].rearrange("p (i tt) -> p i tt", tt=TT),
              dr["x"][s][:, :, tile * TT:(tile + 1) * TT].rearrange("i p tt -> p i tt"),
              "dbf")
        st["xbf_val"][j] = p.c["dbf"]

    load_entry(0)
    load_entry(1)
    load_xbf(0)

    p.vmemset(t["kv"][:, :], 0.0)
    vtm_r = t["vtm"][:, :].rearrange("p (sl e) -> p sl e", e=132)
    p.vmemset(vtm_r[:, :, 128:129], 1.0)

    for tile in range(seq_tiles):
        j = tile
        b = j % 2
        xb = xbufs[b]
        xf = [xb[:, ic * TT:(ic + 1) * TT] for ic in range(NCH)]
        xbfl = [t["xbf"][:, ic * TT:(ic + 1) * TT] for ic in range(NCH)]
        p.w("S", dx[b], st["x_val"][j])
        p.w("V", dx[b], st["x_val"][j])
        p.w("T", "dbf", st["xbf_val"][j])
        p.w("S", "pe", st["sq_last_pe"])
        sq_ck = []
        for ic in range(NCH):
            p.act(t["sq"][:, ic * TT:(ic + 1) * TT], xf[ic], AF.Square)
            sq_ck.append(("se", p.c["se"]))
        xal = [t["xa"][:, ic * TT:(ic + 1) * TT] for ic in range(NCH)]
        sn1 = ln_fm(p, st, bank_wait, bank_consumed, xf, xbfl, xal, (6, 7),
                    sq_ck=sq_ck, warm_bank=0,
                    save_to=(t["r_save"][:, tile * TT:(tile + 1) * TT],
                             t["mr_save"][:, tile * TT:(tile + 1) * TT]))
        ve_xa = p.c["ve"]
        st["xbf_free"] = [("pe", p.c["pe"])]
        if j + 1 < seq_tiles:
            load_xbf(j + 1)

        p.w("T", "dwA", st["wA_val"])
        weave(p, bank_wait, 0, sn1["apply_ck"], per=4)
        p.w("T", "ve", ve_xa)
        p.w("T", "gp", sn1["gp_end"])
        kql = [t["kq"][:, oc * TT:(oc + 1) * TT] for oc in range(NCH)]
        for oc in range(NCH):
            bA, bB = (0, 1) if oc % 2 == 0 else (2, 3)
            bank_wait(bA)
            bank_wait(bB)
            for ic in range(NCH):
                p.mm2(ps[bA][:, :], ps[bB][:, :],
                      t["WA"][:, ic * D + oc * 128: ic * D + (oc + 1) * 128],
                      xal[ic], start=(ic == 0), stop=(ic == NCH - 1),
                      sem="pe" if ic == NCH - 1 else None)
            pe_k = p.c["pe"]
            p.w("S", "pe", pe_k)
            p.w("S", "ve", st["l3_ve"])
            p.w("S", "gp", st["l3_gp"])
            p.act(t["ltmp3"][:, :], ps[bA][:, :], AF.Identity)
            bank_consumed(bA, "se")
            se_i = p.c["se"]
            p.w("V", "se", se_i)
            p.tt(kql[oc], t["ltmp3"][:, :], ps[bB][:, :], ALU.add)
            st["l3_ve"] = p.c["ve"]
            bank_consumed(bB)
        se_k = p.c["se"]
        ve_k = p.c["ve"]
        p.w("T", "ve", ve_k)

        p.w("T", "dwB", st["wB_val"])
        for sub in range(4):
            pairs = ((0, 1), (2, 3)) if sub % 2 == 0 else ((4, 5), (6, 7))
            for vh in range(2):
                bA, bB = pairs[vh]
                bank_wait(bA)
                bank_wait(bB)
                for ic in range(NCH):
                    p.mm2(ps[bA][:, :], ps[bB][:, :],
                          xal[ic][:, sub * 128:(sub + 1) * 128],
                          t["WB"][:, ic * D + vh * 512: ic * D + (vh + 1) * 512],
                          start=(ic == 0), stop=(ic == NCH - 1),
                          sem="pe" if ic == NCH - 1 else None)
                pe_v = p.c["pe"]
                p.w("S", "pe", pe_v)
                p.w("S", "ve", st["l3_ve"])
                p.w("S", "gp", st["l3_gp"])
                p.act(t["ltmp3"][:, :], ps[bA][:, :], AF.Identity)
                bank_consumed(bA, "se")
                se_i = p.c["se"]
                p.w("V", "se", se_i)
                dstv = vtm_r[:, sub * 8 + vh * 4: sub * 8 + (vh + 1) * 4, 0:128]
                p.tt(dstv, t["ltmp3"][:, :].rearrange("p (c e) -> p c e", e=128),
                     ps[bB][:, :].rearrange("p (c e) -> p c e", e=128), ALU.add)
                st["l3_ve"] = p.c["ve"]
                bank_consumed(bB)

        kf_se = [0] * NCH
        for c in range(NCH):
            bk = 6 + c % 2
            bank_wait(bk)
            p.w("T", "se", se_k)
            for sub in range(4):
                p.mm(ps[bk][:, sub * 128:(sub + 1) * 128],
                     kql[c][:, sub * 128:(sub + 1) * 128],
                     t["wfbd"][:, :], start=True, stop=False)
                p.mm(ps[bk][:, sub * 128:(sub + 1) * 128],
                     t["onesrow"][:, :], t["bfrow"][:, :], start=False, stop=True,
                     sem="pe" if sub == 3 else None)
            pe_kf = p.c["pe"]
            p.w("S", "pe", pe_kf)
            kf_dst = t["kftm"][:, :].rearrange("p (s e) -> p s e", e=128)[:, c:c + 25:8, :]
            p.act(kf_dst, ps[bk][:, :].rearrange("p (s e) -> p s e", e=128), AF.Relu)
            kf_se[c] = p.c["se"]
            bank_consumed(bk, "se")

        for c in range(NCH):
            bk = 2 + c % 2
            bank_wait(bk)
            p.w("T", "se", kf_se[c])
            for sub in range(4):
                p.mm(ps[bk][:, 0:129],
                     t["kftm"][:, (sub * 8 + c) * 128:(sub * 8 + c) * 128 + 128],
                     t["vtm"][:, (sub * 8 + c) * 132:(sub * 8 + c) * 132 + 129],
                     start=(sub == 0), stop=(sub == 3),
                     sem="pe" if sub == 3 else None)
            pe_kv = p.c["pe"]
            p.w("V", "pe", pe_kv)
            p.tt(t["kv"][:, c * 132: c * 132 + 129], t["kv"][:, c * 132: c * 132 + 129],
                 ps[bk][:, 0:129], ALU.add)
            bank_consumed(bk)
        st["xbuf_free"][b] = [("ve", p.c["ve"]), ("se", p.c["se"]),
                              ("gp", p.c["gp"])]
        load_entry(j + 2)

    p.vmemset(t["BD"][:, :], 0.0)
    p.vmemset(t["ksc"][:, :], 0.0)
    for c in range(NCH):
        p.vcopy(t["BD"][0:64, c * 128: c * 128 + 64],
                t["kv"][0:64, c * 132: c * 132 + 64])
        p.vcopy(t["BD"][64:128, c * 128 + 64: c * 128 + 128],
                t["kv"][64:128, c * 132 + 64: c * 132 + 128])
        p.vcopy(t["ksc"][0:64, c * 2: c * 2 + 1],
                t["kv"][0:64, c * 132 + 128: c * 132 + 129])
        p.vcopy(t["ksc"][64:128, c * 2 + 1: c * 2 + 2],
                t["kv"][64:128, c * 132 + 128: c * 132 + 129])
    ve_bd = p.c["ve"]

    p.w("G", "pe", p.c["pe"])
    p.dma("G", t["WA"][:, :].rearrange("p (i o) -> p i o", o=D),
          dr["wq"].ap().rearrange("(i p) o -> p i o", p=128), "dwA")
    st["wA_val"] = p.c["dwA"]
    p.dma("G", t["WB"][:, :].rearrange("p (i o) -> p i o", o=D),
          dr["wo"].ap().rearrange("(i p) o -> p i o", p=128), "dwB")
    st["wB_val"] = p.c["dwB"]

    for tile in range(seq_tiles):
        j = seq_tiles + tile
        b = j % 2
        xb = xbufs[b]
        xf = [xb[:, ic * TT:(ic + 1) * TT] for ic in range(NCH)]
        xbfl = [t["xbf"][:, ic * TT:(ic + 1) * TT] for ic in range(NCH)]
        p.w("S", dx[b], st["x_val"][j])
        p.w("V", dx[b], st["x_val"][j])
        xal = [t["xa"][:, ic * TT:(ic + 1) * TT] for ic in range(NCH)]
        bm, bs = 6, 7
        bank_wait(bm)
        bank_wait(bs)
        p.mm(ps[bm][:, :], t["onesrow"][:, :],
             t["r_save"][:, tile * TT:(tile + 1) * TT], start=True, stop=True,
             sem="pe")
        p.mm(ps[bs][:, :], t["onesrow"][:, :],
             t["mr_save"][:, tile * TT:(tile + 1) * TT], start=True, stop=True,
             sem="pe")
        pe_bc1 = p.c["pe"]
        p.w("V", "pe", pe_bc1)
        apply_ck = []
        for ic in range(NCH):
            p.tt(t["ltmp"][:, :], xf[ic], ps[bm][:, :], ALU.mult)
            p.tt(xal[ic], t["ltmp"][:, :], ps[bs][:, :], ALU.subtract)
            apply_ck.append(("ve", p.c["ve"]))
        st["ltmp_ve"] = p.c["ve"]
        bank_consumed(bm)
        bank_consumed(bs)
        ve_xa = p.c["ve"]

        p.w("T", "dwA", st["wA_val"])
        weave(p, bank_wait, 0, apply_ck, per=4)
        p.w("T", "ve", ve_xa)
        kql = [t["kq"][:, oc * TT:(oc + 1) * TT] for oc in range(NCH)]
        for oc in range(NCH):
            bA, bB = (0, 1) if oc % 2 == 0 else (2, 3)
            bank_wait(bA)
            bank_wait(bB)
            for ic in range(NCH):
                p.mm2(ps[bA][:, :], ps[bB][:, :],
                      t["WA"][:, ic * D + oc * 128: ic * D + (oc + 1) * 128],
                      xal[ic], start=(ic == 0), stop=(ic == NCH - 1),
                      sem="pe" if ic == NCH - 1 else None)
            pe_q = p.c["pe"]
            p.w("S", "pe", pe_q)
            p.w("S", "ve", st["l3_ve"])
            p.w("S", "gp", st["l3_gp"])
            p.act(t["ltmp3"][:, :], ps[bA][:, :], AF.Identity)
            bank_consumed(bA, "se")
            se_i = p.c["se"]
            p.w("V", "se", se_i)
            p.tt(kql[oc], t["ltmp3"][:, :], ps[bB][:, :], ALU.add)
            st["l3_ve"] = p.c["ve"]
            bank_consumed(bB)
        ve_q = p.c["ve"]
        p.w("T", "ve", ve_q)

        qfl = [t["qf"][:, c * TT:(c + 1) * TT] for c in range(NCH)]
        qf_se = [0] * NCH
        for c in range(NCH):
            bk = 4 + c % 2
            bank_wait(bk)
            p.mm(ps[bk][:, :], t["wfbd"][:, :], kql[c], start=True, stop=True, sem="pe")
            pe_qf = p.c["pe"]
            p.w("S", "pe", pe_qf)
            p.act(qfl[c], ps[bk][:, :], AF.Relu, bias=t["bf2"][:, 0:1])
            qf_se[c] = p.c["se"]
            bank_consumed(bk, "se")

        if tile == 0:
            p.w("T", "ve", ve_bd)
        rcp2 = [t["rcp_bf"], t["rcp_bfb"]]
        rcp_done = [0] * NCH
        bcast_pe = [0] * NCH
        yl = [t["y"][:, c * TT:(c + 1) * TT] for c in range(NCH)]
        for c in range(NCH):
            bk = 4 + c % 2
            bo_ = 6 + c % 2
            bank_wait(bk)
            p.w("T", "se", qf_se[c])
            p.mm(ps[bk][0:2, :], t["ksc"][:, c * 2:(c + 1) * 2], qfl[c],
                 start=True, stop=True, sem="pe")
            pe_n = p.c["pe"]
            p.w("V", "pe", pe_n)
            p.ts(t["normsb"][:, :], ps[bk][0:2, :], 1e-6, ALU.max)
            bank_consumed(bk)
            p.vrecip(t["normsb"][:, :], t["normsb"][:, :])
            if c >= 2:
                p.w("V", "pe", bcast_pe[c - 2])
            p.vcopy(rcp2[c % 2][:, :], t["normsb"][:, :])
            rcp_done[c] = p.c["ve"]
            bank_wait(bk)
            p.w("T", "ve", rcp_done[c])
            p.mm(ps[bk][:, :], t["pat2"][:, :], rcp2[c % 2][:, :],
                 start=True, stop=True, sem="pe")
            bank_wait(bo_)
            p.mm(ps[bo_][:, :], t["BD"][:, c * 128:(c + 1) * 128], qfl[c],
                 start=True, stop=True, sem="pe")
            bcast_pe[c] = p.c["pe"]
            pe_o2 = p.c["pe"]
            p.w("V", "pe", pe_o2)
            p.vcopy(t["rbc"][:, :], ps[bk][:, :])
            bank_consumed(bk)
            p.tt(yl[c], ps[bo_][:, :], t["rbc"][:, :], ALU.mult)
            bank_consumed(bo_)
        ve_y = p.c["ve"]

        p.w("T", "dwB", st["wB_val"])
        p.w("T", "ve", ve_y)
        zl = [t["z"][:, oc * TT:(oc + 1) * TT] for oc in range(NCH)]
        p.w("S", "pe", st["sq_last_pe"])
        zsq_ck = []
        for oc in range(NCH):
            bA, bB = (0, 1) if oc % 2 == 0 else (2, 3)
            bank_wait(bA)
            bank_wait(bB)
            for ic in range(NCH):
                p.mm2(ps[bA][:, :], ps[bB][:, :],
                      t["WB"][:, ic * D + oc * 128: ic * D + (oc + 1) * 128],
                      yl[ic], start=(ic == 0), stop=(ic == NCH - 1),
                      sem="pe" if ic == NCH - 1 else None)
            pe_z = p.c["pe"]
            p.w("S", "pe", pe_z)
            p.w("S", "ve", st["l3_ve"])
            p.w("S", "gp", st["l3_gp"])
            p.act(t["ltmp3"][:, :], ps[bA][:, :], AF.Identity,
                  bias=t["bo_t"][:, oc:oc + 1])
            bank_consumed(bA, "se")
            se_i = p.c["se"]
            p.w("V", "se", se_i)
            p.tt(zl[oc], t["ltmp3"][:, :], ps[bB][:, :], ALU.add)
            st["l3_ve"] = p.c["ve"]
            bank_consumed(bB)
            ve_z = p.c["ve"]
            p.w("S", "ve", ve_z)
            p.act(t["sq"][:, oc * TT:(oc + 1) * TT], zl[oc], AF.Square)
            zsq_ck.append(("se", p.c["se"]))

        p.w("V", "dsp", p.c["dsp"])
        xpl = [t["xp"][:, ic * TT:(ic + 1) * TT] for ic in range(NCH)]
        sn2 = ln_fm(p, st, bank_wait, bank_consumed, None, zl, xpl, (4, 5), x_add=xf,
                    src_ve=p.c["ve"], sq_ck=zsq_ck, warm_bank=0)
        ve_xp = p.c["ve"]
        gp_xp = sn2["gp_end"]
        st["xbuf_free"][b] = [("ve", ve_xp), ("se", p.c["se"]), ("gp", gp_xp)]
        load_entry(j + 2)

        p.w("Y", "ve", ve_xp)
        p.w("Y", "gp", gp_xp)
        p.dma("Y", dr["xp_hbm"][s, tile], t["xp"][:, :], "dsp")


    _barrier(p, engines="TVS", dma=("dsp",))
    for w in range(nwin):
        emit_ffn_window(p, dr, st, s, w, seq_tiles, win_tiles, bank_wait,
                        bank_consumed, out_dmas)


def ffn_slice_dma(p, st, dr, which, idx):
    k = st["ffn_dma"]
    buf = p.t[("wc0", "wc1", "wc2")[k % 3]]
    sem = ("dw0", "dw1", "dw2")[k % 3]
    if k >= 3:
        done = st["wslice_done"].get(k - 3)
        if done is not None:
            p.w("G", "pe", done)
    p.dma("G", buf[:, :], dr[which][idx], sem)
    st["ffn_dma"] += 1


def ffn_slice_use(p, st):
    k = st["ffn_slice"]
    buf = p.t[("wc0", "wc1", "wc2")[k % 3]]
    sem = ("dw0", "dw1", "dw2")[k % 3]
    p.w("T", sem, 16 * (k // 3 + 1))
    st["ffn_slice"] += 1
    return buf, k


def emit_ffn_window(p, dr, st, s, w, seq_tiles, win_tiles, bank_wait, bank_consumed,
                    out_dmas):
    t = p.t
    ps = t["ps"]
    seq = seq_tiles * TT
    W = win_tiles * TT
    NH = win_tiles

    al = lambda oc, h: t["a"][:, oc * W + h * TT: oc * W + (h + 1) * TT]
    bl = lambda oc, h: t["b"][:, oc * W + h * TT: oc * W + (h + 1) * TT]
    xn_rhs = lambda ic, h: t["xnw"][:, h * 4096 + ic * TT: h * 4096 + (ic + 1) * TT]

    _barrier(p, engines="VS", dma=("dout", "dxr"))
    p.w("Y", "pe", p.c["pe"])
    p.w("Y", "ve", p.c["ve"])
    p.w("Y", "se", p.c["se"])
    p.w("Y", "dsp", p.c["dsp"])
    for ti in range(win_tiles):
        p.dma("Y", t["xpr"][:, ti * 4096:(ti + 1) * 4096],
              dr["xp_hbm"][s, w * win_tiles + ti], "dxr")
    dxr0 = p.c["dxr"]

    for ti in range(win_tiles):
        xpl3 = [t["xpr"][:, ti * 4096 + ic * TT: ti * 4096 + (ic + 1) * TT]
                for ic in range(NCH)]
        p.w("S", "dxr", dxr0)
        p.w("S", "pe", st["sq_last_pe"])
        sq3_ck = []
        x1bf = [t["a"][:, ic * TT:(ic + 1) * TT] for ic in range(NCH)]
        for ic in range(NCH):
            p.act(x1bf[ic], xpl3[ic], AF.Copy)
            p.act(t["sq"][:, ic * TT:(ic + 1) * TT], xpl3[ic], AF.Square)
            sq3_ck.append(("se", p.c["se"]))
        xnl = [t["xnw"][:, ti * 4096 + ic * TT: ti * 4096 + (ic + 1) * TT]
               for ic in range(NCH)]
        ln_fm(p, st, bank_wait, bank_consumed, xpl3, x1bf, xnl, (2, 3),
              sq_ck=sq3_ck, warm_bank=0)
    ve_xnw = p.c["ve"]
    p.w("T", "ve", ve_xnw)

    for slab in range(NCH):
        ffn_slice_dma(p, st, dr, "w1_t", slab)
        ffn_slice_dma(p, st, dr, "w2_t", slab)
        b1, k1 = ffn_slice_use(p, st)
        b2, k2 = ffn_slice_use(p, st)
        for ocl in range(4):
            oc = slab * 4 + ocl
            for h in range(NH):
                w1A, w1B = (0 + 2 * h, 1 + 2 * h)
                w2A, w2B = (4 + 2 * h, 5 + 2 * h)
                bank_wait(w1A)
                bank_wait(w1B)
                for ic in range(NCH):
                    p.mm2(ps[w1A][:, :], ps[w1B][:, :],
                          b1[:, ic * 512 + ocl * 128: ic * 512 + ocl * 128 + 128],
                          xn_rhs(ic, h), start=(ic == 0), stop=(ic == NCH - 1),
                          sem="pe" if ic == NCH - 1 else None)
                pe_w1 = p.c["pe"]
                bank_wait(w2A)
                bank_wait(w2B)
                for ic in range(NCH):
                    p.mm2(ps[w2A][:, :], ps[w2B][:, :],
                          b2[:, ic * 512 + ocl * 128: ic * 512 + ocl * 128 + 128],
                          xn_rhs(ic, h), start=(ic == 0), stop=(ic == NCH - 1),
                          sem="pe" if ic == NCH - 1 else None)
                pe_w2 = p.c["pe"]
                lt0, lt2, lt3 = (("ltmp", "ltmp2", "ltmp3") if h == 0 else
                                 ("ltmpB", "ltmp2B", "ltmp3B"))
                lkey = "ltmp_ve" if h == 0 else "ltmp_veB"
                p.w("S", "pe", pe_w1)
                p.w("S", "ve", st[lkey])
                p.act(t[lt2][:, :], ps[w1A][:, :], AF.Identity)
                bank_consumed(w1A, "se")
                se_i1 = p.c["se"]
                p.w("V", "se", se_i1)
                p.tt(t[lt3][:, :], t[lt2][:, :], ps[w1B][:, :], ALU.add)
                bank_consumed(w1B)
                ve_s1 = p.c["ve"]
                p.w("S", "ve", ve_s1)
                p.act(t[lt0][:, :], t[lt3][:, :], AF.Silu,
                      scale=t["s1"][:, oc:oc + 1])
                se_t1 = p.c["se"]
                p.w("S", "pe", pe_w2)
                p.act(t[lt2][:, :], ps[w2A][:, :], AF.Identity)
                bank_consumed(w2A, "se")
                se_i2 = p.c["se"]
                p.w("V", "pe", pe_w2)
                p.w("V", "se", se_i2)
                p.tt(t[lt3][:, :], t[lt2][:, :], ps[w2B][:, :], ALU.add)
                bank_consumed(w2B)
                p.stt(al(oc, h), t[lt3][:, :], t["s2"][:, oc:oc + 1],
                      t[lt0][:, :], ALU.mult, ALU.mult)
                st[lkey] = p.c["ve"]
        st["wslice_done"][k1] = p.c["pe"]
        st["wslice_done"][k2] = p.c["pe"]

    ve_a = p.c["ve"]
    for wt, sc_t, srcf, dstf in (("h1_t", "sh1", al, bl), ("h2_t", "sh2", bl, al)):
        if wt == "h1_t":
            p.w("T", "ve", ve_a)
        else:
            p.w("T", "se", p.c["se"])
        for oc in range(NHC):
            ffn_slice_dma(p, st, dr, wt, oc)
            wb, k = ffn_slice_use(p, st)
            for h in range(NH):
                bA = 4 * (oc % 2) + 2 * h
                bB = bA + 1
                bank_wait(bA)
                bank_wait(bB)
                for ic in range(NHC):
                    p.mm2(ps[bA][:, :], ps[bB][:, :], wb[:, ic * 128:(ic + 1) * 128],
                          srcf(ic, h), start=(ic == 0), stop=(ic == NHC - 1),
                          sem="pe" if ic == NHC - 1 else None)
                pe_h = p.c["pe"]
                lt0, lt2 = ("ltmp", "ltmp2") if h == 0 else ("ltmpB", "ltmp2B")
                lkey = "ltmp_ve" if h == 0 else "ltmp_veB"
                p.w("S", "pe", pe_h)
                p.w("S", "ve", st[lkey])
                p.act(t[lt0][:, :], ps[bA][:, :], AF.Identity)
                bank_consumed(bA, "se")
                se_i = p.c["se"]
                p.w("V", "se", se_i)
                p.tt(t[lt2][:, :], t[lt0][:, :], ps[bB][:, :], ALU.add)
                bank_consumed(bB)
                st[lkey] = p.c["ve"]
                p.w("S", "ve", st[lkey])
                p.act(dstf(oc, h), t[lt2][:, :], AF.Gelu,
                      scale=t[sc_t][:, oc:oc + 1])
            st["wslice_done"][k] = p.c["pe"]

    p.w("Y", "pe", p.c["pe"])
    p.w("Y", "dsp", p.c["dsp"])
    for ti in range(win_tiles):
        p.dma("Y", t["xpr"][:, ti * 4096:(ti + 1) * 4096],
              dr["xp_hbm"][s, w * win_tiles + ti], "dxr")
    dxr_val = p.c["dxr"]
    fl = lambda oc, h: t["f"][:, oc * W + h * TT: oc * W + (h + 1) * TT]
    p.w("T", "se", p.c["se"])
    for oc in range(NCH):
        ffn_slice_dma(p, st, dr, "w3_t", oc)
        wb, k = ffn_slice_use(p, st)
        for h in range(NH):
            bA = 4 * (oc % 2) + 2 * h
            bB = bA + 1
            bank_wait(bA)
            bank_wait(bB)
            for ic in range(NHC):
                p.mm2(ps[bA][:, :], ps[bB][:, :], wb[:, ic * 128:(ic + 1) * 128],
                      al(ic, h), start=(ic == 0), stop=(ic == NHC - 1),
                      sem="pe" if ic == NHC - 1 else None)
            pe_f = p.c["pe"]
            lt0 = "ltmp" if h == 0 else "ltmpB"
            lkey = "ltmp_ve" if h == 0 else "ltmp_veB"
            p.w("S", "pe", pe_f)
            p.w("S", "ve", st[lkey])
            p.act(t[lt0][:, :], ps[bA][:, :], AF.Identity,
                  scale=t["s3"][:, oc:oc + 1])
            bank_consumed(bA, "se")
            se_i = p.c["se"]
            p.w("V", "se", se_i)
            p.stt(fl(oc, h), ps[bB][:, :], t["s3"][:, oc:oc + 1], t[lt0][:, :],
                  ALU.mult, ALU.add)
            bank_consumed(bB)
            st[lkey] = p.c["ve"]
        st["wslice_done"][k] = p.c["pe"]

    p.w("V", "dxr", dxr_val)
    for h in range(NH):
        fbf = [fl(ic, h) for ic in range(NCH)]
        p.w("S", "pe", st["sq_last_pe"])
        p.w("S", "ve", st["ltmp_ve"])
        p.w("S", "ve", st["ltmp_veB"])
        sq4_ck = []
        for ic in range(NCH):
            p.act(t["sq"][:, ic * TT:(ic + 1) * TT], fbf[ic], AF.Square)
            sq4_ck.append(("se", p.c["se"]))
        xprl = [t["xpr"][:, h * 4096 + ic * TT: h * 4096 + (ic + 1) * TT]
                for ic in range(NCH)]
        outl = [t["outt"][:, ic * TT:(ic + 1) * TT] for ic in range(NCH)]
        p.w("V", "dout", p.c["dout"])
        ln_fm(p, st, bank_wait, bank_consumed, None, fbf, outl, (0, 1), x_add=xprl,
              src_ve=p.c["ve"], sq_ck=sq4_ck, warm_bank=4)
        ve_out = p.c["ve"]
        p.w("Y", "ve", ve_out)
        p.w("Y", "gp", p.c["gp"])
        tglob = w * win_tiles + h
        p.dma("Y",
              dr["out"][s][:, :, tglob * TT:(tglob + 1) * TT].rearrange("i p tt -> p i tt"),
              t["outt"][:, :].rearrange("p (i tt) -> p i tt", tt=TT), "dout")
        out_dmas[0] += 1



def _prep_inputs(L0_attn_x, L0_attn_Wq, L0_attn_Wk, L0_attn_Wv, L0_attn_Wo,
                 L0_attn_bo, L0_attn_Wf, L0_attn_bf,
                 L0_ffn_w1_w, L0_ffn_w1_s, L0_ffn_w2_w, L0_ffn_w2_s,
                 L0_ffn_h1_w, L0_ffn_h1_s, L0_ffn_h2_w, L0_ffn_h2_s,
                 L0_ffn_w3_w, L0_ffn_w3_s):
    f32 = np.float32
    asc = np.ascontiguousarray

    def tile_oi(wT, n_in_ch, n_out_ch, ocp):
        r = wT.reshape(n_in_ch, 128, n_out_ch, ocp).transpose(2, 1, 0, 3)
        return asc(r.reshape(n_out_ch, 128, n_in_ch * ocp).astype(f32))

    sh = {}
    for n, wmat in (("wq", L0_attn_Wq), ("wk", L0_attn_Wk), ("wv", L0_attn_Wv),
                    ("wo", L0_attn_Wo)):
        sh[n] = asc(np.asarray(wmat, dtype=f32))
    wf = np.asarray(L0_attn_Wf, dtype=f32)
    wfbd = np.zeros((128, 128), f32)
    wfbd[:64, :64] = wf
    wfbd[64:, 64:] = wf
    sh["wfbd"] = wfbd
    bfv = np.asarray(L0_attn_bf, dtype=f32)
    sh["bfbc"] = asc(np.broadcast_to(np.concatenate([bfv, bfv])[None, :],
                                     (128, 128)).astype(f32))
    sh["bf2"] = asc(np.concatenate([bfv, bfv])[:, None].astype(f32))
    sh["bo_t"] = asc(np.asarray(L0_attn_bo, dtype=f32).reshape(NCH, 128).T)
    w1T = np.asarray(L0_ffn_w1_w, dtype=f32).T
    w2T = np.asarray(L0_ffn_w2_w, dtype=f32).T
    h1T = np.asarray(L0_ffn_h1_w, dtype=f32).T
    h2T = np.asarray(L0_ffn_h2_w, dtype=f32).T
    w3T = np.asarray(L0_ffn_w3_w, dtype=f32).T
    sh["w1_t"] = tile_oi(w1T, 8, 8, 512)
    sh["w2_t"] = tile_oi(w2T, 8, 8, 512)
    sh["h1_t"] = tile_oi(h1T, 32, 32, 128)
    sh["h2_t"] = tile_oi(h2T, 32, 32, 128)
    sh["w3_t"] = tile_oi(w3T, 32, 8, 128)
    for n, v in (("s1_t", L0_ffn_w1_s), ("s2_t", L0_ffn_w2_s),
                 ("sh1_t", L0_ffn_h1_s), ("sh2_t", L0_ffn_h2_s)):
        sh[n] = asc(np.asarray(v, dtype=f32).reshape(-1, 128).T)
    sh["s3_t"] = asc(np.asarray(L0_ffn_w3_s, dtype=f32).reshape(NCH, 128).T)
    pat = np.zeros((2, 128), f32)
    pat[0, :64] = 1.0
    pat[1, 64:] = 1.0
    sh["pat2"] = pat
    sh["bfrow"] = asc(np.concatenate([bfv, bfv])[None, :].astype(f32))
    sh["warm512"] = np.ones((128, TT), f32)
    sh["onestat"] = np.full((128, 1), 1.0 / D, f32)
    sh["onesrow"] = np.ones((1, 128), f32)

    x = np.asarray(L0_attn_x, dtype=f32)
    percore = []
    for c in range(8):
        xc = x[c * 2:(c + 1) * 2]
        xt = xc.transpose(0, 2, 1).reshape(2, NCH, 128, xc.shape[1])
        percore.append(asc(xt.astype(f32)))
    return sh, percore


_NC_CACHE = {}


def kernel(**inputs):
    sh, percore = _prep_inputs(**inputs)
    if "full" not in _NC_CACHE:
        _NC_CACHE["full"] = build_program()
    nc = _NC_CACHE["full"]
    in_maps = []
    for c in range(8):
        m = dict(sh)
        m["x"] = percore[c]
        in_maps.append(m)
    res = run_bass_kernel_spmd(nc, in_maps, core_ids=list(range(8)))
    outs = []
    for c in range(8):
        o = res.results[c]["out"]
        o = o.reshape(2, D, -1).transpose(0, 2, 1)
        outs.append(o)
    return np.ascontiguousarray(np.concatenate(outs, axis=0)).astype(np.float32)


# version 11
# speedup vs baseline: 1.0797x; 1.0014x over previous
import contextlib
import os as _os
import numpy as np
import concourse.bass as bass
import concourse.mybir as mybir
from concourse.bass_utils import run_bass_kernel_spmd

F32 = mybir.dt.float32
BF16 = mybir.dt.bfloat16
U8 = mybir.dt.uint8
AF = mybir.ActivationFunctionType
ALU = mybir.AluOpType

D = 1024
NCH = 8
HID = 4096
NHC = 32
TT = 512
NSEQ = 2
SEQ_TILES = 4
WIN_TILES = 2
EPS = 1e-5

KB = 1024


class P:

    def __init__(self, nc, ename, eng, t, sems):
        self.nc = nc
        self.E = ename
        self.h = eng
        self.t = t
        self.S = sems
        self.c = {k: 0 for k in sems}

    def bump(self, sem, n):
        self.c[sem] += n
        return self.c[sem]

    def _inc(self, ins, sem, n):
        if sem is None:
            return
        self.bump(sem, n)
        if ins is not None:
            ins.then_inc(self.S[sem], n)

    def w(self, eng, sem, val):
        if val <= 0:
            return
        if self.E == eng:
            self.h.wait_ge(self.S[sem], val)

    def mm(self, out, lhsT, rhs, start, stop, sem=None):
        ins = None
        if self.E == "T":
            ins = self.h.matmul(out, lhsT, rhs, start=start, stop=stop)
        self._inc(ins, sem, 1)

    def gtt(self, out, in0, in1, op, sem="gp"):
        ins = None
        if self.E == "G":
            if self.c["gp"] > 0:
                self.h.wait_ge(self.S["gp"], self.c["gp"])
            ins = self.h.tensor_tensor(out, in0, in1, op)
        self._inc(ins, sem, 1)

    def gw(self, sem, val):
        self.w("G", sem, val)

    def mmwarm(self, out, lhsT, rhs):
        if self.E == "T":
            self.h.matmul(out, lhsT, rhs, start=True, stop=True)

    def mm2(self, outA, outB, lhsT, rhs, start, stop, sem=None):
        ins = None
        if self.E == "T":
            self.h.matmul(outA, lhsT[0:64, :], rhs[0:64, :], start=start,
                          stop=stop, tile_position=(0, 0))
            ins = self.h.matmul(outB, lhsT[64:128, :], rhs[64:128, :], start=start,
                                stop=stop, tile_position=(64, 0))
        self._inc(ins, sem, 1)

    def act(self, out, in_, func, bias=0.0, scale=1.0, sem="se"):
        ins = None
        if self.E == "S":
            if self.c["se"] > 0:
                self.h.wait_ge(self.S["se"], self.c["se"])
            ins = self.h.activation(out, in_, func, bias=bias, scale=scale)
        self._inc(ins, sem, 1)

    def _vsync(self):
        if self.E == "V" and self.c["ve"] > 0:
            self.h.wait_ge(self.S["ve"], self.c["ve"])

    def tt(self, out, in0, in1, op, sem="ve"):
        ins = None
        if self.E == "V":
            self._vsync()
            ins = self.h.tensor_tensor(out, in0, in1, op)
        self._inc(ins, sem, 1)

    def ts(self, out, in0, s1, op, sem="ve"):
        ins = None
        if self.E == "V":
            self._vsync()
            ins = self.h.tensor_single_scalar(out, in0, s1, op)
        self._inc(ins, sem, 1)

    def stt(self, out, in0, scal, in1, op0, op1, sem="ve"):
        ins = None
        if self.E == "V":
            self._vsync()
            ins = self.h.scalar_tensor_tensor(out, in0, scal, in1, op0, op1)
        self._inc(ins, sem, 1)

    def vcopy(self, out, in_, sem="ve"):
        ins = None
        if self.E == "V":
            self._vsync()
            ins = self.h.tensor_copy(out, in_)
        self._inc(ins, sem, 1)

    def vrecip(self, out, in_, sem="ve"):
        ins = None
        if self.E == "V":
            self._vsync()
            ins = self.h.reciprocal(out, in_)
        self._inc(ins, sem, 1)

    def vmemset(self, ap, val, sem="ve"):
        ins = None
        if self.E == "V":
            self._vsync()
            ins = self.h.memset(ap, val)
        self._inc(ins, sem, 1)

    def dma(self, q, out, in_, sem):
        ins = None
        if self.E == q:
            ins = self.h.dma_start(out=out, in_=in_)
        self._inc(ins, sem, 16)


def _alloc_views(nc, arena, layout):
    base = nc.lookup_mloc(arena).addr
    out = {}
    for name, shape, dtype, off in layout:
        out[name] = nc.alloc_sbuf_tensor_at(name, shape, dtype, offset=base + off)
    return out


def build_program(nseq=NSEQ, seq_tiles=SEQ_TILES, win_tiles=WIN_TILES):
    nc = bass.Bass("TRN2", target_bir_lowering=False)
    seq = seq_tiles * TT
    W = win_tiles * TT

    dr = {}
    dr["x"] = nc.dram_tensor("x", [nseq, NCH, 128, seq], F32, kind="ExternalInput")
    dr["out"] = nc.dram_tensor("out", [nseq, NCH, 128, seq], F32, kind="ExternalOutput")
    for n in ("wq", "wk", "wv", "wo"):
        dr[n] = nc.dram_tensor(n, [D, D], F32, kind="ExternalInput")
    for n, shp in (("wfbd", [128, 128]), ("bfbc", [128, 128]), ("bo_t", [128, NCH]),
                   ("bf2", [128, 1]),
                   ("w1_t", [NCH, 128, 4096]), ("w2_t", [NCH, 128, 4096]),
                   ("h1_t", [NHC, 128, 4096]), ("h2_t", [NHC, 128, 4096]),
                   ("w3_t", [NCH, 128, 4096]),
                   ("s1_t", [128, NHC]), ("s2_t", [128, NHC]),
                   ("sh1_t", [128, NHC]), ("sh2_t", [128, NHC]), ("s3_t", [128, NCH]),
                   ("pat2", [2, 128]), ("onestat", [128, 1]), ("onesrow", [1, 128]),
                   ("bfrow", [1, 128]), ("warm512", [128, TT])):
        dr[n] = nc.dram_tensor(n, shp, F32, kind="ExternalInput")
    dr["xp_hbm"] = nc.dram_tensor("xp_hbm", [nseq, seq_tiles, 128, 4096], F32,
                                  kind="Internal")

    ARENA = 206 * KB
    arena = nc.alloc_sbuf_tensor("arena", [128, ARENA], U8)
    SC = 3584
    B0 = SC + 49 * KB
    lay = [
        ("wfbd", [128, 128], BF16, 0),
        ("bfbc", [128, 128], BF16, 256),
        ("bo_t", [128, NCH], F32, 512),
        ("bf2", [128, 1], F32, 544),
        ("s1", [128, NHC], F32, 576),
        ("s2", [128, NHC], F32, 704),
        ("sh1", [128, NHC], F32, 832),
        ("sh2", [128, NHC], F32, 960),
        ("s3", [128, NCH], F32, 1088),
        ("onestat", [128, 1], BF16, 1120),
        ("onesrow", [1, 128], BF16, 1152),
        ("bfrow", [1, 128], BF16, 1408),
        ("warm512", [128, TT], BF16, 1920),
        ("pat2", [2, 128], BF16, 1664),
        ("ltmp", [128, TT], F32, SC),
        ("ltmp2", [128, TT], F32, SC + 2 * KB),
        ("msb", [1, TT], F32, SC + 4 * KB),
        ("vsb", [1, TT], F32, SC + 6 * KB),
        ("r_bf", [1, TT], BF16, SC + 8 * KB),
        ("mr_bf", [1, TT], BF16, SC + 9 * KB),
        ("normsb", [2, TT], F32, SC + 10 * KB),
        ("rcp_bf", [2, TT], BF16, SC + 12 * KB),
        ("rcp_bfb", [2, TT], BF16, SC + 47 * KB),
        ("r_save", [1, 4 * TT], BF16, SC + 13 * KB),
        ("mr_save", [1, 4 * TT], BF16, SC + 17 * KB),
        ("sq", [128, 4096], BF16, SC + 21 * KB),
        ("wc0", [128, 4096], BF16, SC + 29 * KB),
        ("wc1", [128, 4096], BF16, SC + 37 * KB),
        ("ltmp3", [128, TT], F32, SC + 45 * KB),
        ("ltmpB", [128, TT], F32, B0 + 138 * KB),
        ("ltmp2B", [128, TT], F32, B0 + 140 * KB),
        ("ltmp3B", [128, TT], F32, B0 + 142 * KB),
        ("wc2", [128, 4096], BF16, B0 + 144 * KB),

        ("xb0", [128, 4096], F32, B0 + 0),
        ("xb1", [128, 4096], F32, B0 + 16 * KB),
        ("xbf", [128, 4096], BF16, B0 + 32 * KB),
        ("WA", [128, NCH * D], BF16, B0 + 40 * KB),
        ("WB", [128, NCH * D], BF16, B0 + 56 * KB),
        ("xa", [128, 4096], BF16, B0 + 72 * KB),
        ("kq", [128, 4096], BF16, B0 + 80 * KB),
        ("vtm", [128, 32 * 132], BF16, B0 + 88 * KB),
        ("qf", [128, 4096], BF16, B0 + 88 * KB),
        ("kftm", [128, 4096], BF16, B0 + 97 * KB),
        ("y", [128, 4096], BF16, B0 + 97 * KB),
        ("kv", [128, NCH * 132], F32, B0 + 105 * KB),
        ("BD", [128, NCH * 128], BF16, B0 + 110 * KB),
        ("ksc", [128, NCH * 2], BF16, B0 + 112 * KB),
        ("rbc", [128, TT], BF16, B0 + 113 * KB),
        ("z", [128, 4096], BF16, B0 + 114 * KB),
        ("xp", [128, 4096], F32, B0 + 122 * KB),
        ("a", [128, NHC * W], BF16, B0 + 0),
        ("b", [128, NHC * W], BF16, B0 + NHC * W * 2),
        ("xnw", [128, win_tiles * 4096], BF16,
         B0 + NHC * W * 2 + win_tiles * 16 * KB),
        ("xpr", [128, win_tiles * 4096], F32, B0 + NHC * W * 2),
        ("f", [128, NCH * W], BF16, B0 + NHC * W * 2 + win_tiles * 16 * KB),
        ("outt", [128, 4096], F32, B0 + NHC * W * 2 + win_tiles * 16 * KB + NCH * W * 2),
    ]
    need = max(B0 + 2 * NHC * W * 2, B0 + 152 * KB)
    assert need <= ARENA, ("big region overflow", need, ARENA)
    t = _alloc_views(nc, arena, lay)
    t["arena"] = arena
    t["ps"] = [nc.alloc_psum_tensor(f"psb{i}", [128, 512], F32) for i in range(8)]

    sem_names = ["pe", "ve", "se", "gp", "dcon", "dwA", "dwB", "dbf",
                 "dxa", "dxb", "dw0", "dw1", "dw2", "dsp", "dxr", "dout"]
    stack = contextlib.ExitStack()
    sems = {n: stack.enter_context(nc.semaphore(n)) for n in sem_names}

    counters = []
    with nc.Block() as block:
        def run(ename):
            def f(eng):
                p = P(nc, ename, eng, t, sems)
                emit_all(p, dr, nseq, seq_tiles, win_tiles)
                counters.append(dict(p.c))
            return f
        block.tensor(run("T"))
        block.vector(run("V"))
        block.scalar(run("S"))
        block.gpsimd(run("G"))
        block.sync(run("Y"))
    stack.close()
    for c in counters[1:]:
        assert c == counters[0], ("counter mismatch", counters)
    return nc



def _barrier(p, engines="TVSY", dma=()):
    cur = {k: p.c[k] for k in ("pe", "ve", "se", "gp")}
    own = {"T": "pe", "V": "ve", "S": "se", "G": "gp"}
    for e in engines:
        for sem, v in cur.items():
            if own.get(e) != sem:
                p.w(e, sem, v)
        for dsem in dma:
            p.w(e, dsem, p.c[dsem])


def emit_all(p, dr, nseq, seq_tiles, win_tiles):
    t = p.t

    st = {
        "bank_free": [[] for _ in range(8)],
        "xbuf_free": [[], []],
        "xbf_free": [],
        "x_val": {},
        "xbf_val": {},
        "wslice_done": {},
        "ffn_slice": 0,
        "ffn_dma": 0,
        "sq_last_pe": 0,
        "l3_ve": 0,
        "l3_gp": 0,
        "ltmp_ve": 0,
        "ltmp_veB": 0,
        "wA_val": 0, "wB_val": 0,
    }

    def bank_wait(b):
        for semv in st["bank_free"][b]:
            p.w("T", *semv)
        st["bank_free"][b] = []

    def bank_consumed(b, sem="ve"):
        st["bank_free"][b].append((sem, p.c[sem]))

    for n, src in (("wfbd", "wfbd"), ("bfbc", "bfbc"), ("bo_t", "bo_t"), ("bf2", "bf2"),
                   ("s1", "s1_t"), ("s2", "s2_t"), ("sh1", "sh1_t"), ("sh2", "sh2_t"),
                   ("s3", "s3_t"), ("onestat", "onestat"), ("onesrow", "onesrow"),
                   ("pat2", "pat2"), ("bfrow", "bfrow"), ("warm512", "warm512")):
        p.dma("G", t[n][:, :], dr[src][:, :], "dcon")
    NCONST = p.c["dcon"]
    for e in "TVSY":
        p.w(e, "dcon", NCONST)
    p.w("G", "dcon", NCONST)

    out_dmas = [0]
    for s in range(nseq):
        emit_seq(p, dr, st, s, seq_tiles, win_tiles, bank_wait, bank_consumed, out_dmas)

    p.w("Y", "dout", p.c["dout"])


_WARM_PER = int(_os.environ.get("MK_WARM_PER", "0"))


def weave(p, bank_wait, wb, checkpoints, per=3):
    if _WARM_PER >= 0:
        per = _WARM_PER
    if per == 0:
        return
    """Keep the PE clock warm while tracking a cross-engine chain: step through
    semaphore checkpoints, emitting a few dummy matmuls between each so PE-idle
    stretches stay under the HAM re-throttle window."""
    t = p.t
    bank_wait(wb)
    for sem, v in checkpoints:
        p.w("T", sem, v)
        for _ in range(per):
            p.mmwarm(t["ps"][wb][0:1, :], t["onestat"][:, 0:1], t["warm512"][:, :])


def ln_fm(p, st, bank_wait, bank_consumed, src_f32, src_bf, dst, stat_banks,
          x_add=None, src_ve=0, sq_ck=None, warm_bank=None, save_to=None):
    t = p.t
    ps = t["ps"]
    bm, bs = stat_banks
    se_sq = p.c["se"]
    bank_wait(bm)
    bank_wait(bs)
    if warm_bank is not None and sq_ck:
        weave(p, bank_wait, warm_bank, sq_ck, per=3)
    p.w("T", "se", se_sq)
    p.w("T", "ve", src_ve)
    for ic in range(NCH):
        p.mm(ps[bm][0:1, :], t["onestat"][:, 0:1], src_bf[ic],
             start=(ic == 0), stop=(ic == NCH - 1),
             sem="pe" if ic == NCH - 1 else None)
    for ic in range(NCH):
        p.mm(ps[bs][0:1, :], t["onestat"][:, 0:1], t["sq"][:, ic * TT:(ic + 1) * TT],
             start=(ic == 0), stop=(ic == NCH - 1),
             sem="pe" if ic == NCH - 1 else None)
    pe_stats = p.c["pe"]
    st["sq_last_pe"] = pe_stats
    p.w("V", "pe", pe_stats)
    p.vcopy(t["msb"][:, :], ps[bm][0:1, :])
    p.tt(t["vsb"][:, :], t["msb"][:, :], t["msb"][:, :], ALU.mult)
    p.stt(t["vsb"][:, :], ps[bs][0:1, :], float(EPS), t["vsb"][:, :],
          ALU.add, ALU.subtract)
    ve_v = p.c["ve"]
    p.w("S", "ve", ve_v)
    p.act(t["vsb"][:, :], t["vsb"][:, :], AF.Ln)
    p.act(t["vsb"][:, :], t["vsb"][:, :], AF.Exp, scale=-0.5)
    se_sqrt = p.c["se"]
    p.w("V", "se", se_sqrt)
    p.vcopy(t["r_bf"][:, :], t["vsb"][:, :])
    p.tt(t["msb"][:, :], t["msb"][:, :], t["vsb"][:, :], ALU.mult)
    p.vcopy(t["mr_bf"][:, :], t["msb"][:, :])
    if save_to is not None:
        p.vcopy(save_to[0], t["r_bf"][:, :])
        p.vcopy(save_to[1], t["mr_bf"][:, :])
    ve_r = p.c["ve"]
    if warm_bank is not None:
        ck = [("ve", ve_v), ("se", p.c["se"]), ("ve", ve_r)]
        weave(p, bank_wait, warm_bank, ck, per=4)
    p.w("T", "ve", ve_r)
    p.mm(ps[bm][:, :], t["onesrow"][:, :], t["r_bf"][:, :], start=True, stop=True,
         sem="pe")
    p.mm(ps[bs][:, :], t["onesrow"][:, :], t["mr_bf"][:, :], start=True, stop=True,
         sem="pe")
    pe_bc = p.c["pe"]
    p.w("V", "pe", pe_bc)
    apply_ck = []
    for ic in range(NCH):
        srcm = src_f32[ic] if src_f32 is not None else src_bf[ic]
        p.tt(t["ltmp"][:, :], srcm, ps[bm][:, :], ALU.mult)
        if x_add is None:
            p.tt(dst[ic], t["ltmp"][:, :], ps[bs][:, :], ALU.subtract)
        else:
            p.tt(t["ltmp2"][:, :], t["ltmp"][:, :], ps[bs][:, :], ALU.subtract)
            p.tt(dst[ic], t["ltmp2"][:, :], x_add[ic], ALU.add)
        apply_ck.append(("ve", p.c["ve"]))
    st["ltmp_ve"] = p.c["ve"]
    bank_consumed(bm)
    bank_consumed(bs)
    return {"pe_stats": pe_stats, "pe_bc": pe_bc, "ve_end": p.c["ve"],
            "gp_end": p.c["gp"], "apply_ck": apply_ck}


def emit_seq(p, dr, st, s, seq_tiles, win_tiles, bank_wait, bank_consumed, out_dmas):
    t = p.t
    ps = t["ps"]
    seq = seq_tiles * TT
    nwin = seq_tiles // win_tiles
    xbufs = [t["xb0"], t["xb1"]]
    dx = ["dxa", "dxb"]

    _barrier(p, engines="TVSY", dma=("dsp", "dout"))
    _barrier(p, engines="G")

    p.dma("G", t["WA"][:, :].rearrange("p (i o) -> p i o", o=D),
          dr["wk"].ap().rearrange("(i p) o -> p i o", p=128), "dwA")
    st["wA_val"] = p.c["dwA"]
    p.dma("G", t["WB"][:, :].rearrange("p (i o) -> p i o", o=D),
          dr["wv"].ap().rearrange("(i p) o -> p i o", p=128), "dwB")
    st["wB_val"] = p.c["dwB"]

    entries = [(1, i) for i in range(seq_tiles)] + [(2, i) for i in range(seq_tiles)]

    def load_entry(j):
        if j >= len(entries):
            return
        _, tile = entries[j]
        b = j % 2
        for semv in st["xbuf_free"][b]:
            p.w("Y", *semv)
        st["xbuf_free"][b] = []
        p.dma("Y", xbufs[b][:, :].rearrange("p (i tt) -> p i tt", tt=TT),
              dr["x"][s][:, :, tile * TT:(tile + 1) * TT].rearrange("i p tt -> p i tt"),
              dx[b])
        st["x_val"][j] = p.c[dx[b]]

    def load_xbf(j):
        if j >= len(entries):
            return
        _, tile = entries[j]
        for semv in st["xbf_free"]:
            p.w("G", *semv)
        st["xbf_free"] = []
        p.dma("G", t["xbf"][:, :].rearrange("p (i tt) -> p i tt", tt=TT),
              dr["x"][s][:, :, tile * TT:(tile + 1) * TT].rearrange("i p tt -> p i tt"),
              "dbf")
        st["xbf_val"][j] = p.c["dbf"]

    load_entry(0)
    load_entry(1)
    load_xbf(0)

    p.vmemset(t["kv"][:, :], 0.0)
    vtm_r = t["vtm"][:, :].rearrange("p (sl e) -> p sl e", e=132)
    p.vmemset(vtm_r[:, :, 128:129], 1.0)

    for tile in range(seq_tiles):
        j = tile
        b = j % 2
        xb = xbufs[b]
        xf = [xb[:, ic * TT:(ic + 1) * TT] for ic in range(NCH)]
        xbfl = [t["xbf"][:, ic * TT:(ic + 1) * TT] for ic in range(NCH)]
        p.w("S", dx[b], st["x_val"][j])
        p.w("V", dx[b], st["x_val"][j])
        p.w("T", "dbf", st["xbf_val"][j])
        p.w("S", "pe", st["sq_last_pe"])
        sq_ck = []
        for ic in range(NCH):
            p.act(t["sq"][:, ic * TT:(ic + 1) * TT], xf[ic], AF.Square)
            sq_ck.append(("se", p.c["se"]))
        xal = [t["xa"][:, ic * TT:(ic + 1) * TT] for ic in range(NCH)]
        sn1 = ln_fm(p, st, bank_wait, bank_consumed, xf, xbfl, xal, (6, 7),
                    sq_ck=sq_ck, warm_bank=0,
                    save_to=(t["r_save"][:, tile * TT:(tile + 1) * TT],
                             t["mr_save"][:, tile * TT:(tile + 1) * TT]))
        ve_xa = p.c["ve"]
        st["xbf_free"] = [("pe", p.c["pe"])]
        if j + 1 < seq_tiles:
            load_xbf(j + 1)

        p.w("T", "dwA", st["wA_val"])
        weave(p, bank_wait, 0, sn1["apply_ck"], per=4)
        p.w("T", "ve", ve_xa)
        p.w("T", "gp", sn1["gp_end"])
        kql = [t["kq"][:, oc * TT:(oc + 1) * TT] for oc in range(NCH)]
        for oc in range(NCH):
            bA, bB = (0, 1) if oc % 2 == 0 else (2, 3)
            bank_wait(bA)
            bank_wait(bB)
            for ic in range(NCH):
                p.mm2(ps[bA][:, :], ps[bB][:, :],
                      t["WA"][:, ic * D + oc * 128: ic * D + (oc + 1) * 128],
                      xal[ic], start=(ic == 0), stop=(ic == NCH - 1),
                      sem="pe" if ic == NCH - 1 else None)
            pe_k = p.c["pe"]
            p.w("S", "pe", pe_k)
            p.w("S", "ve", st["l3_ve"])
            p.w("S", "gp", st["l3_gp"])
            p.act(t["ltmp3"][:, :], ps[bA][:, :], AF.Identity)
            bank_consumed(bA, "se")
            se_i = p.c["se"]
            p.w("V", "se", se_i)
            p.tt(kql[oc], t["ltmp3"][:, :], ps[bB][:, :], ALU.add)
            st["l3_ve"] = p.c["ve"]
            bank_consumed(bB)
        se_k = p.c["se"]
        ve_k = p.c["ve"]
        p.w("T", "ve", ve_k)

        p.w("T", "dwB", st["wB_val"])
        for sub in range(4):
            pairs = ((0, 1), (2, 3)) if sub % 2 == 0 else ((4, 5), (6, 7))
            for vh in range(2):
                bA, bB = pairs[vh]
                bank_wait(bA)
                bank_wait(bB)
                for ic in range(NCH):
                    p.mm2(ps[bA][:, :], ps[bB][:, :],
                          xal[ic][:, sub * 128:(sub + 1) * 128],
                          t["WB"][:, ic * D + vh * 512: ic * D + (vh + 1) * 512],
                          start=(ic == 0), stop=(ic == NCH - 1),
                          sem="pe" if ic == NCH - 1 else None)
                pe_v = p.c["pe"]
                p.w("S", "pe", pe_v)
                p.w("S", "ve", st["l3_ve"])
                p.w("S", "gp", st["l3_gp"])
                p.act(t["ltmp3"][:, :], ps[bA][:, :], AF.Identity)
                bank_consumed(bA, "se")
                se_i = p.c["se"]
                p.w("V", "se", se_i)
                dstv = vtm_r[:, sub * 8 + vh * 4: sub * 8 + (vh + 1) * 4, 0:128]
                p.tt(dstv, t["ltmp3"][:, :].rearrange("p (c e) -> p c e", e=128),
                     ps[bB][:, :].rearrange("p (c e) -> p c e", e=128), ALU.add)
                st["l3_ve"] = p.c["ve"]
                bank_consumed(bB)

        kf_se = [0] * NCH
        for c in range(NCH):
            bk = 6 + c % 2
            bank_wait(bk)
            p.w("T", "se", se_k)
            for sub in range(4):
                p.mm(ps[bk][:, sub * 128:(sub + 1) * 128],
                     kql[c][:, sub * 128:(sub + 1) * 128],
                     t["wfbd"][:, :], start=True, stop=False)
                p.mm(ps[bk][:, sub * 128:(sub + 1) * 128],
                     t["onesrow"][:, :], t["bfrow"][:, :], start=False, stop=True,
                     sem="pe" if sub == 3 else None)
            pe_kf = p.c["pe"]
            p.w("S", "pe", pe_kf)
            kf_dst = t["kftm"][:, :].rearrange("p (s e) -> p s e", e=128)[:, c:c + 25:8, :]
            p.act(kf_dst, ps[bk][:, :].rearrange("p (s e) -> p s e", e=128), AF.Relu)
            kf_se[c] = p.c["se"]
            bank_consumed(bk, "se")

        for c in range(NCH):
            bk = 2 + c % 2
            bank_wait(bk)
            p.w("T", "se", kf_se[c])
            for sub in range(4):
                p.mm(ps[bk][:, 0:129],
                     t["kftm"][:, (sub * 8 + c) * 128:(sub * 8 + c) * 128 + 128],
                     t["vtm"][:, (sub * 8 + c) * 132:(sub * 8 + c) * 132 + 129],
                     start=(sub == 0), stop=(sub == 3),
                     sem="pe" if sub == 3 else None)
            pe_kv = p.c["pe"]
            p.w("V", "pe", pe_kv)
            p.tt(t["kv"][:, c * 132: c * 132 + 129], t["kv"][:, c * 132: c * 132 + 129],
                 ps[bk][:, 0:129], ALU.add)
            bank_consumed(bk)
        st["xbuf_free"][b] = [("ve", p.c["ve"]), ("se", p.c["se"]),
                              ("gp", p.c["gp"])]
        load_entry(j + 2)

    p.vmemset(t["BD"][:, :], 0.0)
    p.vmemset(t["ksc"][:, :], 0.0)
    for c in range(NCH):
        p.vcopy(t["BD"][0:64, c * 128: c * 128 + 64],
                t["kv"][0:64, c * 132: c * 132 + 64])
        p.vcopy(t["BD"][64:128, c * 128 + 64: c * 128 + 128],
                t["kv"][64:128, c * 132 + 64: c * 132 + 128])
        p.vcopy(t["ksc"][0:64, c * 2: c * 2 + 1],
                t["kv"][0:64, c * 132 + 128: c * 132 + 129])
        p.vcopy(t["ksc"][64:128, c * 2 + 1: c * 2 + 2],
                t["kv"][64:128, c * 132 + 128: c * 132 + 129])
    ve_bd = p.c["ve"]

    p.w("G", "pe", p.c["pe"])
    p.dma("G", t["WA"][:, :].rearrange("p (i o) -> p i o", o=D),
          dr["wq"].ap().rearrange("(i p) o -> p i o", p=128), "dwA")
    st["wA_val"] = p.c["dwA"]
    p.dma("G", t["WB"][:, :].rearrange("p (i o) -> p i o", o=D),
          dr["wo"].ap().rearrange("(i p) o -> p i o", p=128), "dwB")
    st["wB_val"] = p.c["dwB"]


    def p2_partA(tile):
        j = seq_tiles + tile
        b = j % 2
        xb = xbufs[b]
        xf = [xb[:, ic * TT:(ic + 1) * TT] for ic in range(NCH)]
        p.w("S", dx[b], st["x_val"][j])
        p.w("V", dx[b], st["x_val"][j])
        xal = [t["xa"][:, ic * TT:(ic + 1) * TT] for ic in range(NCH)]
        bm, bs = 6, 7
        bank_wait(bm)
        bank_wait(bs)
        p.mm(ps[bm][:, :], t["onesrow"][:, :],
             t["r_save"][:, tile * TT:(tile + 1) * TT], start=True, stop=True,
             sem="pe")
        p.mm(ps[bs][:, :], t["onesrow"][:, :],
             t["mr_save"][:, tile * TT:(tile + 1) * TT], start=True, stop=True,
             sem="pe")
        pe_bc1 = p.c["pe"]
        p.w("V", "pe", pe_bc1)
        apply_ck = []
        for ic in range(NCH):
            p.tt(t["ltmp"][:, :], xf[ic], ps[bm][:, :], ALU.mult)
            p.tt(xal[ic], t["ltmp"][:, :], ps[bs][:, :], ALU.subtract)
            apply_ck.append(("ve", p.c["ve"]))
        st["ltmp_ve"] = p.c["ve"]
        bank_consumed(bm)
        bank_consumed(bs)
        ve_xa = p.c["ve"]

        p.w("T", "dwA", st["wA_val"])
        weave(p, bank_wait, 0, apply_ck, per=4)
        p.w("T", "ve", ve_xa)
        kql = [t["kq"][:, oc * TT:(oc + 1) * TT] for oc in range(NCH)]
        for oc in range(NCH):
            bA, bB = (0, 1) if oc % 2 == 0 else (2, 3)
            bank_wait(bA)
            bank_wait(bB)
            for ic in range(NCH):
                p.mm2(ps[bA][:, :], ps[bB][:, :],
                      t["WA"][:, ic * D + oc * 128: ic * D + (oc + 1) * 128],
                      xal[ic], start=(ic == 0), stop=(ic == NCH - 1),
                      sem="pe" if ic == NCH - 1 else None)
            pe_q = p.c["pe"]
            p.w("S", "pe", pe_q)
            p.w("S", "ve", st["l3_ve"])
            p.act(t["ltmp3"][:, :], ps[bA][:, :], AF.Identity)
            bank_consumed(bA, "se")
            se_i = p.c["se"]
            p.w("V", "se", se_i)
            p.tt(kql[oc], t["ltmp3"][:, :], ps[bB][:, :], ALU.add)
            st["l3_ve"] = p.c["ve"]
            bank_consumed(bB)
        return {"xf": xf, "b": b, "j": j, "tile": tile, "ve_q": p.c["ve"],
                "kql": kql}

    def p2_partB(pa):
        tile = pa["tile"]
        j = pa["j"]
        b = pa["b"]
        xf = pa["xf"]
        kql = pa["kql"]
        p.w("T", "ve", pa["ve_q"])

        qfl = [t["qf"][:, c * TT:(c + 1) * TT] for c in range(NCH)]
        qf_se = [0] * NCH
        for c in range(NCH):
            bk = 4 + c % 2
            bank_wait(bk)
            p.mm(ps[bk][:, :], t["wfbd"][:, :], kql[c], start=True, stop=True,
                 sem="pe")
            pe_qf = p.c["pe"]
            p.w("S", "pe", pe_qf)
            p.act(qfl[c], ps[bk][:, :], AF.Relu, bias=t["bf2"][:, 0:1])
            qf_se[c] = p.c["se"]
            bank_consumed(bk, "se")

        if tile == 0:
            p.w("T", "ve", ve_bd)
        rcp2 = [t["rcp_bf"], t["rcp_bfb"]]
        rcp_done = [0] * NCH
        bcast_pe = [0] * NCH
        yl = [t["y"][:, c * TT:(c + 1) * TT] for c in range(NCH)]
        for c in range(NCH):
            bk = 4 + c % 2
            bo_ = 6 + c % 2
            bank_wait(bk)
            p.w("T", "se", qf_se[c])
            p.mm(ps[bk][0:2, :], t["ksc"][:, c * 2:(c + 1) * 2], qfl[c],
                 start=True, stop=True, sem="pe")
            pe_n = p.c["pe"]
            p.w("V", "pe", pe_n)
            p.ts(t["normsb"][:, :], ps[bk][0:2, :], 1e-6, ALU.max)
            bank_consumed(bk)
            p.vrecip(t["normsb"][:, :], t["normsb"][:, :])
            if c >= 2:
                p.w("V", "pe", bcast_pe[c - 2])
            p.vcopy(rcp2[c % 2][:, :], t["normsb"][:, :])
            rcp_done[c] = p.c["ve"]
            bank_wait(bk)
            p.w("T", "ve", rcp_done[c])
            p.mm(ps[bk][:, :], t["pat2"][:, :], rcp2[c % 2][:, :],
                 start=True, stop=True, sem="pe")
            bank_wait(bo_)
            p.mm(ps[bo_][:, :], t["BD"][:, c * 128:(c + 1) * 128], qfl[c],
                 start=True, stop=True, sem="pe")
            bcast_pe[c] = p.c["pe"]
            pe_o2 = p.c["pe"]
            p.w("V", "pe", pe_o2)
            p.vcopy(t["rbc"][:, :], ps[bk][:, :])
            bank_consumed(bk)
            p.tt(yl[c], ps[bo_][:, :], t["rbc"][:, :], ALU.mult)
            bank_consumed(bo_)
        ve_y = p.c["ve"]

        p.w("T", "dwB", st["wB_val"])
        p.w("T", "ve", ve_y)
        zl = [t["z"][:, oc * TT:(oc + 1) * TT] for oc in range(NCH)]
        p.w("S", "pe", st["sq_last_pe"])
        zsq_ck = []
        for oc in range(NCH):
            bA, bB = (0, 1) if oc % 2 == 0 else (2, 3)
            bank_wait(bA)
            bank_wait(bB)
            for ic in range(NCH):
                p.mm2(ps[bA][:, :], ps[bB][:, :],
                      t["WB"][:, ic * D + oc * 128: ic * D + (oc + 1) * 128],
                      yl[ic], start=(ic == 0), stop=(ic == NCH - 1),
                      sem="pe" if ic == NCH - 1 else None)
            pe_z = p.c["pe"]
            p.w("S", "pe", pe_z)
            p.w("S", "ve", st["l3_ve"])
            p.act(t["ltmp3"][:, :], ps[bA][:, :], AF.Identity,
                  bias=t["bo_t"][:, oc:oc + 1])
            bank_consumed(bA, "se")
            se_i = p.c["se"]
            p.w("V", "se", se_i)
            p.tt(zl[oc], t["ltmp3"][:, :], ps[bB][:, :], ALU.add)
            st["l3_ve"] = p.c["ve"]
            bank_consumed(bB)
            ve_z = p.c["ve"]
            p.w("S", "ve", ve_z)
            p.act(t["sq"][:, oc * TT:(oc + 1) * TT], zl[oc], AF.Square)
            zsq_ck.append(("se", p.c["se"]))

        nxt = p2_partA(tile + 1) if tile + 1 < seq_tiles else None

        p.w("V", "dsp", p.c["dsp"])
        xpl = [t["xp"][:, ic * TT:(ic + 1) * TT] for ic in range(NCH)]
        sn2 = ln_fm(p, st, bank_wait, bank_consumed, None, zl, xpl, (4, 5),
                    x_add=xf, src_ve=p.c["ve"], sq_ck=zsq_ck, warm_bank=0)
        ve_xp = p.c["ve"]
        gp_xp = sn2["gp_end"]
        st["xbuf_free"][b] = [("ve", ve_xp), ("se", p.c["se"]), ("gp", gp_xp)]
        load_entry(j + 2)

        p.w("Y", "ve", ve_xp)
        p.w("Y", "gp", gp_xp)
        p.dma("Y", dr["xp_hbm"][s, tile], t["xp"][:, :], "dsp")
        return nxt

    pa = p2_partA(0)
    for _tile in range(seq_tiles):
        pa = p2_partB(pa)

    _barrier(p, engines="TVS", dma=("dsp",))
    for w in range(nwin):
        emit_ffn_window(p, dr, st, s, w, seq_tiles, win_tiles, bank_wait,
                        bank_consumed, out_dmas)


def ffn_slice_dma(p, st, dr, which, idx):
    k = st["ffn_dma"]
    buf = p.t[("wc0", "wc1", "wc2")[k % 3]]
    sem = ("dw0", "dw1", "dw2")[k % 3]
    if k >= 3:
        done = st["wslice_done"].get(k - 3)
        if done is not None:
            p.w("G", "pe", done)
    p.dma("G", buf[:, :], dr[which][idx], sem)
    st["ffn_dma"] += 1


def ffn_slice_use(p, st):
    k = st["ffn_slice"]
    buf = p.t[("wc0", "wc1", "wc2")[k % 3]]
    sem = ("dw0", "dw1", "dw2")[k % 3]
    p.w("T", sem, 16 * (k // 3 + 1))
    st["ffn_slice"] += 1
    return buf, k


def emit_ffn_window(p, dr, st, s, w, seq_tiles, win_tiles, bank_wait, bank_consumed,
                    out_dmas):
    t = p.t
    ps = t["ps"]
    seq = seq_tiles * TT
    W = win_tiles * TT
    NH = win_tiles

    al = lambda oc, h: t["a"][:, oc * W + h * TT: oc * W + (h + 1) * TT]
    bl = lambda oc, h: t["b"][:, oc * W + h * TT: oc * W + (h + 1) * TT]
    xn_rhs = lambda ic, h: t["xnw"][:, h * 4096 + ic * TT: h * 4096 + (ic + 1) * TT]

    _barrier(p, engines="VS", dma=("dout", "dxr"))
    p.w("Y", "pe", p.c["pe"])
    p.w("Y", "ve", p.c["ve"])
    p.w("Y", "se", p.c["se"])
    p.w("Y", "dsp", p.c["dsp"])
    for ti in range(win_tiles):
        p.dma("Y", t["xpr"][:, ti * 4096:(ti + 1) * 4096],
              dr["xp_hbm"][s, w * win_tiles + ti], "dxr")
    dxr0 = p.c["dxr"]

    for ti in range(win_tiles):
        xpl3 = [t["xpr"][:, ti * 4096 + ic * TT: ti * 4096 + (ic + 1) * TT]
                for ic in range(NCH)]
        p.w("S", "dxr", dxr0)
        p.w("S", "pe", st["sq_last_pe"])
        sq3_ck = []
        x1bf = [t["a"][:, ic * TT:(ic + 1) * TT] for ic in range(NCH)]
        for ic in range(NCH):
            p.act(x1bf[ic], xpl3[ic], AF.Copy)
            p.act(t["sq"][:, ic * TT:(ic + 1) * TT], xpl3[ic], AF.Square)
            sq3_ck.append(("se", p.c["se"]))
        xnl = [t["xnw"][:, ti * 4096 + ic * TT: ti * 4096 + (ic + 1) * TT]
               for ic in range(NCH)]
        ln_fm(p, st, bank_wait, bank_consumed, xpl3, x1bf, xnl, (2, 3),
              sq_ck=sq3_ck, warm_bank=0)
    ve_xnw = p.c["ve"]
    p.w("T", "ve", ve_xnw)

    for slab in range(NCH):
        ffn_slice_dma(p, st, dr, "w1_t", slab)
        ffn_slice_dma(p, st, dr, "w2_t", slab)
        b1, k1 = ffn_slice_use(p, st)
        b2, k2 = ffn_slice_use(p, st)
        for ocl in range(4):
            oc = slab * 4 + ocl
            for h in range(NH):
                w1A, w1B = (0 + 2 * h, 1 + 2 * h)
                w2A, w2B = (4 + 2 * h, 5 + 2 * h)
                bank_wait(w1A)
                bank_wait(w1B)
                for ic in range(NCH):
                    p.mm2(ps[w1A][:, :], ps[w1B][:, :],
                          b1[:, ic * 512 + ocl * 128: ic * 512 + ocl * 128 + 128],
                          xn_rhs(ic, h), start=(ic == 0), stop=(ic == NCH - 1),
                          sem="pe" if ic == NCH - 1 else None)
                pe_w1 = p.c["pe"]
                bank_wait(w2A)
                bank_wait(w2B)
                for ic in range(NCH):
                    p.mm2(ps[w2A][:, :], ps[w2B][:, :],
                          b2[:, ic * 512 + ocl * 128: ic * 512 + ocl * 128 + 128],
                          xn_rhs(ic, h), start=(ic == 0), stop=(ic == NCH - 1),
                          sem="pe" if ic == NCH - 1 else None)
                pe_w2 = p.c["pe"]
                lt0, lt2, lt3 = (("ltmp", "ltmp2", "ltmp3") if h == 0 else
                                 ("ltmpB", "ltmp2B", "ltmp3B"))
                lkey = "ltmp_ve" if h == 0 else "ltmp_veB"
                p.w("S", "pe", pe_w1)
                p.w("S", "ve", st[lkey])
                p.act(t[lt2][:, :], ps[w1A][:, :], AF.Identity)
                bank_consumed(w1A, "se")
                se_i1 = p.c["se"]
                p.w("V", "se", se_i1)
                p.tt(t[lt3][:, :], t[lt2][:, :], ps[w1B][:, :], ALU.add)
                bank_consumed(w1B)
                ve_s1 = p.c["ve"]
                p.w("S", "ve", ve_s1)
                p.act(t[lt0][:, :], t[lt3][:, :], AF.Silu,
                      scale=t["s1"][:, oc:oc + 1])
                se_t1 = p.c["se"]
                p.w("S", "pe", pe_w2)
                p.act(t[lt2][:, :], ps[w2A][:, :], AF.Identity)
                bank_consumed(w2A, "se")
                se_i2 = p.c["se"]
                p.w("V", "pe", pe_w2)
                p.w("V", "se", se_i2)
                p.tt(t[lt3][:, :], t[lt2][:, :], ps[w2B][:, :], ALU.add)
                bank_consumed(w2B)
                p.stt(al(oc, h), t[lt3][:, :], t["s2"][:, oc:oc + 1],
                      t[lt0][:, :], ALU.mult, ALU.mult)
                st[lkey] = p.c["ve"]
        st["wslice_done"][k1] = p.c["pe"]
        st["wslice_done"][k2] = p.c["pe"]

    ve_a = p.c["ve"]
    for wt, sc_t, srcf, dstf in (("h1_t", "sh1", al, bl), ("h2_t", "sh2", bl, al)):
        if wt == "h1_t":
            p.w("T", "ve", ve_a)
        else:
            p.w("T", "se", p.c["se"])
        for oc in range(NHC):
            ffn_slice_dma(p, st, dr, wt, oc)
            wb, k = ffn_slice_use(p, st)
            for h in range(NH):
                bA = 4 * (oc % 2) + 2 * h
                bB = bA + 1
                bank_wait(bA)
                bank_wait(bB)
                for ic in range(NHC):
                    p.mm2(ps[bA][:, :], ps[bB][:, :], wb[:, ic * 128:(ic + 1) * 128],
                          srcf(ic, h), start=(ic == 0), stop=(ic == NHC - 1),
                          sem="pe" if ic == NHC - 1 else None)
                pe_h = p.c["pe"]
                lt0, lt2 = ("ltmp", "ltmp2") if h == 0 else ("ltmpB", "ltmp2B")
                lkey = "ltmp_ve" if h == 0 else "ltmp_veB"
                p.w("S", "pe", pe_h)
                p.w("S", "ve", st[lkey])
                p.act(t[lt0][:, :], ps[bA][:, :], AF.Identity)
                bank_consumed(bA, "se")
                se_i = p.c["se"]
                p.w("V", "se", se_i)
                p.tt(t[lt2][:, :], t[lt0][:, :], ps[bB][:, :], ALU.add)
                bank_consumed(bB)
                st[lkey] = p.c["ve"]
                p.w("S", "ve", st[lkey])
                p.act(dstf(oc, h), t[lt2][:, :], AF.Gelu,
                      scale=t[sc_t][:, oc:oc + 1])
            st["wslice_done"][k] = p.c["pe"]

    p.w("Y", "pe", p.c["pe"])
    p.w("Y", "dsp", p.c["dsp"])
    for ti in range(win_tiles):
        p.dma("Y", t["xpr"][:, ti * 4096:(ti + 1) * 4096],
              dr["xp_hbm"][s, w * win_tiles + ti], "dxr")
    dxr_val = p.c["dxr"]
    fl = lambda oc, h: t["f"][:, oc * W + h * TT: oc * W + (h + 1) * TT]
    p.w("T", "se", p.c["se"])
    for oc in range(NCH):
        ffn_slice_dma(p, st, dr, "w3_t", oc)
        wb, k = ffn_slice_use(p, st)
        for h in range(NH):
            bA = 4 * (oc % 2) + 2 * h
            bB = bA + 1
            bank_wait(bA)
            bank_wait(bB)
            for ic in range(NHC):
                p.mm2(ps[bA][:, :], ps[bB][:, :], wb[:, ic * 128:(ic + 1) * 128],
                      al(ic, h), start=(ic == 0), stop=(ic == NHC - 1),
                      sem="pe" if ic == NHC - 1 else None)
            pe_f = p.c["pe"]
            lt0 = "ltmp" if h == 0 else "ltmpB"
            lkey = "ltmp_ve" if h == 0 else "ltmp_veB"
            p.w("S", "pe", pe_f)
            p.w("S", "ve", st[lkey])
            p.act(t[lt0][:, :], ps[bA][:, :], AF.Identity,
                  scale=t["s3"][:, oc:oc + 1])
            bank_consumed(bA, "se")
            se_i = p.c["se"]
            p.w("V", "se", se_i)
            p.stt(fl(oc, h), ps[bB][:, :], t["s3"][:, oc:oc + 1], t[lt0][:, :],
                  ALU.mult, ALU.add)
            bank_consumed(bB)
            st[lkey] = p.c["ve"]
        st["wslice_done"][k] = p.c["pe"]

    p.w("V", "dxr", dxr_val)
    for h in range(NH):
        fbf = [fl(ic, h) for ic in range(NCH)]
        p.w("S", "pe", st["sq_last_pe"])
        p.w("S", "ve", st["ltmp_ve"])
        p.w("S", "ve", st["ltmp_veB"])
        sq4_ck = []
        for ic in range(NCH):
            p.act(t["sq"][:, ic * TT:(ic + 1) * TT], fbf[ic], AF.Square)
            sq4_ck.append(("se", p.c["se"]))
        xprl = [t["xpr"][:, h * 4096 + ic * TT: h * 4096 + (ic + 1) * TT]
                for ic in range(NCH)]
        outl = [t["outt"][:, ic * TT:(ic + 1) * TT] for ic in range(NCH)]
        p.w("V", "dout", p.c["dout"])
        ln_fm(p, st, bank_wait, bank_consumed, None, fbf, outl, (0, 1), x_add=xprl,
              src_ve=p.c["ve"], sq_ck=sq4_ck, warm_bank=4)
        ve_out = p.c["ve"]
        p.w("Y", "ve", ve_out)
        p.w("Y", "gp", p.c["gp"])
        tglob = w * win_tiles + h
        p.dma("Y",
              dr["out"][s][:, :, tglob * TT:(tglob + 1) * TT].rearrange("i p tt -> p i tt"),
              t["outt"][:, :].rearrange("p (i tt) -> p i tt", tt=TT), "dout")
        out_dmas[0] += 1



def _prep_inputs(L0_attn_x, L0_attn_Wq, L0_attn_Wk, L0_attn_Wv, L0_attn_Wo,
                 L0_attn_bo, L0_attn_Wf, L0_attn_bf,
                 L0_ffn_w1_w, L0_ffn_w1_s, L0_ffn_w2_w, L0_ffn_w2_s,
                 L0_ffn_h1_w, L0_ffn_h1_s, L0_ffn_h2_w, L0_ffn_h2_s,
                 L0_ffn_w3_w, L0_ffn_w3_s):
    f32 = np.float32
    asc = np.ascontiguousarray

    def tile_oi(wT, n_in_ch, n_out_ch, ocp):
        r = wT.reshape(n_in_ch, 128, n_out_ch, ocp).transpose(2, 1, 0, 3)
        return asc(r.reshape(n_out_ch, 128, n_in_ch * ocp).astype(f32))

    sh = {}
    for n, wmat in (("wq", L0_attn_Wq), ("wk", L0_attn_Wk), ("wv", L0_attn_Wv),
                    ("wo", L0_attn_Wo)):
        sh[n] = asc(np.asarray(wmat, dtype=f32))
    wf = np.asarray(L0_attn_Wf, dtype=f32)
    wfbd = np.zeros((128, 128), f32)
    wfbd[:64, :64] = wf
    wfbd[64:, 64:] = wf
    sh["wfbd"] = wfbd
    bfv = np.asarray(L0_attn_bf, dtype=f32)
    sh["bfbc"] = asc(np.broadcast_to(np.concatenate([bfv, bfv])[None, :],
                                     (128, 128)).astype(f32))
    sh["bf2"] = asc(np.concatenate([bfv, bfv])[:, None].astype(f32))
    sh["bo_t"] = asc(np.asarray(L0_attn_bo, dtype=f32).reshape(NCH, 128).T)
    w1T = np.asarray(L0_ffn_w1_w, dtype=f32).T
    w2T = np.asarray(L0_ffn_w2_w, dtype=f32).T
    h1T = np.asarray(L0_ffn_h1_w, dtype=f32).T
    h2T = np.asarray(L0_ffn_h2_w, dtype=f32).T
    w3T = np.asarray(L0_ffn_w3_w, dtype=f32).T
    sh["w1_t"] = tile_oi(w1T, 8, 8, 512)
    sh["w2_t"] = tile_oi(w2T, 8, 8, 512)
    sh["h1_t"] = tile_oi(h1T, 32, 32, 128)
    sh["h2_t"] = tile_oi(h2T, 32, 32, 128)
    sh["w3_t"] = tile_oi(w3T, 32, 8, 128)
    for n, v in (("s1_t", L0_ffn_w1_s), ("s2_t", L0_ffn_w2_s),
                 ("sh1_t", L0_ffn_h1_s), ("sh2_t", L0_ffn_h2_s)):
        sh[n] = asc(np.asarray(v, dtype=f32).reshape(-1, 128).T)
    sh["s3_t"] = asc(np.asarray(L0_ffn_w3_s, dtype=f32).reshape(NCH, 128).T)
    pat = np.zeros((2, 128), f32)
    pat[0, :64] = 1.0
    pat[1, 64:] = 1.0
    sh["pat2"] = pat
    sh["bfrow"] = asc(np.concatenate([bfv, bfv])[None, :].astype(f32))
    sh["warm512"] = np.ones((128, TT), f32)
    sh["onestat"] = np.full((128, 1), 1.0 / D, f32)
    sh["onesrow"] = np.ones((1, 128), f32)

    x = np.asarray(L0_attn_x, dtype=f32)
    percore = []
    for c in range(8):
        xc = x[c * 2:(c + 1) * 2]
        xt = xc.transpose(0, 2, 1).reshape(2, NCH, 128, xc.shape[1])
        percore.append(asc(xt.astype(f32)))
    return sh, percore


_NC_CACHE = {}


def kernel(**inputs):
    sh, percore = _prep_inputs(**inputs)
    if "full" not in _NC_CACHE:
        _NC_CACHE["full"] = build_program()
    nc = _NC_CACHE["full"]
    in_maps = []
    for c in range(8):
        m = dict(sh)
        m["x"] = percore[c]
        in_maps.append(m)
    res = run_bass_kernel_spmd(nc, in_maps, core_ids=list(range(8)))
    outs = []
    for c in range(8):
        o = res.results[c]["out"]
        o = o.reshape(2, D, -1).transpose(0, 2, 1)
        outs.append(o)
    return np.ascontiguousarray(np.concatenate(outs, axis=0)).astype(np.float32)


# version 12
# speedup vs baseline: 1.1020x; 1.0206x over previous
import contextlib
import os as _os
import numpy as np
import concourse.bass as bass
import concourse.mybir as mybir
from concourse.bass_utils import run_bass_kernel_spmd

F32 = mybir.dt.float32
BF16 = mybir.dt.bfloat16
U8 = mybir.dt.uint8
AF = mybir.ActivationFunctionType
ALU = mybir.AluOpType

D = 1024
NCH = 8
HID = 4096
NHC = 32
TT = 512
NSEQ = 2
SEQ_TILES = 4
WIN_TILES = 2
EPS = 1e-5

KB = 1024


class P:

    def __init__(self, nc, ename, eng, t, sems):
        self.nc = nc
        self.E = ename
        self.h = eng
        self.t = t
        self.S = sems
        self.c = {k: 0 for k in sems}

    def bump(self, sem, n):
        self.c[sem] += n
        return self.c[sem]

    def _inc(self, ins, sem, n):
        if sem is None:
            return
        self.bump(sem, n)
        if ins is not None:
            ins.then_inc(self.S[sem], n)

    def w(self, eng, sem, val):
        if val <= 0:
            return
        if self.E == eng:
            self.h.wait_ge(self.S[sem], val)

    def mm(self, out, lhsT, rhs, start, stop, sem=None):
        ins = None
        if self.E == "T":
            ins = self.h.matmul(out, lhsT, rhs, start=start, stop=stop)
        self._inc(ins, sem, 1)

    def gtt(self, out, in0, in1, op, sem="gp"):
        ins = None
        if self.E == "G":
            if self.c["gp"] > 0:
                self.h.wait_ge(self.S["gp"], self.c["gp"])
            ins = self.h.tensor_tensor(out, in0, in1, op)
        self._inc(ins, sem, 1)

    def gw(self, sem, val):
        self.w("G", sem, val)

    def mmwarm(self, out, lhsT, rhs):
        if self.E == "T":
            self.h.matmul(out, lhsT, rhs, start=True, stop=True)

    def mm2(self, outA, outB, lhsT, rhs, start, stop, sem=None):
        ins = None
        if self.E == "T":
            self.h.matmul(outA, lhsT[0:64, :], rhs[0:64, :], start=start,
                          stop=stop, tile_position=(0, 0))
            ins = self.h.matmul(outB, lhsT[64:128, :], rhs[64:128, :], start=start,
                                stop=stop, tile_position=(64, 0))
        self._inc(ins, sem, 1)

    def act(self, out, in_, func, bias=0.0, scale=1.0, sem="se"):
        ins = None
        if self.E == "S":
            if self.c["se"] > 0:
                self.h.wait_ge(self.S["se"], self.c["se"])
            ins = self.h.activation(out, in_, func, bias=bias, scale=scale)
        self._inc(ins, sem, 1)

    def _vsync(self):
        if self.E == "V" and self.c["ve"] > 0:
            self.h.wait_ge(self.S["ve"], self.c["ve"])

    def tt(self, out, in0, in1, op, sem="ve"):
        ins = None
        if self.E == "V":
            self._vsync()
            ins = self.h.tensor_tensor(out, in0, in1, op)
        self._inc(ins, sem, 1)

    def ts(self, out, in0, s1, op, sem="ve"):
        ins = None
        if self.E == "V":
            self._vsync()
            ins = self.h.tensor_single_scalar(out, in0, s1, op)
        self._inc(ins, sem, 1)

    def stt(self, out, in0, scal, in1, op0, op1, sem="ve"):
        ins = None
        if self.E == "V":
            self._vsync()
            ins = self.h.scalar_tensor_tensor(out, in0, scal, in1, op0, op1)
        self._inc(ins, sem, 1)

    def vcopy(self, out, in_, sem="ve"):
        ins = None
        if self.E == "V":
            self._vsync()
            ins = self.h.tensor_copy(out, in_)
        self._inc(ins, sem, 1)

    def vrecip(self, out, in_, sem="ve"):
        ins = None
        if self.E == "V":
            self._vsync()
            ins = self.h.reciprocal(out, in_)
        self._inc(ins, sem, 1)

    def vmemset(self, ap, val, sem="ve"):
        ins = None
        if self.E == "V":
            self._vsync()
            ins = self.h.memset(ap, val)
        self._inc(ins, sem, 1)

    def dma(self, q, out, in_, sem):
        ins = None
        if self.E == q:
            ins = self.h.dma_start(out=out, in_=in_)
        self._inc(ins, sem, 16)


def _alloc_views(nc, arena, layout):
    base = nc.lookup_mloc(arena).addr
    out = {}
    for name, shape, dtype, off in layout:
        out[name] = nc.alloc_sbuf_tensor_at(name, shape, dtype, offset=base + off)
    return out


def build_program(nseq=NSEQ, seq_tiles=SEQ_TILES, win_tiles=WIN_TILES):
    nc = bass.Bass("TRN2", target_bir_lowering=False)
    seq = seq_tiles * TT
    W = win_tiles * TT

    dr = {}
    dr["x"] = nc.dram_tensor("x", [nseq, NCH, 128, seq], F32, kind="ExternalInput")
    dr["out"] = nc.dram_tensor("out", [nseq, NCH, 128, seq], F32, kind="ExternalOutput")
    for n in ("wq", "wk", "wv", "wo"):
        dr[n] = nc.dram_tensor(n, [D, D], F32, kind="ExternalInput")
    for n, shp in (("wfbd", [128, 128]), ("bfbc", [128, 128]), ("bo_t", [128, NCH]),
                   ("bf2", [128, 1]),
                   ("w1_t", [NCH, 128, 4096]), ("w2_t", [NCH, 128, 4096]),
                   ("h1_t", [NHC, 128, 4096]), ("h2_t", [NHC, 128, 4096]),
                   ("w3_t", [NCH, 128, 4096]),
                   ("s1_t", [128, NHC]), ("s2_t", [128, NHC]),
                   ("sh1_t", [128, NHC]), ("sh2_t", [128, NHC]), ("s3_t", [128, NCH]),
                   ("pat2", [2, 128]), ("onestat", [128, 1]), ("onesrow", [1, 128]),
                   ("bfrow", [1, 128]), ("warm512", [128, TT])):
        dr[n] = nc.dram_tensor(n, shp, F32, kind="ExternalInput")
    dr["xp_hbm"] = nc.dram_tensor("xp_hbm", [nseq, seq_tiles, 128, 4096], F32,
                                  kind="Internal")

    ARENA = 206 * KB
    arena = nc.alloc_sbuf_tensor("arena", [128, ARENA], U8)
    SC = 3584
    B0 = SC + 49 * KB
    lay = [
        ("wfbd", [128, 128], BF16, 0),
        ("bfbc", [128, 128], BF16, 256),
        ("bo_t", [128, NCH], F32, 512),
        ("bf2", [128, 1], F32, 544),
        ("s1", [128, NHC], F32, 576),
        ("s2", [128, NHC], F32, 704),
        ("sh1", [128, NHC], F32, 832),
        ("sh2", [128, NHC], F32, 960),
        ("s3", [128, NCH], F32, 1088),
        ("onestat", [128, 1], BF16, 1120),
        ("onesrow", [1, 128], BF16, 1152),
        ("bfrow", [1, 128], BF16, 1408),
        ("warm512", [128, TT], BF16, 1920),
        ("pat2", [2, 128], BF16, 1664),
        ("ltmp", [128, TT], F32, SC),
        ("ltmp2", [128, TT], F32, SC + 2 * KB),
        ("msb", [1, TT], F32, SC + 4 * KB),
        ("vsb", [1, TT], F32, SC + 6 * KB),
        ("r_bf", [1, TT], BF16, SC + 8 * KB),
        ("mr_bf", [1, TT], BF16, SC + 9 * KB),
        ("normsb", [2, TT], F32, SC + 10 * KB),
        ("rcp_bf", [2, TT], BF16, SC + 12 * KB),
        ("rcp_bfb", [2, TT], BF16, SC + 47 * KB),
        ("r_save", [1, 4 * TT], BF16, SC + 13 * KB),
        ("mr_save", [1, 4 * TT], BF16, SC + 17 * KB),
        ("sq", [128, 4096], BF16, SC + 21 * KB),
        ("wc0", [128, 4096], BF16, SC + 29 * KB),
        ("wc1", [128, 4096], BF16, SC + 37 * KB),
        ("ltmp3", [128, TT], F32, SC + 45 * KB),
        ("ltmpB", [128, TT], F32, B0 + 138 * KB),
        ("ltmp2B", [128, TT], F32, B0 + 140 * KB),
        ("ltmp3B", [128, TT], F32, B0 + 142 * KB),
        ("wc2", [128, 4096], BF16, B0 + 144 * KB),

        ("xb0", [128, 4096], F32, B0 + 0),
        ("xb1", [128, 4096], F32, B0 + 16 * KB),
        ("xbf", [128, 4096], BF16, B0 + 32 * KB),
        ("WA", [128, NCH * D], BF16, B0 + 40 * KB),
        ("WB", [128, NCH * D], BF16, B0 + 56 * KB),
        ("xa", [128, 4096], BF16, B0 + 72 * KB),
        ("kq", [128, 4096], BF16, B0 + 80 * KB),
        ("vtm", [128, 32 * 132], BF16, B0 + 88 * KB),
        ("qf", [128, 4096], BF16, B0 + 88 * KB),
        ("kftm", [128, 4096], BF16, B0 + 97 * KB),
        ("y", [128, 4096], BF16, B0 + 97 * KB),
        ("kv", [128, NCH * 132], F32, B0 + 105 * KB),
        ("BD", [128, NCH * 128], BF16, B0 + 110 * KB),
        ("ksc", [128, NCH * 2], BF16, B0 + 112 * KB),
        ("rbc", [128, TT], BF16, B0 + 113 * KB),
        ("z", [128, 4096], BF16, B0 + 114 * KB),
        ("xp", [128, 4096], F32, B0 + 122 * KB),
        ("a", [128, NHC * W], BF16, B0 + 0),
        ("b", [128, NHC * W], BF16, B0 + NHC * W * 2),
        ("xnw", [128, win_tiles * 4096], BF16,
         B0 + NHC * W * 2 + win_tiles * 16 * KB),
        ("xpr", [128, win_tiles * 4096], F32, B0 + NHC * W * 2),
        ("f", [128, NCH * W], BF16, B0 + NHC * W * 2 + win_tiles * 16 * KB),
        ("outt", [128, 4096], F32, B0 + NHC * W * 2 + win_tiles * 16 * KB + NCH * W * 2),
    ]
    need = max(B0 + 2 * NHC * W * 2, B0 + 152 * KB)
    assert need <= ARENA, ("big region overflow", need, ARENA)
    t = _alloc_views(nc, arena, lay)
    t["arena"] = arena
    t["ps"] = [nc.alloc_psum_tensor(f"psb{i}", [128, 512], F32) for i in range(8)]

    sem_names = ["pe", "ve", "se", "gp", "dcon", "dwA", "dwB", "dbf",
                 "dxa", "dxb", "dw0", "dw1", "dw2", "dsp", "dxr", "dout"]
    stack = contextlib.ExitStack()
    sems = {n: stack.enter_context(nc.semaphore(n)) for n in sem_names}

    counters = []
    with nc.Block() as block:
        def run(ename):
            def f(eng):
                p = P(nc, ename, eng, t, sems)
                emit_all(p, dr, nseq, seq_tiles, win_tiles)
                counters.append(dict(p.c))
            return f
        block.tensor(run("T"))
        block.vector(run("V"))
        block.scalar(run("S"))
        block.gpsimd(run("G"))
        block.sync(run("Y"))
    stack.close()
    for c in counters[1:]:
        assert c == counters[0], ("counter mismatch", counters)
    return nc



def _barrier(p, engines="TVSY", dma=()):
    cur = {k: p.c[k] for k in ("pe", "ve", "se", "gp")}
    own = {"T": "pe", "V": "ve", "S": "se", "G": "gp"}
    for e in engines:
        for sem, v in cur.items():
            if own.get(e) != sem:
                p.w(e, sem, v)
        for dsem in dma:
            p.w(e, dsem, p.c[dsem])


def emit_all(p, dr, nseq, seq_tiles, win_tiles):
    t = p.t

    st = {
        "bank_free": [[] for _ in range(8)],
        "xbuf_free": [[], []],
        "xbf_free": [],
        "x_val": {},
        "xbf_val": {},
        "wslice_done": {},
        "ffn_slice": 0,
        "ffn_dma": 0,
        "sq_last_pe": 0,
        "l3_ve": 0,
        "l3_gp": 0,
        "ltmp_ve": 0,
        "ltmp_veB": 0,
        "wA_val": 0, "wB_val": 0,
    }

    def bank_wait(b):
        for semv in st["bank_free"][b]:
            p.w("T", *semv)
        st["bank_free"][b] = []

    def bank_consumed(b, sem="ve"):
        st["bank_free"][b].append((sem, p.c[sem]))

    for n, src in (("wfbd", "wfbd"), ("bfbc", "bfbc"), ("bo_t", "bo_t"), ("bf2", "bf2"),
                   ("s1", "s1_t"), ("s2", "s2_t"), ("sh1", "sh1_t"), ("sh2", "sh2_t"),
                   ("s3", "s3_t"), ("onestat", "onestat"), ("onesrow", "onesrow"),
                   ("pat2", "pat2"), ("bfrow", "bfrow"), ("warm512", "warm512")):
        p.dma("G", t[n][:, :], dr[src][:, :], "dcon")
    NCONST = p.c["dcon"]
    for e in "TVSY":
        p.w(e, "dcon", NCONST)
    p.w("G", "dcon", NCONST)

    out_dmas = [0]
    for s in range(nseq):
        emit_seq(p, dr, st, s, seq_tiles, win_tiles, bank_wait, bank_consumed, out_dmas)

    p.w("Y", "dout", p.c["dout"])


_WARM_PER = int(_os.environ.get("MK_WARM_PER", "0"))


def weave(p, bank_wait, wb, checkpoints, per=3):
    if _WARM_PER >= 0:
        per = _WARM_PER
    if per == 0:
        return
    """Keep the PE clock warm while tracking a cross-engine chain: step through
    semaphore checkpoints, emitting a few dummy matmuls between each so PE-idle
    stretches stay under the HAM re-throttle window."""
    t = p.t
    bank_wait(wb)
    for sem, v in checkpoints:
        p.w("T", sem, v)
        for _ in range(per):
            p.mmwarm(t["ps"][wb][0:1, :], t["onestat"][:, 0:1], t["warm512"][:, :])


def ln_fm(p, st, bank_wait, bank_consumed, src_f32, src_bf, dst, stat_banks,
          x_add=None, src_ve=0, sq_ck=None, warm_bank=None, save_to=None):
    t = p.t
    ps = t["ps"]
    bm, bs = stat_banks
    se_sq = p.c["se"]
    bank_wait(bm)
    bank_wait(bs)
    if warm_bank is not None and sq_ck:
        weave(p, bank_wait, warm_bank, sq_ck, per=3)
    p.w("T", "se", se_sq)
    p.w("T", "ve", src_ve)
    for ic in range(NCH):
        p.mm(ps[bm][0:1, :], t["onestat"][:, 0:1], src_bf[ic],
             start=(ic == 0), stop=(ic == NCH - 1),
             sem="pe" if ic == NCH - 1 else None)
    for ic in range(NCH):
        p.mm(ps[bs][0:1, :], t["onestat"][:, 0:1], t["sq"][:, ic * TT:(ic + 1) * TT],
             start=(ic == 0), stop=(ic == NCH - 1),
             sem="pe" if ic == NCH - 1 else None)
    pe_stats = p.c["pe"]
    st["sq_last_pe"] = pe_stats
    p.w("V", "pe", pe_stats)
    p.vcopy(t["msb"][:, :], ps[bm][0:1, :])
    p.tt(t["vsb"][:, :], t["msb"][:, :], t["msb"][:, :], ALU.mult)
    p.stt(t["vsb"][:, :], ps[bs][0:1, :], float(EPS), t["vsb"][:, :],
          ALU.add, ALU.subtract)
    ve_v = p.c["ve"]
    p.w("S", "ve", ve_v)
    p.act(t["vsb"][:, :], t["vsb"][:, :], AF.Ln)
    p.act(t["vsb"][:, :], t["vsb"][:, :], AF.Exp, scale=-0.5)
    se_sqrt = p.c["se"]
    p.w("V", "se", se_sqrt)
    p.vcopy(t["r_bf"][:, :], t["vsb"][:, :])
    p.tt(t["msb"][:, :], t["msb"][:, :], t["vsb"][:, :], ALU.mult)
    p.vcopy(t["mr_bf"][:, :], t["msb"][:, :])
    if save_to is not None:
        p.vcopy(save_to[0], t["r_bf"][:, :])
        p.vcopy(save_to[1], t["mr_bf"][:, :])
    ve_r = p.c["ve"]
    if warm_bank is not None:
        ck = [("ve", ve_v), ("se", p.c["se"]), ("ve", ve_r)]
        weave(p, bank_wait, warm_bank, ck, per=4)
    p.w("T", "ve", ve_r)
    p.mm(ps[bm][:, :], t["onesrow"][:, :], t["r_bf"][:, :], start=True, stop=True,
         sem="pe")
    p.mm(ps[bs][:, :], t["onesrow"][:, :], t["mr_bf"][:, :], start=True, stop=True,
         sem="pe")
    pe_bc = p.c["pe"]
    p.w("V", "pe", pe_bc)
    apply_ck = []
    for ic in range(NCH):
        srcm = src_f32[ic] if src_f32 is not None else src_bf[ic]
        p.tt(t["ltmp"][:, :], srcm, ps[bm][:, :], ALU.mult)
        if x_add is None:
            p.tt(dst[ic], t["ltmp"][:, :], ps[bs][:, :], ALU.subtract)
        else:
            p.tt(t["ltmp2"][:, :], t["ltmp"][:, :], ps[bs][:, :], ALU.subtract)
            p.tt(dst[ic], t["ltmp2"][:, :], x_add[ic], ALU.add)
        apply_ck.append(("ve", p.c["ve"]))
    st["ltmp_ve"] = p.c["ve"]
    bank_consumed(bm)
    bank_consumed(bs)
    return {"pe_stats": pe_stats, "pe_bc": pe_bc, "ve_end": p.c["ve"],
            "gp_end": p.c["gp"], "apply_ck": apply_ck}


def emit_seq(p, dr, st, s, seq_tiles, win_tiles, bank_wait, bank_consumed, out_dmas):
    t = p.t
    ps = t["ps"]
    seq = seq_tiles * TT
    nwin = seq_tiles // win_tiles
    xbufs = [t["xb0"], t["xb1"]]
    dx = ["dxa", "dxb"]

    _barrier(p, engines="TVSY", dma=("dsp", "dout"))
    _barrier(p, engines="G")

    p.dma("G", t["WA"][:, :].rearrange("p (i o) -> p i o", o=D),
          dr["wk"].ap().rearrange("(i p) o -> p i o", p=128), "dwA")
    st["wA_val"] = p.c["dwA"]
    p.dma("G", t["WB"][:, :].rearrange("p (i o) -> p i o", o=D),
          dr["wv"].ap().rearrange("(i p) o -> p i o", p=128), "dwB")
    st["wB_val"] = p.c["dwB"]

    entries = [(1, i) for i in range(seq_tiles)] + [(2, i) for i in range(seq_tiles)]

    def load_entry(j):
        if j >= len(entries):
            return
        _, tile = entries[j]
        b = j % 2
        for semv in st["xbuf_free"][b]:
            p.w("Y", *semv)
        st["xbuf_free"][b] = []
        p.dma("Y", xbufs[b][:, :].rearrange("p (i tt) -> p i tt", tt=TT),
              dr["x"][s][:, :, tile * TT:(tile + 1) * TT].rearrange("i p tt -> p i tt"),
              dx[b])
        st["x_val"][j] = p.c[dx[b]]

    def load_xbf(j):
        if j >= len(entries):
            return
        _, tile = entries[j]
        for semv in st["xbf_free"]:
            p.w("G", *semv)
        st["xbf_free"] = []
        p.dma("G", t["xbf"][:, :].rearrange("p (i tt) -> p i tt", tt=TT),
              dr["x"][s][:, :, tile * TT:(tile + 1) * TT].rearrange("i p tt -> p i tt"),
              "dbf")
        st["xbf_val"][j] = p.c["dbf"]

    load_entry(0)
    load_entry(1)
    load_xbf(0)

    p.vmemset(t["kv"][:, :], 0.0)
    vtm_r = t["vtm"][:, :].rearrange("p (sl e) -> p sl e", e=132)
    p.vmemset(vtm_r[:, :, 128:129], 1.0)

    def p1_ln(tile):
        j = tile
        b = j % 2
        xb = xbufs[b]
        xf = [xb[:, ic * TT:(ic + 1) * TT] for ic in range(NCH)]
        xbfl = [t["xbf"][:, ic * TT:(ic + 1) * TT] for ic in range(NCH)]
        p.w("S", dx[b], st["x_val"][j])
        p.w("V", dx[b], st["x_val"][j])
        p.w("T", "dbf", st["xbf_val"][j])
        p.w("S", "pe", st["sq_last_pe"])
        sq_ck = []
        for ic in range(NCH):
            p.act(t["sq"][:, ic * TT:(ic + 1) * TT], xf[ic], AF.Square)
            sq_ck.append(("se", p.c["se"]))
        xal = [t["xa"][:, ic * TT:(ic + 1) * TT] for ic in range(NCH)]
        sn1 = ln_fm(p, st, bank_wait, bank_consumed, xf, xbfl, xal, (6, 7),
                    sq_ck=sq_ck, warm_bank=0,
                    save_to=(t["r_save"][:, tile * TT:(tile + 1) * TT],
                             t["mr_save"][:, tile * TT:(tile + 1) * TT]))
        st["xbf_free"] = [("pe", p.c["pe"])]
        if j + 1 < seq_tiles:
            load_xbf(j + 1)
        return {"tile": tile, "j": j, "b": b, "xal": xal, "ve_xa": p.c["ve"]}

    pl = p1_ln(0)
    for tile in range(seq_tiles):
        j = pl["j"]
        b = pl["b"]
        xal = pl["xal"]
        ve_xa = pl["ve_xa"]

        p.w("T", "dwA", st["wA_val"])
        p.w("T", "ve", ve_xa)
        kql = [t["kq"][:, oc * TT:(oc + 1) * TT] for oc in range(NCH)]
        for oc in range(NCH):
            bA, bB = (0, 1) if oc % 2 == 0 else (2, 3)
            bank_wait(bA)
            bank_wait(bB)
            for ic in range(NCH):
                p.mm2(ps[bA][:, :], ps[bB][:, :],
                      t["WA"][:, ic * D + oc * 128: ic * D + (oc + 1) * 128],
                      xal[ic], start=(ic == 0), stop=(ic == NCH - 1),
                      sem="pe" if ic == NCH - 1 else None)
            pe_k = p.c["pe"]
            p.w("S", "pe", pe_k)
            p.w("S", "ve", st["l3_ve"])
            p.w("S", "gp", st["l3_gp"])
            p.act(t["ltmp3"][:, :], ps[bA][:, :], AF.Identity)
            bank_consumed(bA, "se")
            se_i = p.c["se"]
            p.w("V", "se", se_i)
            p.tt(kql[oc], t["ltmp3"][:, :], ps[bB][:, :], ALU.add)
            st["l3_ve"] = p.c["ve"]
            bank_consumed(bB)
        se_k = p.c["se"]
        ve_k = p.c["ve"]
        p.w("T", "ve", ve_k)

        p.w("T", "dwB", st["wB_val"])
        for sub in range(4):
            pairs = ((0, 1), (2, 3)) if sub % 2 == 0 else ((4, 5), (6, 7))
            for vh in range(2):
                bA, bB = pairs[vh]
                bank_wait(bA)
                bank_wait(bB)
                for ic in range(NCH):
                    p.mm2(ps[bA][:, :], ps[bB][:, :],
                          xal[ic][:, sub * 128:(sub + 1) * 128],
                          t["WB"][:, ic * D + vh * 512: ic * D + (vh + 1) * 512],
                          start=(ic == 0), stop=(ic == NCH - 1),
                          sem="pe" if ic == NCH - 1 else None)
                pe_v = p.c["pe"]
                p.w("S", "pe", pe_v)
                p.w("S", "ve", st["l3_ve"])
                p.w("S", "gp", st["l3_gp"])
                p.act(t["ltmp3"][:, :], ps[bA][:, :], AF.Identity)
                bank_consumed(bA, "se")
                se_i = p.c["se"]
                p.w("V", "se", se_i)
                dstv = vtm_r[:, sub * 8 + vh * 4: sub * 8 + (vh + 1) * 4, 0:128]
                p.tt(dstv, t["ltmp3"][:, :].rearrange("p (c e) -> p c e", e=128),
                     ps[bB][:, :].rearrange("p (c e) -> p c e", e=128), ALU.add)
                st["l3_ve"] = p.c["ve"]
                bank_consumed(bB)

        pl = p1_ln(tile + 1) if tile + 1 < seq_tiles else None

        kf_se = [0] * NCH
        for c in range(NCH):
            bk = 4 + c % 2
            bank_wait(bk)
            p.w("T", "se", se_k)
            for sub in range(4):
                p.mm(ps[bk][:, sub * 128:(sub + 1) * 128],
                     kql[c][:, sub * 128:(sub + 1) * 128],
                     t["wfbd"][:, :], start=True, stop=False)
                p.mm(ps[bk][:, sub * 128:(sub + 1) * 128],
                     t["onesrow"][:, :], t["bfrow"][:, :], start=False, stop=True,
                     sem="pe" if sub == 3 else None)
            pe_kf = p.c["pe"]
            p.w("S", "pe", pe_kf)
            kf_dst = t["kftm"][:, :].rearrange("p (s e) -> p s e", e=128)[:, c:c + 25:8, :]
            p.act(kf_dst, ps[bk][:, :].rearrange("p (s e) -> p s e", e=128), AF.Relu)
            kf_se[c] = p.c["se"]
            bank_consumed(bk, "se")

        for c in range(NCH):
            bk = 2 + c % 2
            bank_wait(bk)
            p.w("T", "se", kf_se[c])
            for sub in range(4):
                p.mm(ps[bk][:, 0:129],
                     t["kftm"][:, (sub * 8 + c) * 128:(sub * 8 + c) * 128 + 128],
                     t["vtm"][:, (sub * 8 + c) * 132:(sub * 8 + c) * 132 + 129],
                     start=(sub == 0), stop=(sub == 3),
                     sem="pe" if sub == 3 else None)
            pe_kv = p.c["pe"]
            p.w("V", "pe", pe_kv)
            p.tt(t["kv"][:, c * 132: c * 132 + 129], t["kv"][:, c * 132: c * 132 + 129],
                 ps[bk][:, 0:129], ALU.add)
            bank_consumed(bk)
        st["xbuf_free"][b] = [("ve", p.c["ve"]), ("se", p.c["se"]),
                              ("gp", p.c["gp"])]
        load_entry(j + 2)

    p.vmemset(t["BD"][:, :], 0.0)
    p.vmemset(t["ksc"][:, :], 0.0)
    for c in range(NCH):
        p.vcopy(t["BD"][0:64, c * 128: c * 128 + 64],
                t["kv"][0:64, c * 132: c * 132 + 64])
        p.vcopy(t["BD"][64:128, c * 128 + 64: c * 128 + 128],
                t["kv"][64:128, c * 132 + 64: c * 132 + 128])
        p.vcopy(t["ksc"][0:64, c * 2: c * 2 + 1],
                t["kv"][0:64, c * 132 + 128: c * 132 + 129])
        p.vcopy(t["ksc"][64:128, c * 2 + 1: c * 2 + 2],
                t["kv"][64:128, c * 132 + 128: c * 132 + 129])
    ve_bd = p.c["ve"]

    p.w("G", "pe", p.c["pe"])
    p.dma("G", t["WA"][:, :].rearrange("p (i o) -> p i o", o=D),
          dr["wq"].ap().rearrange("(i p) o -> p i o", p=128), "dwA")
    st["wA_val"] = p.c["dwA"]
    p.dma("G", t["WB"][:, :].rearrange("p (i o) -> p i o", o=D),
          dr["wo"].ap().rearrange("(i p) o -> p i o", p=128), "dwB")
    st["wB_val"] = p.c["dwB"]


    def p2_partA(tile):
        j = seq_tiles + tile
        b = j % 2
        xb = xbufs[b]
        xf = [xb[:, ic * TT:(ic + 1) * TT] for ic in range(NCH)]
        p.w("S", dx[b], st["x_val"][j])
        p.w("V", dx[b], st["x_val"][j])
        xal = [t["xa"][:, ic * TT:(ic + 1) * TT] for ic in range(NCH)]
        bm, bs = 6, 7
        bank_wait(bm)
        bank_wait(bs)
        p.mm(ps[bm][:, :], t["onesrow"][:, :],
             t["r_save"][:, tile * TT:(tile + 1) * TT], start=True, stop=True,
             sem="pe")
        p.mm(ps[bs][:, :], t["onesrow"][:, :],
             t["mr_save"][:, tile * TT:(tile + 1) * TT], start=True, stop=True,
             sem="pe")
        pe_bc1 = p.c["pe"]
        p.w("V", "pe", pe_bc1)
        apply_ck = []
        for ic in range(NCH):
            p.tt(t["ltmp"][:, :], xf[ic], ps[bm][:, :], ALU.mult)
            p.tt(xal[ic], t["ltmp"][:, :], ps[bs][:, :], ALU.subtract)
            apply_ck.append(("ve", p.c["ve"]))
        st["ltmp_ve"] = p.c["ve"]
        bank_consumed(bm)
        bank_consumed(bs)
        ve_xa = p.c["ve"]

        p.w("T", "dwA", st["wA_val"])
        weave(p, bank_wait, 0, apply_ck, per=4)
        p.w("T", "ve", ve_xa)
        kql = [t["kq"][:, oc * TT:(oc + 1) * TT] for oc in range(NCH)]
        for oc in range(NCH):
            bA, bB = (0, 1) if oc % 2 == 0 else (2, 3)
            bank_wait(bA)
            bank_wait(bB)
            for ic in range(NCH):
                p.mm2(ps[bA][:, :], ps[bB][:, :],
                      t["WA"][:, ic * D + oc * 128: ic * D + (oc + 1) * 128],
                      xal[ic], start=(ic == 0), stop=(ic == NCH - 1),
                      sem="pe" if ic == NCH - 1 else None)
            pe_q = p.c["pe"]
            p.w("S", "pe", pe_q)
            p.w("S", "ve", st["l3_ve"])
            p.act(t["ltmp3"][:, :], ps[bA][:, :], AF.Identity)
            bank_consumed(bA, "se")
            se_i = p.c["se"]
            p.w("V", "se", se_i)
            p.tt(kql[oc], t["ltmp3"][:, :], ps[bB][:, :], ALU.add)
            st["l3_ve"] = p.c["ve"]
            bank_consumed(bB)
        return {"xf": xf, "b": b, "j": j, "tile": tile, "ve_q": p.c["ve"],
                "kql": kql}

    def p2_partB(pa):
        tile = pa["tile"]
        j = pa["j"]
        b = pa["b"]
        xf = pa["xf"]
        kql = pa["kql"]
        p.w("T", "ve", pa["ve_q"])

        qfl = [t["qf"][:, c * TT:(c + 1) * TT] for c in range(NCH)]
        qf_se = [0] * NCH
        for c in range(NCH):
            bk = 4 + c % 2
            bank_wait(bk)
            p.mm(ps[bk][:, :], t["wfbd"][:, :], kql[c], start=True, stop=True,
                 sem="pe")
            pe_qf = p.c["pe"]
            p.w("S", "pe", pe_qf)
            p.act(qfl[c], ps[bk][:, :], AF.Relu, bias=t["bf2"][:, 0:1])
            qf_se[c] = p.c["se"]
            bank_consumed(bk, "se")

        if tile == 0:
            p.w("T", "ve", ve_bd)
        rcp2 = [t["rcp_bf"], t["rcp_bfb"]]
        rcp_done = [0] * NCH
        bcast_pe = [0] * NCH
        yl = [t["y"][:, c * TT:(c + 1) * TT] for c in range(NCH)]
        for c in range(NCH):
            bk = 4 + c % 2
            bo_ = 6 + c % 2
            bank_wait(bk)
            p.w("T", "se", qf_se[c])
            p.mm(ps[bk][0:2, :], t["ksc"][:, c * 2:(c + 1) * 2], qfl[c],
                 start=True, stop=True, sem="pe")
            pe_n = p.c["pe"]
            p.w("V", "pe", pe_n)
            p.ts(t["normsb"][:, :], ps[bk][0:2, :], 1e-6, ALU.max)
            bank_consumed(bk)
            p.vrecip(t["normsb"][:, :], t["normsb"][:, :])
            if c >= 2:
                p.w("V", "pe", bcast_pe[c - 2])
            p.vcopy(rcp2[c % 2][:, :], t["normsb"][:, :])
            rcp_done[c] = p.c["ve"]
            bank_wait(bk)
            p.w("T", "ve", rcp_done[c])
            p.mm(ps[bk][:, :], t["pat2"][:, :], rcp2[c % 2][:, :],
                 start=True, stop=True, sem="pe")
            bank_wait(bo_)
            p.mm(ps[bo_][:, :], t["BD"][:, c * 128:(c + 1) * 128], qfl[c],
                 start=True, stop=True, sem="pe")
            bcast_pe[c] = p.c["pe"]
            pe_o2 = p.c["pe"]
            p.w("V", "pe", pe_o2)
            p.vcopy(t["rbc"][:, :], ps[bk][:, :])
            bank_consumed(bk)
            p.tt(yl[c], ps[bo_][:, :], t["rbc"][:, :], ALU.mult)
            bank_consumed(bo_)
        ve_y = p.c["ve"]

        p.w("T", "dwB", st["wB_val"])
        p.w("T", "ve", ve_y)
        zl = [t["z"][:, oc * TT:(oc + 1) * TT] for oc in range(NCH)]
        p.w("S", "pe", st["sq_last_pe"])
        zsq_ck = []
        for oc in range(NCH):
            bA, bB = (0, 1) if oc % 2 == 0 else (2, 3)
            bank_wait(bA)
            bank_wait(bB)
            for ic in range(NCH):
                p.mm2(ps[bA][:, :], ps[bB][:, :],
                      t["WB"][:, ic * D + oc * 128: ic * D + (oc + 1) * 128],
                      yl[ic], start=(ic == 0), stop=(ic == NCH - 1),
                      sem="pe" if ic == NCH - 1 else None)
            pe_z = p.c["pe"]
            p.w("S", "pe", pe_z)
            p.w("S", "ve", st["l3_ve"])
            p.act(t["ltmp3"][:, :], ps[bA][:, :], AF.Identity,
                  bias=t["bo_t"][:, oc:oc + 1])
            bank_consumed(bA, "se")
            se_i = p.c["se"]
            p.w("V", "se", se_i)
            p.tt(zl[oc], t["ltmp3"][:, :], ps[bB][:, :], ALU.add)
            st["l3_ve"] = p.c["ve"]
            bank_consumed(bB)
            ve_z = p.c["ve"]
            p.w("S", "ve", ve_z)
            p.act(t["sq"][:, oc * TT:(oc + 1) * TT], zl[oc], AF.Square)
            zsq_ck.append(("se", p.c["se"]))

        nxt = p2_partA(tile + 1) if tile + 1 < seq_tiles else None

        p.w("V", "dsp", p.c["dsp"])
        xpl = [t["xp"][:, ic * TT:(ic + 1) * TT] for ic in range(NCH)]
        sn2 = ln_fm(p, st, bank_wait, bank_consumed, None, zl, xpl, (4, 5),
                    x_add=xf, src_ve=p.c["ve"], sq_ck=zsq_ck, warm_bank=0)
        ve_xp = p.c["ve"]
        gp_xp = sn2["gp_end"]
        st["xbuf_free"][b] = [("ve", ve_xp), ("se", p.c["se"]), ("gp", gp_xp)]
        load_entry(j + 2)

        p.w("Y", "ve", ve_xp)
        p.w("Y", "gp", gp_xp)
        p.dma("Y", dr["xp_hbm"][s, tile], t["xp"][:, :], "dsp")
        return nxt

    pa = p2_partA(0)
    for _tile in range(seq_tiles):
        pa = p2_partB(pa)

    _barrier(p, engines="TVS", dma=("dsp",))
    for w in range(nwin):
        emit_ffn_window(p, dr, st, s, w, seq_tiles, win_tiles, bank_wait,
                        bank_consumed, out_dmas)


def ffn_slice_dma(p, st, dr, which, idx):
    k = st["ffn_dma"]
    buf = p.t[("wc0", "wc1", "wc2")[k % 3]]
    sem = ("dw0", "dw1", "dw2")[k % 3]
    if k >= 3:
        done = st["wslice_done"].get(k - 3)
        if done is not None:
            p.w("G", "pe", done)
    p.dma("G", buf[:, :], dr[which][idx], sem)
    st["ffn_dma"] += 1


def ffn_slice_use(p, st):
    k = st["ffn_slice"]
    buf = p.t[("wc0", "wc1", "wc2")[k % 3]]
    sem = ("dw0", "dw1", "dw2")[k % 3]
    p.w("T", sem, 16 * (k // 3 + 1))
    st["ffn_slice"] += 1
    return buf, k


def emit_ffn_window(p, dr, st, s, w, seq_tiles, win_tiles, bank_wait, bank_consumed,
                    out_dmas):
    t = p.t
    ps = t["ps"]
    seq = seq_tiles * TT
    W = win_tiles * TT
    NH = win_tiles

    al = lambda oc, h: t["a"][:, oc * W + h * TT: oc * W + (h + 1) * TT]
    bl = lambda oc, h: t["b"][:, oc * W + h * TT: oc * W + (h + 1) * TT]
    xn_rhs = lambda ic, h: t["xnw"][:, h * 4096 + ic * TT: h * 4096 + (ic + 1) * TT]

    _barrier(p, engines="VS", dma=("dout", "dxr"))
    p.w("Y", "pe", p.c["pe"])
    p.w("Y", "ve", p.c["ve"])
    p.w("Y", "se", p.c["se"])
    p.w("Y", "dsp", p.c["dsp"])
    for ti in range(win_tiles):
        p.dma("Y", t["xpr"][:, ti * 4096:(ti + 1) * 4096],
              dr["xp_hbm"][s, w * win_tiles + ti], "dxr")
    dxr0 = p.c["dxr"]

    for ti in range(win_tiles):
        xpl3 = [t["xpr"][:, ti * 4096 + ic * TT: ti * 4096 + (ic + 1) * TT]
                for ic in range(NCH)]
        p.w("S", "dxr", dxr0)
        p.w("S", "pe", st["sq_last_pe"])
        sq3_ck = []
        x1bf = [t["a"][:, ic * TT:(ic + 1) * TT] for ic in range(NCH)]
        for ic in range(NCH):
            p.act(x1bf[ic], xpl3[ic], AF.Copy)
            p.act(t["sq"][:, ic * TT:(ic + 1) * TT], xpl3[ic], AF.Square)
            sq3_ck.append(("se", p.c["se"]))
        xnl = [t["xnw"][:, ti * 4096 + ic * TT: ti * 4096 + (ic + 1) * TT]
               for ic in range(NCH)]
        ln_fm(p, st, bank_wait, bank_consumed, xpl3, x1bf, xnl, (2, 3),
              sq_ck=sq3_ck, warm_bank=0)
    ve_xnw = p.c["ve"]
    p.w("T", "ve", ve_xnw)

    for slab in range(NCH):
        ffn_slice_dma(p, st, dr, "w1_t", slab)
        ffn_slice_dma(p, st, dr, "w2_t", slab)
        b1, k1 = ffn_slice_use(p, st)
        b2, k2 = ffn_slice_use(p, st)
        for ocl in range(4):
            oc = slab * 4 + ocl
            for h in range(NH):
                w1A, w1B = (0 + 2 * h, 1 + 2 * h)
                w2A, w2B = (4 + 2 * h, 5 + 2 * h)
                bank_wait(w1A)
                bank_wait(w1B)
                for ic in range(NCH):
                    p.mm2(ps[w1A][:, :], ps[w1B][:, :],
                          b1[:, ic * 512 + ocl * 128: ic * 512 + ocl * 128 + 128],
                          xn_rhs(ic, h), start=(ic == 0), stop=(ic == NCH - 1),
                          sem="pe" if ic == NCH - 1 else None)
                pe_w1 = p.c["pe"]
                bank_wait(w2A)
                bank_wait(w2B)
                for ic in range(NCH):
                    p.mm2(ps[w2A][:, :], ps[w2B][:, :],
                          b2[:, ic * 512 + ocl * 128: ic * 512 + ocl * 128 + 128],
                          xn_rhs(ic, h), start=(ic == 0), stop=(ic == NCH - 1),
                          sem="pe" if ic == NCH - 1 else None)
                pe_w2 = p.c["pe"]
                lt0, lt2, lt3 = (("ltmp", "ltmp2", "ltmp3") if h == 0 else
                                 ("ltmpB", "ltmp2B", "ltmp3B"))
                lkey = "ltmp_ve" if h == 0 else "ltmp_veB"
                p.w("S", "pe", pe_w1)
                p.w("S", "ve", st[lkey])
                p.act(t[lt2][:, :], ps[w1A][:, :], AF.Identity)
                bank_consumed(w1A, "se")
                se_i1 = p.c["se"]
                p.w("V", "se", se_i1)
                p.tt(t[lt3][:, :], t[lt2][:, :], ps[w1B][:, :], ALU.add)
                bank_consumed(w1B)
                ve_s1 = p.c["ve"]
                p.w("S", "ve", ve_s1)
                p.act(t[lt0][:, :], t[lt3][:, :], AF.Silu,
                      scale=t["s1"][:, oc:oc + 1])
                se_t1 = p.c["se"]
                p.w("S", "pe", pe_w2)
                p.act(t[lt2][:, :], ps[w2A][:, :], AF.Identity)
                bank_consumed(w2A, "se")
                se_i2 = p.c["se"]
                p.w("V", "pe", pe_w2)
                p.w("V", "se", se_i2)
                p.tt(t[lt3][:, :], t[lt2][:, :], ps[w2B][:, :], ALU.add)
                bank_consumed(w2B)
                p.stt(al(oc, h), t[lt3][:, :], t["s2"][:, oc:oc + 1],
                      t[lt0][:, :], ALU.mult, ALU.mult)
                st[lkey] = p.c["ve"]
        st["wslice_done"][k1] = p.c["pe"]
        st["wslice_done"][k2] = p.c["pe"]

    ve_a = p.c["ve"]
    for wt, sc_t, srcf, dstf in (("h1_t", "sh1", al, bl), ("h2_t", "sh2", bl, al)):
        if wt == "h1_t":
            p.w("T", "ve", ve_a)
        else:
            p.w("T", "se", p.c["se"])
        for oc in range(NHC):
            ffn_slice_dma(p, st, dr, wt, oc)
            wb, k = ffn_slice_use(p, st)
            for h in range(NH):
                bA = 4 * (oc % 2) + 2 * h
                bB = bA + 1
                bank_wait(bA)
                bank_wait(bB)
                for ic in range(NHC):
                    p.mm2(ps[bA][:, :], ps[bB][:, :], wb[:, ic * 128:(ic + 1) * 128],
                          srcf(ic, h), start=(ic == 0), stop=(ic == NHC - 1),
                          sem="pe" if ic == NHC - 1 else None)
                pe_h = p.c["pe"]
                lt0, lt2 = ("ltmp", "ltmp2") if h == 0 else ("ltmpB", "ltmp2B")
                lkey = "ltmp_ve" if h == 0 else "ltmp_veB"
                p.w("S", "pe", pe_h)
                p.w("S", "ve", st[lkey])
                p.act(t[lt0][:, :], ps[bA][:, :], AF.Identity)
                bank_consumed(bA, "se")
                se_i = p.c["se"]
                p.w("V", "se", se_i)
                p.tt(t[lt2][:, :], t[lt0][:, :], ps[bB][:, :], ALU.add)
                bank_consumed(bB)
                st[lkey] = p.c["ve"]
                p.w("S", "ve", st[lkey])
                p.act(dstf(oc, h), t[lt2][:, :], AF.Gelu,
                      scale=t[sc_t][:, oc:oc + 1])
            st["wslice_done"][k] = p.c["pe"]

    p.w("Y", "pe", p.c["pe"])
    p.w("Y", "dsp", p.c["dsp"])
    for ti in range(win_tiles):
        p.dma("Y", t["xpr"][:, ti * 4096:(ti + 1) * 4096],
              dr["xp_hbm"][s, w * win_tiles + ti], "dxr")
    dxr_val = p.c["dxr"]
    fl = lambda oc, h: t["f"][:, oc * W + h * TT: oc * W + (h + 1) * TT]
    p.w("T", "se", p.c["se"])
    for oc in range(NCH):
        ffn_slice_dma(p, st, dr, "w3_t", oc)
        wb, k = ffn_slice_use(p, st)
        for h in range(NH):
            bA = 4 * (oc % 2) + 2 * h
            bB = bA + 1
            bank_wait(bA)
            bank_wait(bB)
            for ic in range(NHC):
                p.mm2(ps[bA][:, :], ps[bB][:, :], wb[:, ic * 128:(ic + 1) * 128],
                      al(ic, h), start=(ic == 0), stop=(ic == NHC - 1),
                      sem="pe" if ic == NHC - 1 else None)
            pe_f = p.c["pe"]
            lt0 = "ltmp" if h == 0 else "ltmpB"
            lkey = "ltmp_ve" if h == 0 else "ltmp_veB"
            p.w("S", "pe", pe_f)
            p.w("S", "ve", st[lkey])
            p.act(t[lt0][:, :], ps[bA][:, :], AF.Identity,
                  scale=t["s3"][:, oc:oc + 1])
            bank_consumed(bA, "se")
            se_i = p.c["se"]
            p.w("V", "se", se_i)
            p.stt(fl(oc, h), ps[bB][:, :], t["s3"][:, oc:oc + 1], t[lt0][:, :],
                  ALU.mult, ALU.add)
            bank_consumed(bB)
            st[lkey] = p.c["ve"]
        st["wslice_done"][k] = p.c["pe"]

    p.w("V", "dxr", dxr_val)
    for h in range(NH):
        fbf = [fl(ic, h) for ic in range(NCH)]
        p.w("S", "pe", st["sq_last_pe"])
        p.w("S", "ve", st["ltmp_ve"])
        p.w("S", "ve", st["ltmp_veB"])
        sq4_ck = []
        for ic in range(NCH):
            p.act(t["sq"][:, ic * TT:(ic + 1) * TT], fbf[ic], AF.Square)
            sq4_ck.append(("se", p.c["se"]))
        xprl = [t["xpr"][:, h * 4096 + ic * TT: h * 4096 + (ic + 1) * TT]
                for ic in range(NCH)]
        outl = [t["outt"][:, ic * TT:(ic + 1) * TT] for ic in range(NCH)]
        p.w("V", "dout", p.c["dout"])
        ln_fm(p, st, bank_wait, bank_consumed, None, fbf, outl, (0, 1), x_add=xprl,
              src_ve=p.c["ve"], sq_ck=sq4_ck, warm_bank=4)
        ve_out = p.c["ve"]
        p.w("Y", "ve", ve_out)
        p.w("Y", "gp", p.c["gp"])
        tglob = w * win_tiles + h
        p.dma("Y",
              dr["out"][s][:, :, tglob * TT:(tglob + 1) * TT].rearrange("i p tt -> p i tt"),
              t["outt"][:, :].rearrange("p (i tt) -> p i tt", tt=TT), "dout")
        out_dmas[0] += 1



def _prep_inputs(L0_attn_x, L0_attn_Wq, L0_attn_Wk, L0_attn_Wv, L0_attn_Wo,
                 L0_attn_bo, L0_attn_Wf, L0_attn_bf,
                 L0_ffn_w1_w, L0_ffn_w1_s, L0_ffn_w2_w, L0_ffn_w2_s,
                 L0_ffn_h1_w, L0_ffn_h1_s, L0_ffn_h2_w, L0_ffn_h2_s,
                 L0_ffn_w3_w, L0_ffn_w3_s):
    f32 = np.float32
    asc = np.ascontiguousarray

    def tile_oi(wT, n_in_ch, n_out_ch, ocp):
        r = wT.reshape(n_in_ch, 128, n_out_ch, ocp).transpose(2, 1, 0, 3)
        return asc(r.reshape(n_out_ch, 128, n_in_ch * ocp).astype(f32))

    sh = {}
    for n, wmat in (("wq", L0_attn_Wq), ("wk", L0_attn_Wk), ("wv", L0_attn_Wv),
                    ("wo", L0_attn_Wo)):
        sh[n] = asc(np.asarray(wmat, dtype=f32))
    wf = np.asarray(L0_attn_Wf, dtype=f32)
    wfbd = np.zeros((128, 128), f32)
    wfbd[:64, :64] = wf
    wfbd[64:, 64:] = wf
    sh["wfbd"] = wfbd
    bfv = np.asarray(L0_attn_bf, dtype=f32)
    sh["bfbc"] = asc(np.broadcast_to(np.concatenate([bfv, bfv])[None, :],
                                     (128, 128)).astype(f32))
    sh["bf2"] = asc(np.concatenate([bfv, bfv])[:, None].astype(f32))
    sh["bo_t"] = asc(np.asarray(L0_attn_bo, dtype=f32).reshape(NCH, 128).T)
    w1T = np.asarray(L0_ffn_w1_w, dtype=f32).T
    w2T = np.asarray(L0_ffn_w2_w, dtype=f32).T
    h1T = np.asarray(L0_ffn_h1_w, dtype=f32).T
    h2T = np.asarray(L0_ffn_h2_w, dtype=f32).T
    w3T = np.asarray(L0_ffn_w3_w, dtype=f32).T
    sh["w1_t"] = tile_oi(w1T, 8, 8, 512)
    sh["w2_t"] = tile_oi(w2T, 8, 8, 512)
    sh["h1_t"] = tile_oi(h1T, 32, 32, 128)
    sh["h2_t"] = tile_oi(h2T, 32, 32, 128)
    sh["w3_t"] = tile_oi(w3T, 32, 8, 128)
    for n, v in (("s1_t", L0_ffn_w1_s), ("s2_t", L0_ffn_w2_s),
                 ("sh1_t", L0_ffn_h1_s), ("sh2_t", L0_ffn_h2_s)):
        sh[n] = asc(np.asarray(v, dtype=f32).reshape(-1, 128).T)
    sh["s3_t"] = asc(np.asarray(L0_ffn_w3_s, dtype=f32).reshape(NCH, 128).T)
    pat = np.zeros((2, 128), f32)
    pat[0, :64] = 1.0
    pat[1, 64:] = 1.0
    sh["pat2"] = pat
    sh["bfrow"] = asc(np.concatenate([bfv, bfv])[None, :].astype(f32))
    sh["warm512"] = np.ones((128, TT), f32)
    sh["onestat"] = np.full((128, 1), 1.0 / D, f32)
    sh["onesrow"] = np.ones((1, 128), f32)

    x = np.asarray(L0_attn_x, dtype=f32)
    percore = []
    for c in range(8):
        xc = x[c * 2:(c + 1) * 2]
        xt = xc.transpose(0, 2, 1).reshape(2, NCH, 128, xc.shape[1])
        percore.append(asc(xt.astype(f32)))
    return sh, percore


_NC_CACHE = {}


def kernel(**inputs):
    sh, percore = _prep_inputs(**inputs)
    if "full" not in _NC_CACHE:
        _NC_CACHE["full"] = build_program()
    nc = _NC_CACHE["full"]
    in_maps = []
    for c in range(8):
        m = dict(sh)
        m["x"] = percore[c]
        in_maps.append(m)
    res = run_bass_kernel_spmd(nc, in_maps, core_ids=list(range(8)))
    outs = []
    for c in range(8):
        o = res.results[c]["out"]
        o = o.reshape(2, D, -1).transpose(0, 2, 1)
        outs.append(o)
    return np.ascontiguousarray(np.concatenate(outs, axis=0)).astype(np.float32)


# version 13 (final)
# speedup vs baseline: 1.1364x; 1.0312x over previous
import contextlib
import os as _os
import numpy as np
import concourse.bass as bass
import concourse.mybir as mybir
from concourse.bass_utils import run_bass_kernel_spmd

F32 = mybir.dt.float32
BF16 = mybir.dt.bfloat16
U8 = mybir.dt.uint8
AF = mybir.ActivationFunctionType
ALU = mybir.AluOpType

D = 1024
NCH = 8
HID = 4096
NHC = 32
TT = 512
NSEQ = 2
SEQ_TILES = 4
WIN_TILES = 2
EPS = 1e-5

KB = 1024


class P:

    def __init__(self, nc, ename, eng, t, sems):
        self.nc = nc
        self.E = ename
        self.h = eng
        self.t = t
        self.S = sems
        self.c = {k: 0 for k in sems}

    def bump(self, sem, n):
        self.c[sem] += n
        return self.c[sem]

    def _inc(self, ins, sem, n):
        if sem is None:
            return
        self.bump(sem, n)
        if ins is not None:
            ins.then_inc(self.S[sem], n)

    def w(self, eng, sem, val):
        if val <= 0:
            return
        if self.E == eng:
            self.h.wait_ge(self.S[sem], val)

    def mm(self, out, lhsT, rhs, start, stop, sem=None):
        ins = None
        if self.E == "T":
            ins = self.h.matmul(out, lhsT, rhs, start=start, stop=stop)
        self._inc(ins, sem, 1)

    def gtt(self, out, in0, in1, op, sem="gp"):
        ins = None
        if self.E == "G":
            if self.c["gp"] > 0:
                self.h.wait_ge(self.S["gp"], self.c["gp"])
            ins = self.h.tensor_tensor(out, in0, in1, op)
        self._inc(ins, sem, 1)

    def gw(self, sem, val):
        self.w("G", sem, val)

    def mmwarm(self, out, lhsT, rhs):
        if self.E == "T":
            self.h.matmul(out, lhsT, rhs, start=True, stop=True)

    def mm2(self, outA, outB, lhsT, rhs, start, stop, sem=None):
        ins = None
        if self.E == "T":
            self.h.matmul(outA, lhsT[0:64, :], rhs[0:64, :], start=start,
                          stop=stop, tile_position=(0, 0))
            ins = self.h.matmul(outB, lhsT[64:128, :], rhs[64:128, :], start=start,
                                stop=stop, tile_position=(64, 0))
        self._inc(ins, sem, 1)

    def act(self, out, in_, func, bias=0.0, scale=1.0, sem="se"):
        ins = None
        if self.E == "S":
            if self.c["se"] > 0:
                self.h.wait_ge(self.S["se"], self.c["se"])
            ins = self.h.activation(out, in_, func, bias=bias, scale=scale)
        self._inc(ins, sem, 1)

    def _vsync(self):
        if self.E == "V" and self.c["ve"] > 0:
            self.h.wait_ge(self.S["ve"], self.c["ve"])

    def tt(self, out, in0, in1, op, sem="ve"):
        ins = None
        if self.E == "V":
            self._vsync()
            ins = self.h.tensor_tensor(out, in0, in1, op)
        self._inc(ins, sem, 1)

    def ts(self, out, in0, s1, op, sem="ve"):
        ins = None
        if self.E == "V":
            self._vsync()
            ins = self.h.tensor_single_scalar(out, in0, s1, op)
        self._inc(ins, sem, 1)

    def stt(self, out, in0, scal, in1, op0, op1, sem="ve"):
        ins = None
        if self.E == "V":
            self._vsync()
            ins = self.h.scalar_tensor_tensor(out, in0, scal, in1, op0, op1)
        self._inc(ins, sem, 1)

    def vcopy(self, out, in_, sem="ve"):
        ins = None
        if self.E == "V":
            self._vsync()
            ins = self.h.tensor_copy(out, in_)
        self._inc(ins, sem, 1)

    def vrecip(self, out, in_, sem="ve"):
        ins = None
        if self.E == "V":
            self._vsync()
            ins = self.h.reciprocal(out, in_)
        self._inc(ins, sem, 1)

    def vmemset(self, ap, val, sem="ve"):
        ins = None
        if self.E == "V":
            self._vsync()
            ins = self.h.memset(ap, val)
        self._inc(ins, sem, 1)

    def dma(self, q, out, in_, sem):
        ins = None
        if self.E == q:
            ins = self.h.dma_start(out=out, in_=in_)
        self._inc(ins, sem, 16)


def _alloc_views(nc, arena, layout):
    base = nc.lookup_mloc(arena).addr
    out = {}
    for name, shape, dtype, off in layout:
        out[name] = nc.alloc_sbuf_tensor_at(name, shape, dtype, offset=base + off)
    return out


def build_program(nseq=NSEQ, seq_tiles=SEQ_TILES, win_tiles=WIN_TILES):
    nc = bass.Bass("TRN2", target_bir_lowering=False)
    seq = seq_tiles * TT
    W = win_tiles * TT

    dr = {}
    dr["x"] = nc.dram_tensor("x", [nseq, NCH, 128, seq], F32, kind="ExternalInput")
    dr["out"] = nc.dram_tensor("out", [nseq, NCH, 128, seq], F32, kind="ExternalOutput")
    for n in ("wq", "wk", "wv", "wo"):
        dr[n] = nc.dram_tensor(n, [D, D], F32, kind="ExternalInput")
    for n, shp in (("wfbd", [128, 128]), ("bfbc", [128, 128]), ("bo_t", [128, NCH]),
                   ("bf2", [128, 1]),
                   ("w1_t", [NCH, 128, 4096]), ("w2_t", [NCH, 128, 4096]),
                   ("h1_t", [NHC, 128, 4096]), ("h2_t", [NHC, 128, 4096]),
                   ("w3_t", [NCH, 128, 4096]),
                   ("s1_t", [128, NHC]), ("s2_t", [128, NHC]),
                   ("sh1_t", [128, NHC]), ("sh2_t", [128, NHC]), ("s3_t", [128, NCH]),
                   ("pat2", [2, 128]), ("onestat", [128, 1]), ("onesrow", [1, 128]),
                   ("bfrow", [1, 128]), ("warm512", [128, TT])):
        dr[n] = nc.dram_tensor(n, shp, F32, kind="ExternalInput")
    dr["xp_hbm"] = nc.dram_tensor("xp_hbm", [nseq, seq_tiles, 128, 4096], F32,
                                  kind="Internal")

    ARENA = 206 * KB
    arena = nc.alloc_sbuf_tensor("arena", [128, ARENA], U8)
    SC = 3584
    B0 = SC + 49 * KB
    lay = [
        ("wfbd", [128, 128], BF16, 0),
        ("bfbc", [128, 128], BF16, 256),
        ("bo_t", [128, NCH], F32, 512),
        ("bf2", [128, 1], F32, 544),
        ("s1", [128, NHC], F32, 576),
        ("s2", [128, NHC], F32, 704),
        ("sh1", [128, NHC], F32, 832),
        ("sh2", [128, NHC], F32, 960),
        ("s3", [128, NCH], F32, 1088),
        ("onestat", [128, 1], BF16, 1120),
        ("onesrow", [1, 128], BF16, 1152),
        ("bfrow", [1, 128], BF16, 1408),
        ("warm512", [128, TT], BF16, 1920),
        ("pat2", [2, 128], BF16, 1664),
        ("ltmp", [128, TT], F32, SC),
        ("ltmp2", [128, TT], F32, SC + 2 * KB),
        ("msb", [1, TT], F32, SC + 4 * KB),
        ("vsb", [1, TT], F32, SC + 6 * KB),
        ("r_bf", [1, TT], BF16, SC + 8 * KB),
        ("mr_bf", [1, TT], BF16, SC + 9 * KB),
        ("normsb", [2, TT], F32, SC + 10 * KB),
        ("rcp_bf", [2, TT], BF16, SC + 12 * KB),
        ("rcp_bfb", [2, TT], BF16, SC + 47 * KB),
        ("r_save", [1, 4 * TT], BF16, SC + 13 * KB),
        ("mr_save", [1, 4 * TT], BF16, SC + 17 * KB),
        ("sq", [128, 4096], BF16, SC + 21 * KB),
        ("wc0", [128, 4096], BF16, SC + 29 * KB),
        ("wc1", [128, 4096], BF16, SC + 37 * KB),
        ("ltmp3", [128, TT], F32, SC + 45 * KB),
        ("ltmpB", [128, TT], F32, B0 + 138 * KB),
        ("ltmp2B", [128, TT], F32, B0 + 140 * KB),
        ("ltmp3B", [128, TT], F32, B0 + 142 * KB),
        ("wc2", [128, 4096], BF16, B0 + 144 * KB),

        ("xb0", [128, 4096], F32, B0 + 0),
        ("xb1", [128, 4096], F32, B0 + 16 * KB),
        ("xbf", [128, 4096], BF16, B0 + 32 * KB),
        ("WA", [128, NCH * D], BF16, B0 + 40 * KB),
        ("WB", [128, NCH * D], BF16, B0 + 56 * KB),
        ("xa", [128, 4096], BF16, B0 + 72 * KB),
        ("kq", [128, 4096], BF16, B0 + 80 * KB),
        ("vtm", [128, 32 * 132], BF16, B0 + 88 * KB),
        ("qf", [128, 4096], BF16, B0 + 88 * KB),
        ("kftm", [128, 4096], BF16, B0 + 97 * KB),
        ("y", [128, 4096], BF16, B0 + 97 * KB),
        ("kv", [128, NCH * 132], F32, B0 + 105 * KB),
        ("BD", [128, NCH * 128], BF16, B0 + 110 * KB),
        ("ksc", [128, NCH * 2], BF16, B0 + 112 * KB),
        ("rbc", [128, TT], BF16, B0 + 113 * KB),
        ("z", [128, 4096], BF16, B0 + 114 * KB),
        ("xp", [128, 4096], F32, B0 + 122 * KB),
        ("a", [128, NHC * W], BF16, B0 + 0),
        ("b", [128, NHC * W], BF16, B0 + NHC * W * 2),
        ("xnw", [128, win_tiles * 4096], BF16,
         B0 + NHC * W * 2 + win_tiles * 16 * KB),
        ("xpr", [128, win_tiles * 4096], F32, B0 + NHC * W * 2),
        ("f", [128, NCH * W], BF16, B0 + NHC * W * 2 + win_tiles * 16 * KB),
        ("outt", [128, 4096], F32, B0 + NHC * W * 2 + win_tiles * 16 * KB + NCH * W * 2),
    ]
    need = max(B0 + 2 * NHC * W * 2, B0 + 152 * KB)
    assert need <= ARENA, ("big region overflow", need, ARENA)
    t = _alloc_views(nc, arena, lay)
    t["arena"] = arena
    t["ps"] = [nc.alloc_psum_tensor(f"psb{i}", [128, 512], F32) for i in range(8)]

    sem_names = ["pe", "ve", "se", "gp", "dcon", "dwA", "dwB", "dbf",
                 "dxa", "dxb", "dw0", "dw1", "dw2", "dsp", "dxr", "dout"]
    stack = contextlib.ExitStack()
    sems = {n: stack.enter_context(nc.semaphore(n)) for n in sem_names}

    counters = []
    with nc.Block() as block:
        def run(ename):
            def f(eng):
                p = P(nc, ename, eng, t, sems)
                emit_all(p, dr, nseq, seq_tiles, win_tiles)
                counters.append(dict(p.c))
            return f
        block.tensor(run("T"))
        block.vector(run("V"))
        block.scalar(run("S"))
        block.gpsimd(run("G"))
        block.sync(run("Y"))
    stack.close()
    for c in counters[1:]:
        assert c == counters[0], ("counter mismatch", counters)
    return nc



def _barrier(p, engines="TVSY", dma=()):
    cur = {k: p.c[k] for k in ("pe", "ve", "se", "gp")}
    own = {"T": "pe", "V": "ve", "S": "se", "G": "gp"}
    for e in engines:
        for sem, v in cur.items():
            if own.get(e) != sem:
                p.w(e, sem, v)
        for dsem in dma:
            p.w(e, dsem, p.c[dsem])


def emit_all(p, dr, nseq, seq_tiles, win_tiles):
    t = p.t

    st = {
        "bank_free": [[] for _ in range(8)],
        "xbuf_free": [[], []],
        "xbf_free": [],
        "x_val": {},
        "xbf_val": {},
        "wslice_done": {},
        "ffn_slice": 0,
        "ffn_dma": 0,
        "sq_last_pe": 0,
        "l3_ve": 0,
        "l3_gp": 0,
        "ltmp_ve": 0,
        "ltmp_veB": 0,
        "wA_val": 0, "wB_val": 0,
        "dsp_tile": {},
    }

    def bank_wait(b):
        for semv in st["bank_free"][b]:
            p.w("T", *semv)
        st["bank_free"][b] = []

    def bank_consumed(b, sem="ve"):
        st["bank_free"][b].append((sem, p.c[sem]))

    for n, src in (("wfbd", "wfbd"), ("bfbc", "bfbc"), ("bo_t", "bo_t"), ("bf2", "bf2"),
                   ("s1", "s1_t"), ("s2", "s2_t"), ("sh1", "sh1_t"), ("sh2", "sh2_t"),
                   ("s3", "s3_t"), ("onestat", "onestat"), ("onesrow", "onesrow"),
                   ("pat2", "pat2"), ("bfrow", "bfrow"), ("warm512", "warm512")):
        p.dma("G", t[n][:, :], dr[src][:, :], "dcon")
    NCONST = p.c["dcon"]
    for e in "TVSY":
        p.w(e, "dcon", NCONST)
    p.w("G", "dcon", NCONST)

    out_dmas = [0]
    for s in range(nseq):
        emit_seq(p, dr, st, s, seq_tiles, win_tiles, bank_wait, bank_consumed, out_dmas)

    p.w("Y", "dout", p.c["dout"])


_WARM_PER = int(_os.environ.get("MK_WARM_PER", "0"))


def weave(p, bank_wait, wb, checkpoints, per=3):
    if _WARM_PER >= 0:
        per = _WARM_PER
    if per == 0:
        return
    """Keep the PE clock warm while tracking a cross-engine chain: step through
    semaphore checkpoints, emitting a few dummy matmuls between each so PE-idle
    stretches stay under the HAM re-throttle window."""
    t = p.t
    bank_wait(wb)
    for sem, v in checkpoints:
        p.w("T", sem, v)
        for _ in range(per):
            p.mmwarm(t["ps"][wb][0:1, :], t["onestat"][:, 0:1], t["warm512"][:, :])


def ln_fm(p, st, bank_wait, bank_consumed, src_f32, src_bf, dst, stat_banks,
          x_add=None, src_ve=0, sq_ck=None, warm_bank=None, save_to=None):
    t = p.t
    ps = t["ps"]
    bm, bs = stat_banks
    se_sq = p.c["se"]
    bank_wait(bm)
    bank_wait(bs)
    if warm_bank is not None and sq_ck:
        weave(p, bank_wait, warm_bank, sq_ck, per=3)
    p.w("T", "se", se_sq)
    p.w("T", "ve", src_ve)
    for ic in range(NCH):
        p.mm(ps[bm][0:1, :], t["onestat"][:, 0:1], src_bf[ic],
             start=(ic == 0), stop=(ic == NCH - 1),
             sem="pe" if ic == NCH - 1 else None)
    for ic in range(NCH):
        p.mm(ps[bs][0:1, :], t["onestat"][:, 0:1], t["sq"][:, ic * TT:(ic + 1) * TT],
             start=(ic == 0), stop=(ic == NCH - 1),
             sem="pe" if ic == NCH - 1 else None)
    pe_stats = p.c["pe"]
    st["sq_last_pe"] = pe_stats
    p.w("V", "pe", pe_stats)
    p.vcopy(t["msb"][:, :], ps[bm][0:1, :])
    p.tt(t["vsb"][:, :], t["msb"][:, :], t["msb"][:, :], ALU.mult)
    p.stt(t["vsb"][:, :], ps[bs][0:1, :], float(EPS), t["vsb"][:, :],
          ALU.add, ALU.subtract)
    ve_v = p.c["ve"]
    p.w("S", "ve", ve_v)
    p.act(t["vsb"][:, :], t["vsb"][:, :], AF.Ln)
    p.act(t["vsb"][:, :], t["vsb"][:, :], AF.Exp, scale=-0.5)
    se_sqrt = p.c["se"]
    p.w("V", "se", se_sqrt)
    p.vcopy(t["r_bf"][:, :], t["vsb"][:, :])
    p.tt(t["msb"][:, :], t["msb"][:, :], t["vsb"][:, :], ALU.mult)
    p.vcopy(t["mr_bf"][:, :], t["msb"][:, :])
    if save_to is not None:
        p.vcopy(save_to[0], t["r_bf"][:, :])
        p.vcopy(save_to[1], t["mr_bf"][:, :])
    ve_r = p.c["ve"]
    if warm_bank is not None:
        ck = [("ve", ve_v), ("se", p.c["se"]), ("ve", ve_r)]
        weave(p, bank_wait, warm_bank, ck, per=4)
    p.w("T", "ve", ve_r)
    p.mm(ps[bm][:, :], t["onesrow"][:, :], t["r_bf"][:, :], start=True, stop=True,
         sem="pe")
    p.mm(ps[bs][:, :], t["onesrow"][:, :], t["mr_bf"][:, :], start=True, stop=True,
         sem="pe")
    pe_bc = p.c["pe"]
    p.w("V", "pe", pe_bc)
    apply_ck = []
    for ic in range(NCH):
        srcm = src_f32[ic] if src_f32 is not None else src_bf[ic]
        p.tt(t["ltmp"][:, :], srcm, ps[bm][:, :], ALU.mult)
        if x_add is None:
            p.tt(dst[ic], t["ltmp"][:, :], ps[bs][:, :], ALU.subtract)
        else:
            p.tt(t["ltmp2"][:, :], t["ltmp"][:, :], ps[bs][:, :], ALU.subtract)
            p.tt(dst[ic], t["ltmp2"][:, :], x_add[ic], ALU.add)
        apply_ck.append(("ve", p.c["ve"]))
    st["ltmp_ve"] = p.c["ve"]
    bank_consumed(bm)
    bank_consumed(bs)
    return {"pe_stats": pe_stats, "pe_bc": pe_bc, "ve_end": p.c["ve"],
            "gp_end": p.c["gp"], "apply_ck": apply_ck}


def emit_seq(p, dr, st, s, seq_tiles, win_tiles, bank_wait, bank_consumed, out_dmas):
    t = p.t
    ps = t["ps"]
    seq = seq_tiles * TT
    nwin = seq_tiles // win_tiles
    xbufs = [t["xb0"], t["xb1"]]
    dx = ["dxa", "dxb"]

    _barrier(p, engines="TVS", dma=("dsp", "dout"))
    pe_seq = p.c["pe"]
    ve_seq = p.c["ve"]
    se_seq = p.c["se"]
    p.w("Y", "pe", pe_seq)
    p.w("G", "pe", pe_seq)

    entries = [(1, i) for i in range(seq_tiles)] + [(2, i) for i in range(seq_tiles)]

    def load_entry(j):
        if j >= len(entries):
            return
        _, tile = entries[j]
        b = j % 2
        for semv in st["xbuf_free"][b]:
            p.w("Y", *semv)
        st["xbuf_free"][b] = []
        p.dma("Y", xbufs[b][:, :].rearrange("p (i tt) -> p i tt", tt=TT),
              dr["x"][s][:, :, tile * TT:(tile + 1) * TT].rearrange("i p tt -> p i tt"),
              dx[b])
        st["x_val"][j] = p.c[dx[b]]

    def load_xbf(j):
        if j >= len(entries):
            return
        _, tile = entries[j]
        for semv in st["xbf_free"]:
            p.w("G", *semv)
        st["xbf_free"] = []
        p.dma("G", t["xbf"][:, :].rearrange("p (i tt) -> p i tt", tt=TT),
              dr["x"][s][:, :, tile * TT:(tile + 1) * TT].rearrange("i p tt -> p i tt"),
              "dbf")
        st["xbf_val"][j] = p.c["dbf"]

    load_entry(0)
    load_entry(1)
    load_xbf(0)

    p.dma("G", t["WA"][:, :].rearrange("p (i o) -> p i o", o=D),
          dr["wk"].ap().rearrange("(i p) o -> p i o", p=128), "dwA")
    st["wA_val"] = p.c["dwA"]
    p.w("G", "ve", ve_seq)
    p.w("G", "se", se_seq)
    p.dma("G", t["WB"][:, :].rearrange("p (i o) -> p i o", o=D),
          dr["wv"].ap().rearrange("(i p) o -> p i o", p=128), "dwB")
    st["wB_val"] = p.c["dwB"]

    p.vmemset(t["kv"][:, :], 0.0)
    vtm_r = t["vtm"][:, :].rearrange("p (sl e) -> p sl e", e=132)
    p.vmemset(vtm_r[:, :, 128:129], 1.0)

    def p1_ln(tile):
        j = tile
        b = j % 2
        xb = xbufs[b]
        xf = [xb[:, ic * TT:(ic + 1) * TT] for ic in range(NCH)]
        xbfl = [t["xbf"][:, ic * TT:(ic + 1) * TT] for ic in range(NCH)]
        p.w("S", dx[b], st["x_val"][j])
        p.w("V", dx[b], st["x_val"][j])
        p.w("T", "dbf", st["xbf_val"][j])
        p.w("S", "pe", st["sq_last_pe"])
        sq_ck = []
        for ic in range(NCH):
            p.act(t["sq"][:, ic * TT:(ic + 1) * TT], xf[ic], AF.Square)
            sq_ck.append(("se", p.c["se"]))
        xal = [t["xa"][:, ic * TT:(ic + 1) * TT] for ic in range(NCH)]
        sn1 = ln_fm(p, st, bank_wait, bank_consumed, xf, xbfl, xal, (6, 7),
                    sq_ck=sq_ck, warm_bank=0,
                    save_to=(t["r_save"][:, tile * TT:(tile + 1) * TT],
                             t["mr_save"][:, tile * TT:(tile + 1) * TT]))
        st["xbf_free"] = [("pe", p.c["pe"])]
        if j + 1 < seq_tiles:
            load_xbf(j + 1)
        return {"tile": tile, "j": j, "b": b, "xal": xal, "ve_xa": p.c["ve"]}

    pl = p1_ln(0)
    for tile in range(seq_tiles):
        j = pl["j"]
        b = pl["b"]
        xal = pl["xal"]
        ve_xa = pl["ve_xa"]

        p.w("T", "dwA", st["wA_val"])
        p.w("T", "ve", ve_xa)
        kql = [t["kq"][:, oc * TT:(oc + 1) * TT] for oc in range(NCH)]
        for oc in range(NCH):
            bA, bB = (0, 1) if oc % 2 == 0 else (2, 3)
            bank_wait(bA)
            bank_wait(bB)
            for ic in range(NCH):
                p.mm2(ps[bA][:, :], ps[bB][:, :],
                      t["WA"][:, ic * D + oc * 128: ic * D + (oc + 1) * 128],
                      xal[ic], start=(ic == 0), stop=(ic == NCH - 1),
                      sem="pe" if ic == NCH - 1 else None)
            pe_k = p.c["pe"]
            p.w("S", "pe", pe_k)
            p.w("S", "ve", st["l3_ve"])
            p.w("S", "gp", st["l3_gp"])
            p.act(t["ltmp3"][:, :], ps[bA][:, :], AF.Identity)
            bank_consumed(bA, "se")
            se_i = p.c["se"]
            p.w("V", "se", se_i)
            p.tt(kql[oc], t["ltmp3"][:, :], ps[bB][:, :], ALU.add)
            st["l3_ve"] = p.c["ve"]
            bank_consumed(bB)
        se_k = p.c["se"]
        ve_k = p.c["ve"]
        p.w("T", "ve", ve_k)

        p.w("T", "dwB", st["wB_val"])
        for sub in range(4):
            pairs = ((0, 1), (2, 3)) if sub % 2 == 0 else ((4, 5), (6, 7))
            for vh in range(2):
                bA, bB = pairs[vh]
                bank_wait(bA)
                bank_wait(bB)
                for ic in range(NCH):
                    p.mm2(ps[bA][:, :], ps[bB][:, :],
                          xal[ic][:, sub * 128:(sub + 1) * 128],
                          t["WB"][:, ic * D + vh * 512: ic * D + (vh + 1) * 512],
                          start=(ic == 0), stop=(ic == NCH - 1),
                          sem="pe" if ic == NCH - 1 else None)
                pe_v = p.c["pe"]
                p.w("S", "pe", pe_v)
                p.w("S", "ve", st["l3_ve"])
                p.w("S", "gp", st["l3_gp"])
                p.act(t["ltmp3"][:, :], ps[bA][:, :], AF.Identity)
                bank_consumed(bA, "se")
                se_i = p.c["se"]
                p.w("V", "se", se_i)
                dstv = vtm_r[:, sub * 8 + vh * 4: sub * 8 + (vh + 1) * 4, 0:128]
                p.tt(dstv, t["ltmp3"][:, :].rearrange("p (c e) -> p c e", e=128),
                     ps[bB][:, :].rearrange("p (c e) -> p c e", e=128), ALU.add)
                st["l3_ve"] = p.c["ve"]
                bank_consumed(bB)

        pl = p1_ln(tile + 1) if tile + 1 < seq_tiles else None

        kf_se = [0] * NCH
        for c in range(NCH):
            bk = 4 + c % 2
            bank_wait(bk)
            p.w("T", "se", se_k)
            for sub in range(4):
                p.mm(ps[bk][:, sub * 128:(sub + 1) * 128],
                     kql[c][:, sub * 128:(sub + 1) * 128],
                     t["wfbd"][:, :], start=True, stop=False)
                p.mm(ps[bk][:, sub * 128:(sub + 1) * 128],
                     t["onesrow"][:, :], t["bfrow"][:, :], start=False, stop=True,
                     sem="pe" if sub == 3 else None)
            pe_kf = p.c["pe"]
            p.w("S", "pe", pe_kf)
            kf_dst = t["kftm"][:, :].rearrange("p (s e) -> p s e", e=128)[:, c:c + 25:8, :]
            p.act(kf_dst, ps[bk][:, :].rearrange("p (s e) -> p s e", e=128), AF.Relu)
            kf_se[c] = p.c["se"]
            bank_consumed(bk, "se")

        for c in range(NCH):
            bk = 2 + c % 2
            bank_wait(bk)
            p.w("T", "se", kf_se[c])
            for sub in range(4):
                p.mm(ps[bk][:, 0:129],
                     t["kftm"][:, (sub * 8 + c) * 128:(sub * 8 + c) * 128 + 128],
                     t["vtm"][:, (sub * 8 + c) * 132:(sub * 8 + c) * 132 + 129],
                     start=(sub == 0), stop=(sub == 3),
                     sem="pe" if sub == 3 else None)
            pe_kv = p.c["pe"]
            p.w("V", "pe", pe_kv)
            p.tt(t["kv"][:, c * 132: c * 132 + 129], t["kv"][:, c * 132: c * 132 + 129],
                 ps[bk][:, 0:129], ALU.add)
            bank_consumed(bk)
        st["xbuf_free"][b] = [("ve", p.c["ve"]), ("se", p.c["se"]),
                              ("gp", p.c["gp"])]
        load_entry(j + 2)

    p.vmemset(t["BD"][:, :], 0.0)
    p.vmemset(t["ksc"][:, :], 0.0)
    for c in range(NCH):
        p.vcopy(t["BD"][0:64, c * 128: c * 128 + 64],
                t["kv"][0:64, c * 132: c * 132 + 64])
        p.vcopy(t["BD"][64:128, c * 128 + 64: c * 128 + 128],
                t["kv"][64:128, c * 132 + 64: c * 132 + 128])
        p.vcopy(t["ksc"][0:64, c * 2: c * 2 + 1],
                t["kv"][0:64, c * 132 + 128: c * 132 + 129])
        p.vcopy(t["ksc"][64:128, c * 2 + 1: c * 2 + 2],
                t["kv"][64:128, c * 132 + 128: c * 132 + 129])
    ve_bd = p.c["ve"]

    p.w("G", "pe", p.c["pe"])
    p.dma("G", t["WA"][:, :].rearrange("p (i o) -> p i o", o=D),
          dr["wq"].ap().rearrange("(i p) o -> p i o", p=128), "dwA")
    st["wA_val"] = p.c["dwA"]
    p.dma("G", t["WB"][:, :].rearrange("p (i o) -> p i o", o=D),
          dr["wo"].ap().rearrange("(i p) o -> p i o", p=128), "dwB")
    st["wB_val"] = p.c["dwB"]


    def p2_partA(tile):
        j = seq_tiles + tile
        b = j % 2
        xb = xbufs[b]
        xf = [xb[:, ic * TT:(ic + 1) * TT] for ic in range(NCH)]
        p.w("S", dx[b], st["x_val"][j])
        p.w("V", dx[b], st["x_val"][j])
        xal = [t["xa"][:, ic * TT:(ic + 1) * TT] for ic in range(NCH)]
        bm, bs = 6, 7
        bank_wait(bm)
        bank_wait(bs)
        p.mm(ps[bm][:, :], t["onesrow"][:, :],
             t["r_save"][:, tile * TT:(tile + 1) * TT], start=True, stop=True,
             sem="pe")
        p.mm(ps[bs][:, :], t["onesrow"][:, :],
             t["mr_save"][:, tile * TT:(tile + 1) * TT], start=True, stop=True,
             sem="pe")
        pe_bc1 = p.c["pe"]
        p.w("V", "pe", pe_bc1)
        apply_ck = []
        for ic in range(NCH):
            p.tt(t["ltmp"][:, :], xf[ic], ps[bm][:, :], ALU.mult)
            p.tt(xal[ic], t["ltmp"][:, :], ps[bs][:, :], ALU.subtract)
            apply_ck.append(("ve", p.c["ve"]))
        st["ltmp_ve"] = p.c["ve"]
        bank_consumed(bm)
        bank_consumed(bs)
        ve_xa = p.c["ve"]

        p.w("T", "dwA", st["wA_val"])
        weave(p, bank_wait, 0, apply_ck, per=4)
        p.w("T", "ve", ve_xa)
        kql = [t["kq"][:, oc * TT:(oc + 1) * TT] for oc in range(NCH)]
        for oc in range(NCH):
            bA, bB = (0, 1) if oc % 2 == 0 else (2, 3)
            bank_wait(bA)
            bank_wait(bB)
            for ic in range(NCH):
                p.mm2(ps[bA][:, :], ps[bB][:, :],
                      t["WA"][:, ic * D + oc * 128: ic * D + (oc + 1) * 128],
                      xal[ic], start=(ic == 0), stop=(ic == NCH - 1),
                      sem="pe" if ic == NCH - 1 else None)
            pe_q = p.c["pe"]
            p.w("S", "pe", pe_q)
            p.w("S", "ve", st["l3_ve"])
            p.act(t["ltmp3"][:, :], ps[bA][:, :], AF.Identity)
            bank_consumed(bA, "se")
            se_i = p.c["se"]
            p.w("V", "se", se_i)
            p.tt(kql[oc], t["ltmp3"][:, :], ps[bB][:, :], ALU.add)
            st["l3_ve"] = p.c["ve"]
            bank_consumed(bB)
        return {"xf": xf, "b": b, "j": j, "tile": tile, "ve_q": p.c["ve"],
                "kql": kql}

    def p2_partB(pa):
        tile = pa["tile"]
        j = pa["j"]
        b = pa["b"]
        xf = pa["xf"]
        kql = pa["kql"]
        p.w("T", "ve", pa["ve_q"])

        qfl = [t["qf"][:, c * TT:(c + 1) * TT] for c in range(NCH)]
        qf_se = [0] * NCH
        for c in range(NCH):
            bk = 4 + c % 2
            bank_wait(bk)
            p.mm(ps[bk][:, :], t["wfbd"][:, :], kql[c], start=True, stop=True,
                 sem="pe")
            pe_qf = p.c["pe"]
            p.w("S", "pe", pe_qf)
            p.act(qfl[c], ps[bk][:, :], AF.Relu, bias=t["bf2"][:, 0:1])
            qf_se[c] = p.c["se"]
            bank_consumed(bk, "se")

        if tile == 0:
            p.w("T", "ve", ve_bd)
        rcp2 = [t["rcp_bf"], t["rcp_bfb"]]
        rcp_done = [0] * NCH
        bcast_pe = [0] * NCH
        yl = [t["y"][:, c * TT:(c + 1) * TT] for c in range(NCH)]
        for c in range(NCH):
            bk = 4 + c % 2
            bo_ = 6 + c % 2
            bank_wait(bk)
            p.w("T", "se", qf_se[c])
            p.mm(ps[bk][0:2, :], t["ksc"][:, c * 2:(c + 1) * 2], qfl[c],
                 start=True, stop=True, sem="pe")
            pe_n = p.c["pe"]
            p.w("V", "pe", pe_n)
            p.ts(t["normsb"][:, :], ps[bk][0:2, :], 1e-6, ALU.max)
            bank_consumed(bk)
            p.vrecip(t["normsb"][:, :], t["normsb"][:, :])
            if c >= 2:
                p.w("V", "pe", bcast_pe[c - 2])
            p.vcopy(rcp2[c % 2][:, :], t["normsb"][:, :])
            rcp_done[c] = p.c["ve"]
            bank_wait(bk)
            p.w("T", "ve", rcp_done[c])
            p.mm(ps[bk][:, :], t["pat2"][:, :], rcp2[c % 2][:, :],
                 start=True, stop=True, sem="pe")
            bank_wait(bo_)
            p.mm(ps[bo_][:, :], t["BD"][:, c * 128:(c + 1) * 128], qfl[c],
                 start=True, stop=True, sem="pe")
            bcast_pe[c] = p.c["pe"]
            pe_o2 = p.c["pe"]
            p.w("V", "pe", pe_o2)
            p.vcopy(t["rbc"][:, :], ps[bk][:, :])
            bank_consumed(bk)
            p.tt(yl[c], ps[bo_][:, :], t["rbc"][:, :], ALU.mult)
            bank_consumed(bo_)
        ve_y = p.c["ve"]

        p.w("T", "dwB", st["wB_val"])
        p.w("T", "ve", ve_y)
        zl = [t["z"][:, oc * TT:(oc + 1) * TT] for oc in range(NCH)]
        p.w("S", "pe", st["sq_last_pe"])
        zsq_ck = []
        for oc in range(NCH):
            bA, bB = (0, 1) if oc % 2 == 0 else (2, 3)
            bank_wait(bA)
            bank_wait(bB)
            for ic in range(NCH):
                p.mm2(ps[bA][:, :], ps[bB][:, :],
                      t["WB"][:, ic * D + oc * 128: ic * D + (oc + 1) * 128],
                      yl[ic], start=(ic == 0), stop=(ic == NCH - 1),
                      sem="pe" if ic == NCH - 1 else None)
            pe_z = p.c["pe"]
            p.w("S", "pe", pe_z)
            p.w("S", "ve", st["l3_ve"])
            p.act(t["ltmp3"][:, :], ps[bA][:, :], AF.Identity,
                  bias=t["bo_t"][:, oc:oc + 1])
            bank_consumed(bA, "se")
            se_i = p.c["se"]
            p.w("V", "se", se_i)
            p.tt(zl[oc], t["ltmp3"][:, :], ps[bB][:, :], ALU.add)
            st["l3_ve"] = p.c["ve"]
            bank_consumed(bB)
            ve_z = p.c["ve"]
            p.w("S", "ve", ve_z)
            p.act(t["sq"][:, oc * TT:(oc + 1) * TT], zl[oc], AF.Square)
            zsq_ck.append(("se", p.c["se"]))

        nxt = p2_partA(tile + 1) if tile + 1 < seq_tiles else None

        p.w("V", "dsp", p.c["dsp"])
        xpl = [t["xp"][:, ic * TT:(ic + 1) * TT] for ic in range(NCH)]
        sn2 = ln_fm(p, st, bank_wait, bank_consumed, None, zl, xpl, (4, 5),
                    x_add=xf, src_ve=p.c["ve"], sq_ck=zsq_ck, warm_bank=0)
        ve_xp = p.c["ve"]
        gp_xp = sn2["gp_end"]
        st["xbuf_free"][b] = [("ve", ve_xp), ("se", p.c["se"]), ("gp", gp_xp)]
        load_entry(j + 2)

        p.w("Y", "ve", ve_xp)
        p.w("Y", "gp", gp_xp)
        p.dma("Y", dr["xp_hbm"][s, tile], t["xp"][:, :], "dsp")
        st["dsp_tile"][(s, tile)] = p.c["dsp"]
        return nxt

    pa = p2_partA(0)
    for _tile in range(seq_tiles):
        pa = p2_partB(pa)

    _barrier(p, engines="TVS", dma=("dsp",))
    for w in range(nwin):
        emit_ffn_window(p, dr, st, s, w, seq_tiles, win_tiles, bank_wait,
                        bank_consumed, out_dmas)


def ffn_slice_dma(p, st, dr, which, idx):
    k = st["ffn_dma"]
    buf = p.t[("wc0", "wc1", "wc2")[k % 3]]
    sem = ("dw0", "dw1", "dw2")[k % 3]
    if k >= 3:
        done = st["wslice_done"].get(k - 3)
        if done is not None:
            p.w("G", "pe", done)
    p.dma("G", buf[:, :], dr[which][idx], sem)
    st["ffn_dma"] += 1


def ffn_slice_use(p, st):
    k = st["ffn_slice"]
    buf = p.t[("wc0", "wc1", "wc2")[k % 3]]
    sem = ("dw0", "dw1", "dw2")[k % 3]
    p.w("T", sem, 16 * (k // 3 + 1))
    st["ffn_slice"] += 1
    return buf, k


def emit_ffn_window(p, dr, st, s, w, seq_tiles, win_tiles, bank_wait, bank_consumed,
                    out_dmas):
    t = p.t
    ps = t["ps"]
    seq = seq_tiles * TT
    W = win_tiles * TT
    NH = win_tiles

    al = lambda oc, h: t["a"][:, oc * W + h * TT: oc * W + (h + 1) * TT]
    bl = lambda oc, h: t["b"][:, oc * W + h * TT: oc * W + (h + 1) * TT]
    xn_rhs = lambda ic, h: t["xnw"][:, h * 4096 + ic * TT: h * 4096 + (ic + 1) * TT]

    _barrier(p, engines="VS", dma=("dout", "dxr"))
    p.w("Y", "pe", p.c["pe"])
    p.w("Y", "ve", p.c["ve"])
    p.w("Y", "se", p.c["se"])
    dxr_t = []
    for ti in range(win_tiles):
        p.w("Y", "dsp", st["dsp_tile"][(s, w * win_tiles + ti)])
        p.dma("Y", t["xpr"][:, ti * 4096:(ti + 1) * 4096],
              dr["xp_hbm"][s, w * win_tiles + ti], "dxr")
        dxr_t.append(p.c["dxr"])

    for ti in range(win_tiles):
        xpl3 = [t["xpr"][:, ti * 4096 + ic * TT: ti * 4096 + (ic + 1) * TT]
                for ic in range(NCH)]
        p.w("S", "dxr", dxr_t[ti])
        p.w("S", "pe", st["sq_last_pe"])
        sq3_ck = []
        x1bf = [t["a"][:, ic * TT:(ic + 1) * TT] for ic in range(NCH)]
        for ic in range(NCH):
            p.act(x1bf[ic], xpl3[ic], AF.Copy)
            p.act(t["sq"][:, ic * TT:(ic + 1) * TT], xpl3[ic], AF.Square)
            sq3_ck.append(("se", p.c["se"]))
        xnl = [t["xnw"][:, ti * 4096 + ic * TT: ti * 4096 + (ic + 1) * TT]
               for ic in range(NCH)]
        ln_fm(p, st, bank_wait, bank_consumed, xpl3, x1bf, xnl, (2, 3),
              sq_ck=sq3_ck, warm_bank=0)
    ve_xnw = p.c["ve"]
    p.w("T", "ve", ve_xnw)

    for slab in range(NCH):
        ffn_slice_dma(p, st, dr, "w1_t", slab)
        ffn_slice_dma(p, st, dr, "w2_t", slab)
        b1, k1 = ffn_slice_use(p, st)
        b2, k2 = ffn_slice_use(p, st)
        for ocl in range(4):
            oc = slab * 4 + ocl
            for h in range(NH):
                w1A, w1B = (0 + 2 * h, 1 + 2 * h)
                w2A, w2B = (4 + 2 * h, 5 + 2 * h)
                bank_wait(w1A)
                bank_wait(w1B)
                for ic in range(NCH):
                    p.mm2(ps[w1A][:, :], ps[w1B][:, :],
                          b1[:, ic * 512 + ocl * 128: ic * 512 + ocl * 128 + 128],
                          xn_rhs(ic, h), start=(ic == 0), stop=(ic == NCH - 1),
                          sem="pe" if ic == NCH - 1 else None)
                pe_w1 = p.c["pe"]
                bank_wait(w2A)
                bank_wait(w2B)
                for ic in range(NCH):
                    p.mm2(ps[w2A][:, :], ps[w2B][:, :],
                          b2[:, ic * 512 + ocl * 128: ic * 512 + ocl * 128 + 128],
                          xn_rhs(ic, h), start=(ic == 0), stop=(ic == NCH - 1),
                          sem="pe" if ic == NCH - 1 else None)
                pe_w2 = p.c["pe"]
                lt0, lt2, lt3 = (("ltmp", "ltmp2", "ltmp3") if h == 0 else
                                 ("ltmpB", "ltmp2B", "ltmp3B"))
                lkey = "ltmp_ve" if h == 0 else "ltmp_veB"
                p.w("S", "pe", pe_w1)
                p.w("S", "ve", st[lkey])
                p.act(t[lt2][:, :], ps[w1A][:, :], AF.Identity)
                bank_consumed(w1A, "se")
                se_i1 = p.c["se"]
                p.w("V", "se", se_i1)
                p.tt(t[lt3][:, :], t[lt2][:, :], ps[w1B][:, :], ALU.add)
                bank_consumed(w1B)
                ve_s1 = p.c["ve"]
                p.w("S", "ve", ve_s1)
                p.act(t[lt0][:, :], t[lt3][:, :], AF.Silu,
                      scale=t["s1"][:, oc:oc + 1])
                se_t1 = p.c["se"]
                p.w("S", "pe", pe_w2)
                p.act(t[lt2][:, :], ps[w2A][:, :], AF.Identity)
                bank_consumed(w2A, "se")
                se_i2 = p.c["se"]
                p.w("V", "pe", pe_w2)
                p.w("V", "se", se_i2)
                p.tt(t[lt3][:, :], t[lt2][:, :], ps[w2B][:, :], ALU.add)
                bank_consumed(w2B)
                p.stt(al(oc, h), t[lt3][:, :], t["s2"][:, oc:oc + 1],
                      t[lt0][:, :], ALU.mult, ALU.mult)
                st[lkey] = p.c["ve"]
        st["wslice_done"][k1] = p.c["pe"]
        st["wslice_done"][k2] = p.c["pe"]

    ve_a = p.c["ve"]
    for wt, sc_t, srcf, dstf in (("h1_t", "sh1", al, bl), ("h2_t", "sh2", bl, al)):
        if wt == "h1_t":
            p.w("T", "ve", ve_a)
        else:
            p.w("T", "se", p.c["se"])
        for oc in range(NHC):
            ffn_slice_dma(p, st, dr, wt, oc)
            wb, k = ffn_slice_use(p, st)
            for h in range(NH):
                bA = 4 * (oc % 2) + 2 * h
                bB = bA + 1
                bank_wait(bA)
                bank_wait(bB)
                for ic in range(NHC):
                    p.mm2(ps[bA][:, :], ps[bB][:, :], wb[:, ic * 128:(ic + 1) * 128],
                          srcf(ic, h), start=(ic == 0), stop=(ic == NHC - 1),
                          sem="pe" if ic == NHC - 1 else None)
                pe_h = p.c["pe"]
                lt0, lt2 = ("ltmp", "ltmp2") if h == 0 else ("ltmpB", "ltmp2B")
                lkey = "ltmp_ve" if h == 0 else "ltmp_veB"
                p.w("S", "pe", pe_h)
                p.w("S", "ve", st[lkey])
                p.act(t[lt0][:, :], ps[bA][:, :], AF.Identity)
                bank_consumed(bA, "se")
                se_i = p.c["se"]
                p.w("V", "se", se_i)
                p.tt(t[lt2][:, :], t[lt0][:, :], ps[bB][:, :], ALU.add)
                bank_consumed(bB)
                st[lkey] = p.c["ve"]
                p.w("S", "ve", st[lkey])
                p.act(dstf(oc, h), t[lt2][:, :], AF.Gelu,
                      scale=t[sc_t][:, oc:oc + 1])
            st["wslice_done"][k] = p.c["pe"]

    p.w("Y", "pe", p.c["pe"])
    p.w("Y", "dsp", p.c["dsp"])
    for ti in range(win_tiles):
        p.dma("Y", t["xpr"][:, ti * 4096:(ti + 1) * 4096],
              dr["xp_hbm"][s, w * win_tiles + ti], "dxr")
    dxr_val = p.c["dxr"]
    fl = lambda oc, h: t["f"][:, oc * W + h * TT: oc * W + (h + 1) * TT]
    p.w("T", "se", p.c["se"])
    for oc in range(NCH):
        ffn_slice_dma(p, st, dr, "w3_t", oc)
        wb, k = ffn_slice_use(p, st)
        for h in range(NH):
            bA = 4 * (oc % 2) + 2 * h
            bB = bA + 1
            bank_wait(bA)
            bank_wait(bB)
            for ic in range(NHC):
                p.mm2(ps[bA][:, :], ps[bB][:, :], wb[:, ic * 128:(ic + 1) * 128],
                      al(ic, h), start=(ic == 0), stop=(ic == NHC - 1),
                      sem="pe" if ic == NHC - 1 else None)
            pe_f = p.c["pe"]
            lt0 = "ltmp" if h == 0 else "ltmpB"
            lkey = "ltmp_ve" if h == 0 else "ltmp_veB"
            p.w("S", "pe", pe_f)
            p.w("S", "ve", st[lkey])
            p.act(t[lt0][:, :], ps[bA][:, :], AF.Identity,
                  scale=t["s3"][:, oc:oc + 1])
            bank_consumed(bA, "se")
            se_i = p.c["se"]
            p.w("V", "se", se_i)
            p.stt(fl(oc, h), ps[bB][:, :], t["s3"][:, oc:oc + 1], t[lt0][:, :],
                  ALU.mult, ALU.add)
            bank_consumed(bB)
            st[lkey] = p.c["ve"]
        st["wslice_done"][k] = p.c["pe"]

    p.w("V", "dxr", dxr_val)
    for h in range(NH):
        fbf = [fl(ic, h) for ic in range(NCH)]
        p.w("S", "pe", st["sq_last_pe"])
        p.w("S", "ve", st["ltmp_ve"])
        p.w("S", "ve", st["ltmp_veB"])
        sq4_ck = []
        for ic in range(NCH):
            p.act(t["sq"][:, ic * TT:(ic + 1) * TT], fbf[ic], AF.Square)
            sq4_ck.append(("se", p.c["se"]))
        xprl = [t["xpr"][:, h * 4096 + ic * TT: h * 4096 + (ic + 1) * TT]
                for ic in range(NCH)]
        outl = [t["outt"][:, ic * TT:(ic + 1) * TT] for ic in range(NCH)]
        p.w("V", "dout", p.c["dout"])
        ln_fm(p, st, bank_wait, bank_consumed, None, fbf, outl, (0, 1), x_add=xprl,
              src_ve=p.c["ve"], sq_ck=sq4_ck, warm_bank=4)
        ve_out = p.c["ve"]
        p.w("Y", "ve", ve_out)
        p.w("Y", "gp", p.c["gp"])
        tglob = w * win_tiles + h
        p.dma("Y",
              dr["out"][s][:, :, tglob * TT:(tglob + 1) * TT].rearrange("i p tt -> p i tt"),
              t["outt"][:, :].rearrange("p (i tt) -> p i tt", tt=TT), "dout")
        out_dmas[0] += 1



def _prep_inputs(L0_attn_x, L0_attn_Wq, L0_attn_Wk, L0_attn_Wv, L0_attn_Wo,
                 L0_attn_bo, L0_attn_Wf, L0_attn_bf,
                 L0_ffn_w1_w, L0_ffn_w1_s, L0_ffn_w2_w, L0_ffn_w2_s,
                 L0_ffn_h1_w, L0_ffn_h1_s, L0_ffn_h2_w, L0_ffn_h2_s,
                 L0_ffn_w3_w, L0_ffn_w3_s):
    f32 = np.float32
    asc = np.ascontiguousarray

    def tile_oi(wT, n_in_ch, n_out_ch, ocp):
        r = wT.reshape(n_in_ch, 128, n_out_ch, ocp).transpose(2, 1, 0, 3)
        return asc(r.reshape(n_out_ch, 128, n_in_ch * ocp).astype(f32))

    sh = {}
    for n, wmat in (("wq", L0_attn_Wq), ("wk", L0_attn_Wk), ("wv", L0_attn_Wv),
                    ("wo", L0_attn_Wo)):
        sh[n] = asc(np.asarray(wmat, dtype=f32))
    wf = np.asarray(L0_attn_Wf, dtype=f32)
    wfbd = np.zeros((128, 128), f32)
    wfbd[:64, :64] = wf
    wfbd[64:, 64:] = wf
    sh["wfbd"] = wfbd
    bfv = np.asarray(L0_attn_bf, dtype=f32)
    sh["bfbc"] = asc(np.broadcast_to(np.concatenate([bfv, bfv])[None, :],
                                     (128, 128)).astype(f32))
    sh["bf2"] = asc(np.concatenate([bfv, bfv])[:, None].astype(f32))
    sh["bo_t"] = asc(np.asarray(L0_attn_bo, dtype=f32).reshape(NCH, 128).T)
    w1T = np.asarray(L0_ffn_w1_w, dtype=f32).T
    w2T = np.asarray(L0_ffn_w2_w, dtype=f32).T
    h1T = np.asarray(L0_ffn_h1_w, dtype=f32).T
    h2T = np.asarray(L0_ffn_h2_w, dtype=f32).T
    w3T = np.asarray(L0_ffn_w3_w, dtype=f32).T
    sh["w1_t"] = tile_oi(w1T, 8, 8, 512)
    sh["w2_t"] = tile_oi(w2T, 8, 8, 512)
    sh["h1_t"] = tile_oi(h1T, 32, 32, 128)
    sh["h2_t"] = tile_oi(h2T, 32, 32, 128)
    sh["w3_t"] = tile_oi(w3T, 32, 8, 128)
    for n, v in (("s1_t", L0_ffn_w1_s), ("s2_t", L0_ffn_w2_s),
                 ("sh1_t", L0_ffn_h1_s), ("sh2_t", L0_ffn_h2_s)):
        sh[n] = asc(np.asarray(v, dtype=f32).reshape(-1, 128).T)
    sh["s3_t"] = asc(np.asarray(L0_ffn_w3_s, dtype=f32).reshape(NCH, 128).T)
    pat = np.zeros((2, 128), f32)
    pat[0, :64] = 1.0
    pat[1, 64:] = 1.0
    sh["pat2"] = pat
    sh["bfrow"] = asc(np.concatenate([bfv, bfv])[None, :].astype(f32))
    sh["warm512"] = np.ones((128, TT), f32)
    sh["onestat"] = np.full((128, 1), 1.0 / D, f32)
    sh["onesrow"] = np.ones((1, 128), f32)

    x = np.asarray(L0_attn_x, dtype=f32)
    percore = []
    for c in range(8):
        xc = x[c * 2:(c + 1) * 2]
        xt = xc.transpose(0, 2, 1).reshape(2, NCH, 128, xc.shape[1])
        percore.append(asc(xt.astype(f32)))
    return sh, percore


_NC_CACHE = {}


def kernel(**inputs):
    sh, percore = _prep_inputs(**inputs)
    if "full" not in _NC_CACHE:
        _NC_CACHE["full"] = build_program()
    nc = _NC_CACHE["full"]
    in_maps = []
    for c in range(8):
        m = dict(sh)
        m["x"] = percore[c]
        in_maps.append(m)
    res = run_bass_kernel_spmd(nc, in_maps, core_ids=list(range(8)))
    outs = []
    for c in range(8):
        o = res.results[c]["out"]
        o = o.reshape(2, D, -1).transpose(0, 2, 1)
        outs.append(o)
    return np.ascontiguousarray(np.concatenate(outs, axis=0)).astype(np.float32)
